# version 01
# baseline (speedup 1.0000x reference)
import numpy as np
import ml_dtypes
import concourse.bass as bass
import concourse.mybir as mybir
from concourse.bass_utils import run_bass_kernel_spmd

DBG = {}
F32 = mybir.dt.float32
BF16 = mybir.dt.bfloat16
AF = mybir.ActivationFunctionType
ALU = mybir.AluOpType
AX = mybir.AxisListType
bfnp = ml_dtypes.bfloat16

L = 2048
D = 1024
CH = 512
NCH = L // CH
GUARD = 4
XW = GUARD + L
EPS = 1e-5

ENGINES = ("sync", "scalar", "tensor", "vector", "gpsimd")
SELF_SEM = {"vector": "sV", "scalar": "sS", "gpsimd": "sG"}


class Prog:
    def __init__(self, nc):
        self.nc = nc
        self.ops = []
        self.cnt = {}
        self.sems = {}
        self.bank = [[] for _ in range(8)]

    def sem(self, name):
        if name not in self.sems:
            self.sems[name] = self.nc.alloc_semaphore(name=name)
            self.cnt[name] = 0
        return self.sems[name]

    def raw(self, engine, fn):
        self.ops.append((engine, fn))

    def op(self, engine, method, *a, wait=(), inc=None, incn=1, **kw):
        waits = list(wait)
        sems = self.sems
        auto = SELF_SEM.get(engine) if method != "dma_start" else None
        if auto is not None:
            self.sem(auto)
            prev = self.cnt[auto]
            if prev > 0:
                waits.append((auto, prev))
            assert inc is None or inc == auto, \
                f"op on {engine} cannot carry extra inc {inc} (one sync update per inst)"
            inc = auto
        if inc is not None:
            self.sem(inc)
        for s, _ in waits:
            self.sem(s)
        waits = tuple(waits)

        def fn(eng, method=method, a=a, kw=kw, waits=waits, inc=inc, incn=incn):
            for s, v in waits:
                if v > 0:
                    eng.wait_ge(sems[s], v)
            inst = getattr(eng, method)(*a, **kw)
            if inc is not None:
                inst.then_inc(sems[inc], incn)
        self.raw(engine, fn)
        ret = None
        if inc is not None:
            self.cnt[inc] += incn
            ret = self.cnt[inc]
        return ret

    def wait(self, engine, *wait):
        waits = tuple(wait)
        sems = self.sems
        for s, _ in waits:
            self.sem(s)

        def fn(eng, waits=waits):
            for s, v in waits:
                if v > 0:
                    eng.wait_ge(sems[s], v)
        self.raw(engine, fn)

    def dma(self, queue, out, in_, wait=(), inc=None, **kw):
        return self.op(queue, "dma_start", out, in_, wait=wait, inc=inc, incn=16, **kw)

    def barrier(self):
        vals = {}
        for e in ENGINES:
            n = {"tensor": "sT", "sync": "bar_sync"}.get(e, SELF_SEM.get(e))
            self.sem(n)
            self.op(e, "nop", inc=n if e in ("tensor", "sync") else None)
            vals[e] = (n, self.cnt[n])
        for e in ENGINES:
            self.wait(e, *[vals[x] for x in ENGINES])
        self.bank = [[] for _ in range(8)]

    def bank_wait(self, i):
        w = self.bank[i]
        self.bank[i] = []
        return w

    def bank_done(self, i, sem, val):
        self.bank[i].append((sem, val))

    def build(self):
        with self.nc.Block() as block:
            for en in ENGINES:
                ops = [fn for (e, fn) in self.ops if e == en]

                def body(eng, ops=ops):
                    for fn in ops:
                        fn(eng)
                getattr(block, en)(body)


class Arena:
    def __init__(self, nc, nbytes):
        self.nc = nc
        self.base, _ = nc.bump_sbuf(nbytes)
        self.nbytes = nbytes
        self.off = 0
        self.marks = []
        self.n = 0

    def alloc(self, name, shape, dtype):
        esz = mybir.dt.size(dtype)
        need = int(np.prod(shape[1:])) * esz
        off = (self.off + 31) // 32 * 32
        assert off + need <= self.nbytes, f"arena overflow: {name}: {off}+{need} > {self.nbytes}"
        self.off = off + need
        self.n += 1
        return self.nc.alloc_sbuf_tensor_at(f"ar_{name}_{self.n}", list(shape), dtype,
                                            offset=self.base + off)

    def push(self):
        self.marks.append(self.off)

    def pop(self):
        self.off = self.marks.pop()


def fap(t, off, dims):
    a = t.ap()
    return bass.AP(a.tensor, a.offset + off, [list(a.ap[0])] + [list(d) for d in dims])


def fap_d(t, off, dims):
    return bass.AP(t, off, [list(d) for d in dims])


STAGE_FNS = {}


def stage(name):
    def deco(fn):
        STAGE_FNS[name] = fn
        return fn
    return deco


class Ctx:
    pass


def tok(c0, n=CH):
    return slice(GUARD + c0, GUARD + c0 + n)



def emit_ln_xn(ctx, xn, gamma_beta=None, tok0=0, ntok=L, shifts=None, src=None,
               xn_slice=None):
    P, X, PS, AR, ONES, cc = ctx.P, ctx.X, ctx.PS, ctx.AR, ctx.ONES, ctx.const_col
    if src is None:
        src = X
    if shifts is None:
        shifts = [0] * 8
    inplace = xn is None
    xoff = 0 if xn_slice is None else xn_slice

    def ssl(i, c0, n):
        return slice(GUARD + tok0 + c0 + shifts[i], GUARD + tok0 + c0 + shifts[i] + n)

    nch = (ntok + CH - 1) // CH
    AR.push()
    xb = [AR.alloc(f"lnxb{i}", [128, ntok], BF16) for i in range(8)]
    sq = [AR.alloc(f"lnsq{i}", [128, ntok], BF16) for i in range(8)]
    rstd = AR.alloc("lnrstd", [128, ntok], F32)
    arow = AR.alloc("lnarow", [128, ntok], F32)
    mu = AR.alloc("lnmu", [128, CH], F32)
    var = AR.alloc("lnvar", [128, CH], F32)

    for i in range(8):
        eng = "vector" if i % 2 == 0 else "gpsimd"
        P.op(eng, "tensor_copy", xb[i][:, :], src[i][:, ssl(i, 0, ntok)])
        P.op("scalar", "activation", sq[i][:, :], src[i][:, ssl(i, 0, ntok)], AF.Square)
    vcast = P.cnt["sV"]
    gcast = P.cnt["sG"]
    scast = P.cnt["sS"]

    prev_ps_read_v = 0
    prev_var_read_s = 0
    for c in range(nch):
        cw = min(CH, ntok - c * CH)
        cs = slice(c * CH, c * CH + cw)
        w0 = P.bank_wait(0) + ([("sV", vcast), ("sG", gcast)] if c == 0
                               else [("sV", prev_ps_read_v)])
        for i in range(8):
            P.op("tensor", "matmul", PS[0][:, 0:cw], ONES[:, :], xb[i][:, cs],
                 start=(i == 0), stop=(i == 7), wait=w0 if i == 0 else ())
        w1 = P.bank_wait(1) + ([("sS", scast)] if c == 0 else [])
        for i in range(8):
            P.op("tensor", "matmul", PS[1][:, 0:cw], ONES[:, :], sq[i][:, cs],
                 start=(i == 0), stop=(i == 7), wait=w1 if i == 0 else (),
                 inc="sT" if i == 7 else None)
        mmv = P.cnt["sT"]
        P.op("vector", "tensor_scalar", mu[:, 0:cw], PS[0][:, 0:cw], 1.0 / D, None, ALU.mult,
             wait=[("sT", mmv)])
        P.op("vector", "tensor_scalar", var[:, 0:cw], PS[1][:, 0:cw], 1.0 / D, None, ALU.mult,
             wait=[("sS", prev_var_read_s)] if prev_var_read_s else ())
        prev_ps_read_v = P.cnt["sV"]
        P.bank_done(0, "sV", prev_ps_read_v)
        P.bank_done(1, "sV", prev_ps_read_v)
        P.op("vector", "tensor_tensor", rstd[:, cs], mu[:, 0:cw], mu[:, 0:cw], ALU.mult)
        vmm = P.op("vector", "tensor_tensor", var[:, 0:cw], var[:, 0:cw], rstd[:, cs],
                   ALU.subtract)
        P.op("scalar", "activation", var[:, 0:cw], var[:, 0:cw], AF.Ln, bias=cc(EPS),
             wait=[("sV", vmm)])
        prev_var_read_s = P.cnt["sS"]
        srow = P.op("scalar", "activation", rstd[:, cs], var[:, 0:cw], AF.Exp, scale=cc(-0.5))
        varow = P.op("vector", "tensor_tensor", arow[:, cs], mu[:, 0:cw], rstd[:, cs],
                     ALU.mult, wait=[("sS", srow)])
        prev_ps_read_v = varow
        for i in range(8):
            eng = "vector" if i % 2 == 0 else "gpsimd"
            wt = [("sV", varow)] if eng == "gpsimd" else ()
            if inplace:
                P.op(eng, "tensor_tensor", src[i][:, ssl(i, c * CH, cw)],
                     src[i][:, ssl(i, c * CH, cw)], rstd[:, cs], ALU.mult, wait=wt)
                P.op(eng, "tensor_tensor", src[i][:, ssl(i, c * CH, cw)],
                     src[i][:, ssl(i, c * CH, cw)], arow[:, cs], ALU.subtract)
                continue
            xsl = slice(xoff + c * CH, xoff + c * CH + cw)
            P.op(eng, "tensor_tensor", xb[i][:, cs], src[i][:, ssl(i, c * CH, cw)],
                 rstd[:, cs], ALU.mult, wait=wt)
            if gamma_beta is None:
                P.op(eng, "tensor_tensor", xn[i][:, xsl], xb[i][:, cs], arow[:, cs],
                     ALU.subtract)
            else:
                P.op(eng, "tensor_tensor", xb[i][:, cs], xb[i][:, cs], arow[:, cs],
                     ALU.subtract)
                g_ap, b_ap = gamma_beta[i]
                P.op(eng, "tensor_scalar", xn[i][:, xsl], xb[i][:, cs], g_ap, b_ap,
                     ALU.mult, op1=ALU.add)
    AR.pop()
    return [("sV", P.cnt["sV"]), ("sG", P.cnt["sG"])]



def emit_top2_sel(P, x, sel, width, gsz, ng, m, eq, cand, IDX, IDXB):
    for p in range(2):
        P.op("vector", "tensor_reduce", m[:, 0:ng], fap(x, 0, [[gsz, ng], [1, gsz]]),
             AX.X, ALU.max)
        P.op("vector", "tensor_tensor", eq[:, 0:width], x[:, 0:width],
             fap(m, 0, [[1, ng], [0, gsz]]), ALU.is_ge)
        P.op("vector", "scalar_tensor_tensor", cand[:, 0:width], eq[:, 0:width], -1e4,
             IDXB[:, 0:width], ALU.mult, ALU.add)
        P.op("vector", "tensor_reduce", m[:, 0:ng], fap(cand, 0, [[gsz, ng], [1, gsz]]),
             AX.X, ALU.min)
        tgt = sel if p == 0 else eq
        P.op("vector", "tensor_tensor", tgt[:, 0:width], IDX[:, 0:width],
             fap(m, 0, [[1, ng], [0, gsz]]), ALU.is_equal)
        if p == 0:
            P.op("vector", "scalar_tensor_tensor", x[:, 0:width], sel[:, 0:width], -1e30,
                 x[:, 0:width], ALU.mult, ALU.add)
        else:
            P.op("vector", "tensor_tensor", sel[:, 0:width], sel[:, 0:width],
                 eq[:, 0:width], ALU.add)


def emit_moe(ctx):
    P, X, PS, AR, dram, cc = ctx.P, ctx.X, ctx.PS, ctx.AR, ctx.dram, ctx.const_col
    E = 8

    AR.push()
    xn = [AR.alloc(f"moxn{i}", [128, L], BF16) for i in range(8)]
    lnw = emit_ln_xn(ctx, xn)

    IDX8 = AR.alloc("moidx8", [128, 128], F32)
    IDX8B = AR.alloc("moidx8b", [128, 128], F32)
    IDX4 = AR.alloc("moidx4", [128, 64], F32)
    IDX4B = AR.alloc("moidx4b", [128, 64], F32)
    gw = AR.alloc("mogw", [128, 64], BF16)
    biasb = AR.alloc("mobias", [128, 128], F32)
    ltm = AR.alloc("moltm", [128, 128], F32)
    scc = AR.alloc("moscc", [128, 128], F32)
    gsb = AR.alloc("mogsb", [128, 64], F32)
    gsel = AR.alloc("mogsel", [128, 64], F32)
    smask = AR.alloc("mosmask", [128, 128], F32)
    sel = AR.alloc("mosel", [128, 128], F32)
    eqb = AR.alloc("moeq", [128, 128], F32)
    cnd = AR.alloc("mocnd", [128, 128], F32)
    mrow = AR.alloc("momrow", [128, 32], F32)
    den = AR.alloc("moden", [128, 16], F32)
    cwt = AR.alloc("mocwt", [128, 128], F32)
    cwR = AR.alloc("mocwR", [128, L], F32)
    gpb = [AR.alloc(f"mogp{b}", [128, 4096], BF16) for b in range(2)]
    upb = [AR.alloc(f"moup{b}", [128, 4096], BF16) for b in range(2)]
    dnb = [AR.alloc(f"modn{b}", [128, 4096], BF16) for b in range(2)]
    hall = [AR.alloc(f"moh{e}", [128, 4 * CH], BF16) for e in range(E)]
    cwb = [AR.alloc(f"mocwb{b}", [128, CH], F32) for b in range(2)]
    sil = AR.alloc("mosil", [128, CH], F32)

    P.dma("gpsimd", gw[:, :], fap_d(dram["moe_gwT"], 0, [[8, 128], [8 * 128, 8], [1, 8]]),
          wait=lnw, inc="qW")
    wgw = P.cnt["qW"]
    P.dma("sync", biasb[:, :], fap_d(dram["moe_bias16"], 0, [[0, 128], [1, 128]]),
          wait=lnw, inc="qD")
    wbias = P.cnt["qD"]

    for tt in range(16):
        b = tt % 2
        w = P.bank_wait(b)
        if tt == 0:
            w = w + [("qW", wgw)]
        for k in range(8):
            P.op("tensor", "matmul", PS[b][0:128, 0:8], xn[k][:, tt * 128:(tt + 1) * 128],
                 gw[:, k * 8:(k + 1) * 8], start=(k == 0), stop=(k == 7),
                 wait=w if k == 0 else (), inc="sT" if k == 7 else None)
        mmv = P.cnt["sT"]
        sv = P.op("scalar", "activation", ltm[:, tt * 8:(tt + 1) * 8], PS[b][0:128, 0:8],
                  AF.Sigmoid, wait=[("sT", mmv)])
        P.bank_done(b, "sS", sv)
    sigv = P.cnt["sS"]

    for nm, t, wdt in [("idx8", IDX8, 128), ("idx8b", IDX8B, 128), ("idx4", IDX4, 64),
                       ("idx4b", IDX4B, 64)]:
        P.dma("sync", t[:, :], fap_d(dram[nm], 0, [[0, 128], [1, wdt]]), wait=lnw, inc="qD")
    widx = P.cnt["qD"]
    P.op("vector", "tensor_tensor", scc[:, :], ltm[:, :], biasb[:, :], ALU.add,
         wait=[("sS", sigv), ("qD", widx)])
    P.op("vector", "tensor_tensor", gsb[:, :], fap(scc, 0, [[2, 64]]),
         fap(scc, 1, [[2, 64]]), ALU.add)
    emit_top2_sel(P, gsb, gsel, 64, 4, 16, mrow, eqb, cnd, IDX4, IDX4B)
    P.op("vector", "tensor_copy", smask[:, :], fap(gsel, 0, [[4, 16], [1, 4], [0, 2]]))
    P.op("vector", "tensor_tensor", scc[:, :], scc[:, :], smask[:, :], ALU.mult)
    emit_top2_sel(P, scc, sel, 128, 8, 16, mrow, eqb, cnd, IDX8, IDX8B)
    P.op("vector", "tensor_tensor", cwt[:, :], ltm[:, :], sel[:, :], ALU.mult)
    P.op("vector", "tensor_reduce", den[:, 0:16], fap(cwt, 0, [[8, 16], [1, 8]]),
         AX.X, ALU.add)
    P.op("vector", "tensor_single_scalar", den[:, 0:16], den[:, 0:16], 1e-20, ALU.add)
    P.op("vector", "reciprocal", den[:, 0:16], den[:, 0:16])
    gatev = P.op("vector", "tensor_tensor", cwt[:, :], cwt[:, :],
                 fap(den, 0, [[1, 16], [0, 8]]), ALU.mult)

    for blk in range(4):
        w = P.bank_wait(2)
        for j in range(4):
            tt = blk * 4 + j
            P.op("tensor", "transpose", PS[2][0:8, j * 128:(j + 1) * 128],
                 cwt[:, tt * 8:(tt + 1) * 8], ctx.IDNF[:, :],
                 wait=(w + [("sV", gatev)]) if j == 0 else (),
                 inc="sT" if j == 3 else None)
        trv = P.cnt["sT"]
        sv = P.op("vector", "tensor_copy", cwR[0:8, blk * 512:(blk + 1) * 512],
                  PS[2][0:8, 0:512], wait=[("sT", trv)])
        P.bank_done(2, "sV", sv)
    cwrv = P.cnt["sV"]
    cwd = ctx.scratch_cw
    P.dma("sync", cwd.ap(), cwR[0:8, :], wait=[("sV", cwrv)], inc="qD")
    wcw = P.cnt["qD"]

    def load_gu(e, b, guard):
        t = dram["moe_gpT"]
        ap = fap_d(t, e * 1024 * 512, [[512, 128], [128 * 512, 8], [1, 512]])
        P.dma("gpsimd", gpb[b][:, :], ap, wait=guard, inc="qW")
        t = dram["moe_upT"]
        ap = fap_d(t, e * 1024 * 512, [[512, 128], [128 * 512, 8], [1, 512]])
        return P.dma("gpsimd", upb[b][:, :], ap, inc="qW")

    def load_dn(e, b, guard):
        t = dram["moe_dnT"]
        ap = fap_d(t, e * 512 * 1024, [[1024, 128], [128 * 1024, 4], [1, 1024]])
        return P.dma("gpsimd", dnb[b][:, :], ap, wait=guard, inc="qW")

    gufree = {0: [], 1: []}
    dnfree = {0: [], 1: []}
    cwfree = {0: [], 1: []}
    wv_gu = {0: load_gu(0, 0, [])}
    wv_dn = {}

    for c in range(NCH):
        cs = tok(c * CH)
        hv = {}
        for e in range(E):
            pb = e % 2
            nxt = e + 1 if e + 1 < E else (0 if c + 1 < NCH else None)
            if nxt is not None:
                nb = (e + 1) % 2
                wv_gu[nxt] = load_gu(nxt, nb, gufree[nb])
                gufree[nb] = []
            cwv = P.dma("sync", cwb[pb][:, :],
                        fap_d(cwd, e * L + c * CH, [[0, 128], [1, CH]]),
                        wait=[("qD", wcw)] + cwfree[pb], inc="qD")
            cwfree[pb] = []
            for m in range(4):
                bg = 4 + (m % 2)
                bu = 6 + (m % 2)
                w = P.bank_wait(bg)
                if m == 0:
                    w = w + [("qW", wv_gu[e])]
                for k in range(8):
                    P.op("tensor", "matmul", PS[bg][:, :],
                         gpb[pb][:, k * 512 + m * 128: k * 512 + (m + 1) * 128],
                         xn[k][:, c * CH:(c + 1) * CH], start=(k == 0), stop=(k == 7),
                         wait=w if k == 0 else ())
                w = P.bank_wait(bu)
                for k in range(8):
                    P.op("tensor", "matmul", PS[bu][:, :],
                         upb[pb][:, k * 512 + m * 128: k * 512 + (m + 1) * 128],
                         xn[k][:, c * CH:(c + 1) * CH], start=(k == 0), stop=(k == 7),
                         wait=w if k == 0 else (), inc="sT" if k == 7 else None)
                mmv = P.cnt["sT"]
                if m == 3:
                    gufree[pb] = [("sT", mmv)]
                sv_ = P.op("scalar", "activation", sil[:, :], PS[bg][:, :], AF.Silu,
                           wait=[("sT", mmv)])
                P.bank_done(bg, "sS", sv_)
                hv1 = P.op("vector", "tensor_tensor", hall[e][:, m * CH:(m + 1) * CH],
                           sil[:, :], PS[bu][:, :], ALU.mult,
                           wait=[("sS", sv_), ("qD", cwv)])
                P.bank_done(bu, "sV", hv1)
                hv[e] = P.op("vector", "tensor_tensor", hall[e][:, m * CH:(m + 1) * CH],
                             hall[e][:, m * CH:(m + 1) * CH], cwb[pb][:, :], ALU.mult)
            cwfree[pb] = [("sV", hv[e])]

        wv_dn[0] = load_dn(0, 0, dnfree[0])
        dnfree[0] = []
        for e in range(E):
            db = e % 2
            if e + 1 < E:
                wv_dn[e + 1] = load_dn(e + 1, 1 - db, dnfree[1 - db])
                dnfree[1 - db] = []
            for m in range(8):
                for k in range(4):
                    w = []
                    if e == 0 and k == 0:
                        w = P.bank_wait(m) + [("sV", hv[E - 1])]
                    if k == 0:
                        w = list(w) + [("qW", wv_dn[e])]
                    P.op("tensor", "matmul", PS[m][:, :],
                         dnb[db][:, k * 1024 + m * 128: k * 1024 + (m + 1) * 128],
                         hall[e][:, k * CH:(k + 1) * CH],
                         start=(e == 0 and k == 0), stop=(e == E - 1 and k == 3),
                         wait=w, inc="sT" if (m == 7 and k == 3) else None)
            dnfree[db] = [("sT", P.cnt["sT"])]
        dn_all = P.cnt["sT"]
        for m in range(8):
            sv = P.op("vector", "scalar_tensor_tensor", X[m][:, cs], PS[m][:, :], 0.5,
                      X[m][:, cs], ALU.mult, ALU.add, wait=[("sT", dn_all)] if m == 0 else ())
            P.bank_done(m, "sV", sv)
    AR.pop()


@stage("s1")
def s1(ctx):
    emit_moe(ctx)


@stage("s3")
def s3(ctx):
    emit_moe(ctx)








def emit_ln_rows(ctx, rtot, arow, gelem=None, double=True):
    P, X, PS, AR, ONES, cc = ctx.P, ctx.X, ctx.PS, ctx.AR, ctx.ONES, ctx.const_col
    AR.push()
    xb = [AR.alloc(f"lrxb{i}", [128, L], BF16) for i in range(8)]
    sq = [AR.alloc(f"lrsq{i}", [128, L], BF16) for i in range(8)]
    mu = AR.alloc("lrmu", [128, CH], F32)
    var = AR.alloc("lrvar", [128, CH], F32)
    t1 = AR.alloc("lrt1", [128, CH], F32)

    for i in range(8):
        eng = "vector" if i % 2 == 0 else "gpsimd"
        P.op(eng, "tensor_copy", xb[i][:, :], X[i][:, tok(0, L)])
        P.op("scalar", "activation", sq[i][:, :], X[i][:, tok(0, L)], AF.Square)
    vcast, gcast, scast = P.cnt["sV"], P.cnt["sG"], P.cnt["sS"]

    prev_v = 0
    prev_s = 0
    for c in range(NCH):
        cs = slice(c * CH, (c + 1) * CH)
        w0 = P.bank_wait(0) + ([("sV", vcast), ("sG", gcast)] if c == 0 else [("sV", prev_v)])
        for i in range(8):
            P.op("tensor", "matmul", PS[0][:, :], ONES[:, :], xb[i][:, cs],
                 start=(i == 0), stop=(i == 7), wait=w0 if i == 0 else ())
        w1 = P.bank_wait(1) + ([("sS", scast)] if c == 0 else [])
        for i in range(8):
            P.op("tensor", "matmul", PS[1][:, :], ONES[:, :], sq[i][:, cs],
                 start=(i == 0), stop=(i == 7), wait=w1 if i == 0 else (),
                 inc="sT" if i == 7 else None)
        mmv = P.cnt["sT"]
        P.op("vector", "tensor_scalar", mu[:, :], PS[0][:, :], 1.0 / D, None, ALU.mult,
             wait=[("sT", mmv), ("sS", prev_s)] if prev_s else [("sT", mmv)])
        P.op("vector", "tensor_scalar", var[:, :], PS[1][:, :], 1.0 / D, None, ALU.mult)
        prev_v = P.cnt["sV"]
        P.bank_done(0, "sV", prev_v)
        P.bank_done(1, "sV", prev_v)
        P.op("vector", "tensor_tensor", t1[:, :], mu[:, :], mu[:, :], ALU.mult)
        vmm = P.op("vector", "tensor_tensor", var[:, :], var[:, :], t1[:, :], ALU.subtract)
        P.op("scalar", "activation", t1[:, :], var[:, :], AF.Ln, bias=cc(EPS),
             wait=[("sV", vmm)])
        s1 = P.op("scalar", "activation", rtot[:, cs], t1[:, :], AF.Exp, scale=cc(-0.5))
        prev_s = s1
        if double:
            P.op("vector", "tensor_tensor", t1[:, :], rtot[:, cs], rtot[:, cs], ALU.mult,
                 wait=[("sS", s1)])
            v2 = P.op("vector", "tensor_tensor", t1[:, :], t1[:, :], var[:, :], ALU.mult)
            P.op("scalar", "activation", t1[:, :], t1[:, :], AF.Ln, bias=cc(EPS),
                 wait=[("sV", v2)])
            s2_ = P.op("scalar", "activation", t1[:, :], t1[:, :], AF.Exp, scale=cc(-0.5))
            prev_s = s2_
            P.op("vector", "tensor_tensor", rtot[:, cs], rtot[:, cs], t1[:, :],
                 ALU.mult, wait=[("sS", s2_)])
            P.op("vector", "tensor_tensor", arow[:, cs], mu[:, :], rtot[:, cs], ALU.mult)
        else:
            P.op("vector", "tensor_tensor", arow[:, cs], mu[:, :], rtot[:, cs], ALU.mult,
                 wait=[("sS", s1)])
        prev_v = P.cnt["sV"]
    AR.pop()
    return [("sV", P.cnt["sV"]), ("sS", P.cnt["sS"])]


def emit_l0attn(ctx):
    P, X, PS, AR, dram, cc = ctx.P, ctx.X, ctx.PS, ctx.AR, ctx.dram, ctx.const_col
    NL, NS, DI, DC, DTR = 2, 16, 1024, 512, 32
    MC = 256
    NMC = L // MC
    scr = ctx.scratch_bc

    AR.push()
    rtot = AR.alloc("martot", [128, L], F32)
    arow = AR.alloc("maarow", [128, L], F32)
    lnw = emit_ln_rows(ctx, rtot, arow, double=True)

    lng = AR.alloc("malng", [128, 8], F32)
    lnb = AR.alloc("malnb", [128, 8], F32)
    rms = AR.alloc("marms", [128, 8], F32)
    cwc = AR.alloc("macwc", [128, 64], F32)
    cbc = AR.alloc("macbc", [128, 16], F32)
    dtb = AR.alloc("madtb", [128, 16], F32)
    Alog = AR.alloc("maAlog", [128, 256], F32)
    Dpc = AR.alloc("maDpc", [128, 16], F32)
    pob = AR.alloc("mapob", [128, 8], F32)
    H = [AR.alloc(f"maH{g}", [128, L], F32) for g in range(4)]
    hstate = AR.alloc("mahst", [128, 128], F32)
    for nm, t in [("ma_lng", lng), ("ma_lnb", lnb), ("ma_rms", rms), ("ma_convw", cwc),
                  ("ma_convb", cbc), ("ma_dtb", dtb), ("ma_Alog", Alog), ("ma_Dp", Dpc),
                  ("ma_pob", pob)]:
        P.dma("sync", t[:, :], dram[nm].ap(), wait=lnw if nm == "ma_lng" else (), inc="qD")
    wcst = P.cnt["qD"]
    av = P.op("scalar", "activation", Alog[:, :], Alog[:, :], AF.Exp, wait=[("qD", wcst)])
    av = P.op("vector", "tensor_scalar", Alog[:, :], Alog[:, :], -1.0, None, ALU.mult,
              wait=[("sS", av)])

    for g in range(4):
        wt = [("sV", av)] if g == 0 else ()
        P.op("vector", "tensor_tensor", H[g][:, :], X[g][:, tok(0, L)], rtot[:, :],
             ALU.mult, wait=wt)
        P.op("vector", "tensor_tensor", H[g][:, :], H[g][:, :], arow[:, :], ALU.subtract)
        P.op("vector", "tensor_scalar", H[g][:, :], H[g][:, :], lng[:, g:g + 1],
             lnb[:, g:g + 1], ALU.mult, op1=ALU.add)
    hiv = [("sV", P.cnt["sV"])]

    WIN = AR.alloc("maWIN", [128, 4 * 2048], BF16)
    WXP = AR.alloc("maWXP", [128, 8 * 64], BF16)
    WDT = AR.alloc("maWDT", [128, 1024], BF16)
    WMO = AR.alloc("maWMO", [128, 8 * 512], BF16)

    for li in range(DBG.get('ma_layers', NL)):
        guard = [("sT", P.cnt["sT"])] if li > 0 else list(hiv)
        P.dma("gpsimd", WIN[:, :], fap_d(dram["ma_inWT"], li * 512 * 2048,
              [[2048, 128], [128 * 2048, 4], [1, 2048]]), wait=guard, inc="qW")
        P.dma("gpsimd", WXP[:, :], fap_d(dram["ma_xpWT"], li * 1024 * 64,
              [[64, 128], [128 * 64, 8], [1, 64]]), inc="qW")
        P.dma("gpsimd", WDT[0:32, :], fap_d(dram["ma_dtWT"], li * 32 * 1024,
              [[1024, 32], [1, 1024]]), inc="qW")
        wlv = P.dma("gpsimd", WMO[:, :], fap_d(dram["ma_moWT"], li * 1024 * 512,
                    [[512, 128], [128 * 512, 8], [1, 512]]), inc="qW")
        zs = P.op("vector", "memset", hstate[:, :], 0.0,
                  wait=[("sG", P.cnt["sG"])])

        scr_free = []
        bb_free = {0: [], 1: []}
        da_free = {0: [], 1: []}
        sc_free = {0: [], 1: []}
        db_free = {0: [], 1: []}
        for mc in range(DBG.get('ma_chunks', NMC)):
            ts0 = mc * MC
            for e in ("vector", "gpsimd", "scalar"):
                P.wait(e, ("sT", P.cnt["sT"]), ("sV", P.cnt["sV"]), ("sG", P.cnt["sG"]))
            AR.push()
            hn = [AR.alloc(f"mahn{g}", [128, MC], BF16) for g in range(4)]
            rr = AR.alloc("marr", [128, MC], F32)
            XC = [AR.alloc(f"maXC{g}", [128, 4 + MC], F32) for g in range(8)]
            XTL = AR.alloc("maXTL", [128, 4 * 8], F32)
            XS = [AR.alloc(f"maXS{g}", [128, MC], F32) for g in range(8)]
            XSB = [AR.alloc(f"maXSB{g}", [128, MC], BF16) for g in range(8)]
            SZ = [AR.alloc(f"maSZ{g}", [128, MC], BF16) for g in range(8)]
            DT_ = [AR.alloc(f"maDT{g}", [128, MC], F32) for g in range(8)]
            UU = [AR.alloc(f"maUU{g}", [128, MC], BF16) for g in range(8)]
            YG = [AR.alloc(f"maYG{g}", [128, MC], F32) for g in range(8)]
            YB = [AR.alloc(f"maYB{g}", [128, MC], BF16) for g in range(8)]
            dtt = AR.alloc("madtt", [32, MC], BF16)
            DA = [AR.alloc(f"maDA{r}", [128, MC], F32) for r in range(2)]
            DB = [AR.alloc(f"maDB{r}", [128, MC], F32) for r in range(2)]
            SC = [AR.alloc(f"maSC{r}", [128, MC], F32) for r in range(2)]
            Bb = [AR.alloc(f"maBb{r}", [128, MC], F32) for r in range(2)]
            Cb = [AR.alloc(f"maCb{r}", [128, MC], F32) for r in range(2)]
            tmp = AR.alloc("matmp", [128, MC], F32)
            tmpg = AR.alloc("matmpg", [128, MC], F32)

            for g in range(4):
                P.op("scalar", "activation", hn[g][:, :], H[g][:, ts0:ts0 + MC], AF.Square)
            sqv = P.cnt["sS"]
            w = P.bank_wait(0) + [("sS", sqv)]
            for g in range(4):
                P.op("tensor", "matmul", PS[0][:, 0:MC], ctx.ONES[:, :], hn[g][:, :],
                     start=(g == 0), stop=(g == 3), wait=w if g == 0 else (),
                     inc="sT" if g == 3 else None)
            mmv = P.cnt["sT"]
            P.op("scalar", "activation", rr[:, :], PS[0][:, 0:MC], AF.Ln, bias=cc(EPS),
                 scale=cc(1.0 / DC), wait=[("sT", mmv)])
            sv = P.op("scalar", "activation", rr[:, :], rr[:, :], AF.Exp, scale=cc(-0.5))
            P.bank_done(0, "sS", sv)
            for g in range(4):
                P.op("vector", "tensor_tensor", tmp[:, :], H[g][:, ts0:ts0 + MC],
                     rr[:, :], ALU.mult, wait=[("sS", sv)] if g == 0 else ())
                P.op("vector", "tensor_scalar", hn[g][:, :], tmp[:, :],
                     rms[:, li * 4 + g:li * 4 + g + 1], None, ALU.mult)
            hnv = [("sV", P.cnt["sV"])]

            for m in range(16):
                b = m % 4
                w = P.bank_wait(b)
                if m == 0:
                    w = w + [("qW", wlv)] + hnv
                for k in range(4):
                    P.op("tensor", "matmul", PS[b][:, 0:MC],
                         WIN[:, k * 2048 + m * 128:k * 2048 + (m + 1) * 128],
                         hn[k][:, :], start=(k == 0), stop=(k == 3),
                         wait=w if k == 0 else (), inc="sT" if k == 3 else None)
                mmv = P.cnt["sT"]
                if m < 8:
                    sv = P.op("scalar", "activation", XC[m][:, 4:4 + MC], PS[b][:, 0:MC],
                              AF.Copy, wait=[("sT", mmv)])
                else:
                    sv = P.op("scalar", "activation", SZ[m - 8][:, :], PS[b][:, 0:MC],
                              AF.Silu, wait=[("sT", mmv)])
                P.bank_done(b, "sS", sv)
            xzv = P.cnt["sS"]

            if mc == 0:
                tlv = P.op("vector", "memset", XTL[:, :], 0.0)
            for g in range(8):
                eng = "vector" if g % 2 == 0 else "gpsimd"
                P.op(eng, "tensor_copy", XC[g][:, 0:4], XTL[:, g * 4:(g + 1) * 4],
                     wait=[("sS", xzv)] if g < 2 else ())
            tcv = [("sV", P.cnt["sV"]), ("sG", P.cnt["sG"])]
            for g in range(8):
                eng = "vector" if g % 2 == 0 else "gpsimd"
                P.op(eng, "tensor_copy", XTL[:, g * 4:(g + 1) * 4], XC[g][:, MC:MC + 4])
            tsv = [("sV", P.cnt["sV"]), ("sG", P.cnt["sG"])]

            for g in range(8):
                wt = tcv + tsv if g == 0 else ()
                P.op("vector", "tensor_scalar", XS[g][:, :], XC[g][:, 1:1 + MC],
                     cwc[:, li * 32 + 0 * 8 + g:li * 32 + 0 * 8 + g + 1], None, ALU.mult,
                     wait=wt)
                for kk2 in range(1, 4):
                    P.op("vector", "scalar_tensor_tensor", XS[g][:, :],
                         XC[g][:, 1 + kk2:1 + kk2 + MC],
                         cwc[:, li * 32 + kk2 * 8 + g:li * 32 + kk2 * 8 + g + 1],
                         XS[g][:, :], ALU.mult, ALU.add)
            cvv = [("sV", P.cnt["sV"])]
            for g in range(8):
                P.op("scalar", "activation", XS[g][:, :], XS[g][:, :], AF.Silu,
                     bias=cbc[:, li * 8 + g:li * 8 + g + 1], wait=cvv if g == 0 else ())
                P.op("scalar", "activation", XSB[g][:, :], XS[g][:, :], AF.Copy)
            xsv = P.cnt["sS"]

            w = P.bank_wait(4) + [("sS", xsv)]
            for k in range(8):
                P.op("tensor", "matmul", PS[4][0:64, 0:MC], WXP[:, k * 64:(k + 1) * 64],
                     XSB[k][:, :], start=(k == 0), stop=(k == 7),
                     wait=w if k == 0 else (), inc="sT" if k == 7 else None)
            mmv = P.cnt["sT"]
            dte = P.op("vector", "tensor_copy", dtt[0:32, :], PS[4][0:32, 0:MC],
                       wait=[("sT", mmv)])
            bce = P.op("vector", "tensor_copy", tmp[32:64, :], PS[4][32:64, 0:MC])
            P.bank_done(4, "sV", bce)
            bcd = P.dma("sync", fap_d(scr, 0, [[MC, 32], [1, MC]]),
                        tmp[32:64, :], wait=[("sV", bce)] + scr_free, inc="qD")
            scr_free = []

            for mg in range(8):
                b = mg % 2
                w = P.bank_wait(b)
                P.op("tensor", "matmul", PS[b][:, 0:MC],
                     WDT[0:32, mg * 128:(mg + 1) * 128], dtt[0:32, :],
                     start=True, stop=True, wait=w + ([("sV", dte)] if mg == 0 else []),
                     inc="sT")
                mmv = P.cnt["sT"]
                sv = P.op("scalar", "activation", DT_[mg][:, :], PS[b][:, 0:MC], AF.Abs,
                          bias=dtb[:, li * 8 + mg:li * 8 + mg + 1], wait=[("sT", mmv)])
                sv = P.op("scalar", "activation", tmp[:, :], DT_[mg][:, :], AF.Exp,
                          scale=cc(-1.0),
                          wait=[("sV", P.cnt["sV"])] if mg > 0 else [("qD", bcd)])
                sv = P.op("scalar", "activation", tmp[:, :], tmp[:, :], AF.Ln, bias=cc(1.0))
                rv = P.op("vector", "tensor_scalar", DT_[mg][:, :], PS[b][:, 0:MC],
                          dtb[:, li * 8 + mg:li * 8 + mg + 1], 0.0, ALU.add, op1=ALU.max,
                          wait=[("sS", sv)])
                rv = P.op("vector", "tensor_tensor", DT_[mg][:, :], DT_[mg][:, :],
                          tmp[:, :], ALU.add)
                P.bank_done(b, "sV", rv)
                P.op("gpsimd", "tensor_tensor", UU[mg][:, :], DT_[mg][:, :], XSB[mg][:, :],
                     ALU.mult, wait=[("sV", rv)])
            duv = [("sV", P.cnt["sV"]), ("sG", P.cnt["sG"])]

            for g in range(8):
                P.op("vector", "tensor_scalar", YG[g][:, :], XS[g][:, :],
                     Dpc[:, li * 8 + g:li * 8 + g + 1], None, ALU.mult,
                     wait=[("sS", xsv)] if g == 0 else ())
            ygv = P.cnt["sV"]

            for n in range(NS):
                r = n % 2
                bv = P.dma("sync", Bb[r][:, :], fap_d(scr, n * MC, [[0, 128], [1, MC]]),
                           wait=[("qD", bcd)] + bb_free[r], inc="qD")
                cv2 = P.dma("sync", Cb[r][:, :], fap_d(scr, (16 + n) * MC, [[0, 128], [1, MC]]),
                            inc="qD")
                if n == NS - 1:
                    scr_free = [("qD", cv2)]
                for g in range(8):
                    rg = (n * 8 + g) % 2
                    col = g * 16 + n
                    da = P.op("scalar", "activation", DA[rg][:, :], DT_[g][:, :], AF.Exp,
                              scale=Alog[:, li * 128 + col:li * 128 + col + 1],
                              wait=da_free[rg] + (duv if n == 0 and g == 0 else []))
                    da_free[rg] = []
                    db = P.op("gpsimd", "tensor_tensor", DB[rg][:, :], UU[g][:, :],
                              Bb[r][:, :], ALU.mult,
                              wait=([("qD", cv2)] if g == 0 else []) + db_free[rg])
                    db_free[rg] = []
                    scv = P.op("vector", "tensor_tensor_scan", SC[rg][:, :], DA[rg][:, :],
                               DB[rg][:, :], hstate[:, col:col + 1], ALU.mult, ALU.add,
                               wait=[("sS", da), ("sG", db)] + sc_free[rg] +
                                    ([("sV", ygv)] if n == 0 and g == 0 else []))
                    da_free[rg] = [("sV", scv)]
                    hsv = P.op("gpsimd", "tensor_copy", hstate[:, col:col + 1],
                               SC[rg][:, MC - 1:MC], wait=[("sV", scv)])
                    tv = P.op("vector", "tensor_tensor", DB[rg][:, :], SC[rg][:, :],
                              Cb[r][:, :], ALU.mult)
                    db_free[rg] = [("sV", tv)]
                    yv = P.op("vector", "tensor_tensor", YG[g][:, :], YG[g][:, :],
                              DB[rg][:, :], ALU.add)
                    sc_free[rg] = [("sV", yv), ("sG", hsv)]
                    if g == 7:
                        bb_free[r] = [("sV", yv), ("sG", P.cnt["sG"])]
            scan_done = P.cnt["sV"]

            for g in range(8):
                eng = "vector" if g % 2 == 0 else "gpsimd"
                wt = [("sV", scan_done)] if g < 2 else ()
                P.op(eng, "tensor_tensor", YB[g][:, :], YG[g][:, :], SZ[g][:, :],
                     ALU.mult, wait=wt)
            ybv = [("sV", P.cnt["sV"]), ("sG", P.cnt["sG"])]

            for m in range(4):
                b = 4 + m % 2
                w = P.bank_wait(b)
                if m == 0:
                    w = w + ybv
                for k in range(8):
                    P.op("tensor", "matmul", PS[b][:, 0:MC],
                         WMO[:, k * 512 + m * 128:k * 512 + (m + 1) * 128],
                         YB[k][:, :], start=(k == 0), stop=(k == 7),
                         wait=w if k == 0 else (), inc="sT" if k == 7 else None)
                mmv = P.cnt["sT"]
                hv2 = P.op("vector", "tensor_tensor", H[m][:, ts0:ts0 + MC],
                           PS[b][:, 0:MC], H[m][:, ts0:ts0 + MC], ALU.add,
                           wait=[("sT", mmv)])
                P.bank_done(b, "sV", hv2)
            xc_tail_guard = []
            AR.pop()

    if DBG.get("ma_skip_final"):
        AR.pop()
        return
    WPO = AR.alloc("maWPO", [128, 8 * 1024], BF16)
    CAT = [AR.alloc(f"maCAT{g}", [128, L], BF16) for g in range(8)]
    stmp = AR.alloc("mastmp", [128, CH], F32)
    wpo = P.dma("gpsimd", WPO[:, :], fap_d(dram["ma_poWT"], 0,
                [[1024, 128], [128 * 1024, 8], [1, 1024]]),
                wait=[("sT", P.cnt["sT"]), ("sV", P.cnt["sV"])], inc="qW")
    hdone = P.cnt["sV"]
    for g in range(8):
        wt = [("sV", hdone)] if g == 0 else ()
        P.op("vector", "tensor_tensor", CAT[g][:, :], X[g][:, tok(0, L)], rtot[:, :],
             ALU.mult, wait=wt)
        P.op("vector", "tensor_tensor", CAT[g][:, :], CAT[g][:, :], arow[:, :], ALU.subtract)
        P.op("vector", "tensor_scalar", CAT[g][:, :], CAT[g][:, :], lng[:, g:g + 1],
             lnb[:, g:g + 1], ALU.mult, op1=ALU.add)
        if g >= 4:
            P.op("vector", "tensor_tensor", CAT[g][:, :], CAT[g][:, :], H[g - 4][:, :],
                 ALU.add)
    catv = [("sV", P.cnt["sV"])]

    for ch in range(NCH):
        cs = tok(ch * CH)
        for m in range(8):
            w = P.bank_wait(m)
            if ch == 0 and m == 0:
                w = w + [("qW", wpo)] + catv
            for k in range(8):
                P.op("tensor", "matmul", PS[m][:, :],
                     WPO[:, k * 1024 + m * 128:k * 1024 + (m + 1) * 128],
                     CAT[k][:, ch * CH:(ch + 1) * CH], start=(k == 0), stop=(k == 7),
                     wait=w if k == 0 else (), inc="sT" if k == 7 else None)
            mmv = P.cnt["sT"]
            c0 = ch * CH
            P.op("vector", "tensor_tensor", stmp[:, :], X[m][:, cs], rtot[:, c0:c0 + CH],
                 ALU.mult, wait=[("sT", mmv)])
            P.op("vector", "tensor_tensor", stmp[:, :], stmp[:, :], arow[:, c0:c0 + CH],
                 ALU.subtract)
            P.op("vector", "tensor_scalar", stmp[:, :], stmp[:, :], lng[:, m:m + 1],
                 lnb[:, m:m + 1], ALU.mult, op1=ALU.add)
            P.op("vector", "tensor_tensor", stmp[:, :], stmp[:, :], X[m][:, cs], ALU.add)
            sv = P.op("vector", "scalar_tensor_tensor", X[m][:, cs], PS[m][:, :],
                      pob[:, m:m + 1], stmp[:, :], ALU.add, ALU.add)
            P.bank_done(m, "sV", sv)
    AR.pop()


@stage("s2")
def s2(ctx):
    emit_l0attn(ctx)



def emit_l1ffn(ctx, scale=0.5):
    P, X, PS, AR, dram, cc = ctx.P, ctx.X, ctx.PS, ctx.AR, ctx.dram, ctx.const_col
    DEPTH = DBG.get("lf_depth", 6)
    SKIP_FFN = DBG.get("lf_skip_ffn", False)
    SKIP_ATTN = DBG.get("lf_skip_attn", False)
    HID = 2816
    NJB = 6
    JBW = [512, 512, 512, 512, 512, 256]
    NKS = HID // 128

    xs = ctx.scratch_x
    for i in range(8):
        P.dma("sync", fap_d(xs, i * 128 * L, [[L, 128], [1, L]]), X[i][:, tok(0, L)],
              inc="qD")
    xsv = P.cnt["qD"]
    for e in ("vector", "gpsimd", "scalar"):
        P.wait(e, ("qD", xsv))
    lnw = emit_ln_xn(ctx, None)

    AR.push()
    lng = AR.alloc("lflng", [128, 48], F32)
    lnb = AR.alloc("lflnb", [128, 48], F32)
    b1c = AR.alloc("lfb1", [128, 264], F32)
    b2c = AR.alloc("lfb2", [128, 48], F32)
    qsc = AR.alloc("lfqs", [128, 6], F32)
    ksc = AR.alloc("lfks", [128, 6], F32)
    for nm, t in [("lf_lng", lng), ("lf_lnb", lnb), ("lf_b1", b1c),
                  ("lf_b2", b2c), ("lf_qs", qsc), ("lf_ks", ksc)]:
        P.dma("sync", t[:, :], dram[nm].ap(), wait=lnw if nm == "lf_lng" else (), inc="qD")
    cstv = P.cnt["qD"]

    prev_done = [("qD", cstv)]
    for li in range(DEPTH):
        AR.push()
        ysb = [AR.alloc(f"lfys{i}", [128, L], BF16) for i in range(8)]
        cq = AR.alloc("lfcq", [128, L], BF16)
        sq_t = AR.alloc("lfsqt", [128, L], BF16)
        ck = AR.alloc("lfck", [128, L], BF16)
        sk = AR.alloc("lfsk", [128, L], BF16)
        ROT = AR.alloc("lfrot", [128, 128], BF16)
        HON = AR.alloc("lfhon", [128, 128], BF16)
        M384 = AR.alloc("lfm384", [128, 384], F32)
        cguard = list(prev_done) if li > 0 else [("sT", P.cnt["sT"])]
        for nm, t in [("rot_cq", cq), ("rot_sqp", sq_t), ("rot_ck", ck), ("rot_skp", sk),
                      ("rotmat", ROT), ("hones2", HON), ("m384f", M384)]:
            P.dma("sync", t[:, :], dram[nm].ap(),
                  wait=cguard if nm == "rot_cq" else (), inc="qD")
        cstv2 = P.cnt["qD"]
        for i in range(8):
            eng = "vector" if i % 2 == 0 else "gpsimd"
            wt = []
            if i < 2:
                wt = list(prev_done) + [("qD", cstv2)]
                if li == 0:
                    wt += lnw
            sh = GUARD - (1 if i >= 4 else 0)
            P.op(eng, "tensor_copy", ysb[i][:, :], X[i][:, sh:sh + L], wait=wt)
        ysv = [("sV", P.cnt["sV"]), ("sG", P.cnt["sG"])]

        for half in range(2 if not SKIP_ATTN else 0):
            AR.push()
            W0 = AR.alloc("lfW0", [128, 4096], BF16)
            W1_ = AR.alloc("lfW1x", [128, 4096], BF16)
            qt_ = [AR.alloc(f"lfq{m}", [128, L], BF16) for m in range(4)]
            kt_ = [AR.alloc(f"lfk{m}", [128, L], BF16) for m in range(4)]
            vt = [AR.alloc(f"lfv{t}", [128, 512], BF16) for t in range(16)]
            osb = [AR.alloc(f"lfo{m}", [128, L], BF16) for m in range(4)]
            mark = AR.off
            qq = AR.alloc("lfqq", [128, L], BF16)
            rq = AR.alloc("lfrq", [128, 512], F32)
            qq_end = AR.off
            AR.off = mark
            lg = AR.alloc("lflg", [128, 512], F32)
            lg2 = AR.alloc("lflg2", [128, 512], F32)
            PT = AR.alloc("lfPT", [128, 384], BF16)
            PT2 = AR.alloc("lfPT2", [128, 384], BF16)
            rowm = AR.alloc("lfrowm", [128, 2], F32)
            rowm2 = AR.alloc("lfrowm2", [128, 2], F32)
            AR.off = max(AR.off, qq_end)
            sqb = vt

            base = li * 1024 * 3072 + half * 512
            guard = [("sT", P.cnt["sT"])]
            wvq = P.dma("gpsimd", W0[:, :], fap_d(dram["lf_wqkv"], base,
                        [[3072, 128], [128 * 3072, 8], [1, 512]]), wait=guard, inc="qW")
            wvk = P.dma("gpsimd", W1_[:, :], fap_d(dram["lf_wqkv"], base + 1024,
                        [[3072, 128], [128 * 3072, 8], [1, 512]]), inc="qW")
            WQ, WK, WV, WO = W0, W1_, W0, W1_

            def qk_pipeline(Wbuf, wv, out_tiles, rtab_c, rtab_s, sc_col, hon_guard):
                for m in range(4):
                    qq_guard = [("sG", P.cnt["sG"])]
                    for ch in range(NCH):
                        b = (m * NCH + ch) % 2
                        w = P.bank_wait(b)
                        if m == 0 and ch == 0:
                            w = w + [("qW", wv)] + ysv
                        for k in range(8):
                            P.op("tensor", "matmul", PS[b][:, :],
                                 Wbuf[:, k * 512 + m * 128: k * 512 + (m + 1) * 128],
                                 ysb[k][:, ch * CH:(ch + 1) * CH], start=(k == 0),
                                 stop=(k == 7), wait=w if k == 0 else (),
                                 inc="sT" if k == 7 else None)
                        mmv = P.cnt["sT"]
                        sv = P.op("scalar", "activation", qq[:, ch * CH:(ch + 1) * CH],
                                  PS[b][:, :], AF.Copy,
                                  wait=[("sT", mmv)] + (qq_guard if ch == 0 else []))
                        P.bank_done(b, "sS", sv)
                    for ch in range(NCH):
                        P.op("scalar", "activation", sqb[ch][:, :],
                             qq[:, ch * CH:(ch + 1) * CH], AF.Square,
                             wait=[("sT", hon_guard[0])] if ch == 0 else ())
                    sqv = P.cnt["sS"]
                    for ch in range(NCH):
                        b2_ = 2 + (ch % 2)
                        w = P.bank_wait(b2_)
                        P.op("tensor", "matmul", PS[b2_][:, :], HON[:, :],
                             sqb[ch][:, :], start=True, stop=True,
                             wait=w + [("sS", sqv)], inc="sT")
                        hon_guard[0] = P.cnt["sT"]
                        mmv = P.cnt["sT"]
                        P.op("scalar", "activation", rq[:, :], PS[b2_][:, :], AF.Ln,
                             bias=cc(1e-30),
                             wait=[("sT", mmv), ("sV", P.cnt["sV"])])
                        sv = P.op("scalar", "activation", rq[:, :], rq[:, :], AF.Exp,
                                  scale=cc(-0.5))
                        P.bank_done(b2_, "sS", sv)
                        P.op("vector", "tensor_scalar", rq[:, :], rq[:, :], 1e12, sc_col,
                             ALU.min, op1=ALU.mult, wait=[("sS", sv)])
                        P.op("vector", "tensor_tensor", qq[:, ch * CH:(ch + 1) * CH],
                             qq[:, ch * CH:(ch + 1) * CH], rq[:, :], ALU.mult)
                        qrv = P.op("vector", "tensor_tensor",
                                   out_tiles[m][:, ch * CH:(ch + 1) * CH],
                                   qq[:, ch * CH:(ch + 1) * CH],
                                   rtab_s[:, ch * CH:(ch + 1) * CH], ALU.mult)
                        b3 = 4 + (ch % 2)
                        w = P.bank_wait(b3)
                        P.op("tensor", "matmul", PS[b3][:, :], ROT[:, :],
                             out_tiles[m][:, ch * CH:(ch + 1) * CH], start=True,
                             stop=True, wait=w + [("sV", qrv)], inc="sT")
                        mmv = P.cnt["sT"]
                        gv1 = P.op("gpsimd", "tensor_tensor", qq[:, ch * CH:(ch + 1) * CH],
                                   qq[:, ch * CH:(ch + 1) * CH],
                                   rtab_c[:, ch * CH:(ch + 1) * CH], ALU.mult,
                                   wait=[("sV", qrv)])
                        vv1 = P.op("vector", "tensor_tensor",
                                   out_tiles[m][:, ch * CH:(ch + 1) * CH],
                                   PS[b3][:, :], qq[:, ch * CH:(ch + 1) * CH], ALU.add,
                                   wait=[("sT", mmv), ("sG", gv1)])
                        P.bank_done(b3, "sV", vv1)

            hon_guard = [0]
            qk_pipeline(WQ, wvq, qt_, cq, sq_t, qsc[:, li:li + 1], hon_guard)
            wvv = P.dma("gpsimd", WV[:, :], fap_d(dram["lf_wqkv"], base + 2048,
                        [[3072, 128], [128 * 3072, 8], [1, 512]]),
                        wait=[("sT", P.cnt["sT"])], inc="qW")
            qk_pipeline(WK, wvk, kt_, ck, sk, ksc[:, li:li + 1], hon_guard)
            wvo = P.dma("gpsimd", WO[:, :], fap_d(dram["lf_wo"],
                        li * 1024 * 1024 + half * 512 * 1024,
                        [[1024, 128], [128 * 1024, 4], [1, 1024]]),
                        wait=[("sT", P.cnt["sT"])], inc="qW")

            for tt in range(16):
                b = tt % 2
                w = P.bank_wait(b)
                if tt == 0:
                    w = w + [("qW", wvv)]
                for k in range(8):
                    P.op("tensor", "matmul", PS[b][:, :],
                         ysb[k][:, tt * 128:(tt + 1) * 128],
                         WV[:, k * 512:(k + 1) * 512], start=(k == 0), stop=(k == 7),
                         wait=w if k == 0 else (), inc="sT" if k == 7 else None)
                mmv = P.cnt["sT"]
                sv = P.op("scalar", "activation", vt[tt][:, :], PS[b][:, :], AF.Copy,
                          wait=[("sT", mmv)])
                P.bank_done(b, "sS", sv)

            QKS = 8.0
            lgs = [lg, lg2]
            PTs = [PT, PT2]
            rws = [rowm, rowm2]
            qk_done = [("sG", P.cnt["sG"]), ("sV", P.cnt["sV"]), ("sT", P.cnt["sT"])]
            lg_free = {0: list(qk_done), 1: list(qk_done)}
            pt_free = {0: list(qk_done), 1: list(qk_done)}
            rw_free = {0: list(qk_done), 1: list(qk_done)}
            inst = 0
            for hh in range(8):
                ft = hh // 2
                pb = (hh % 2) * 64
                for qti in range(16):
                    ctx0 = max(0, qti * 128 - 256)
                    cw_ = qti * 128 + 128 - ctx0
                    nct = cw_ // 128
                    r = inst % 2
                    inst += 1
                    lgr, ptr, rwr = lgs[r], PTs[r], rws[r]
                    b = r
                    w = P.bank_wait(b)
                    P.op("tensor", "matmul", PS[b][:, 0:cw_],
                         qt_[ft][pb:pb + 64, qti * 128:(qti + 1) * 128],
                         kt_[ft][pb:pb + 64, ctx0:ctx0 + cw_],
                         start=True, stop=True, wait=w, inc="sT")
                    mmv = P.cnt["sT"]
                    sv = P.op("scalar", "activation", lgr[:, 0:cw_], PS[b][:, 0:cw_],
                              AF.Copy, scale=cc(QKS),
                              wait=[("sT", mmv)] + lg_free[r])
                    lg_free[r] = []
                    P.bank_done(b, "sS", sv)
                    gv_ = P.op("gpsimd", "tensor_tensor", lgr[:, 0:cw_], lgr[:, 0:cw_],
                               M384[:, 384 - cw_:384], ALU.add, wait=[("sS", sv)])
                    rv = P.op("vector", "tensor_reduce", rwr[:, 0:1], lgr[:, 0:cw_],
                              AX.X, ALU.max, wait=[("sG", gv_)] + rw_free[r])
                    rw_free[r] = []
                    nm = P.op("gpsimd", "tensor_scalar", rwr[:, 1:2], rwr[:, 0:1],
                              -1.0, None, ALU.mult, wait=[("sV", rv)])
                    se = P.op("scalar", "activation", lgr[:, 0:cw_], lgr[:, 0:cw_],
                              AF.Exp, bias=rwr[:, 1:2], accum_out=rwr[:, 0:1],
                              wait=[("sG", nm)])
                    P.op("vector", "reciprocal", rwr[:, 1:2], rwr[:, 0:1],
                         wait=[("sS", se)])
                    nv = P.op("vector", "tensor_scalar", lgr[:, 0:cw_], lgr[:, 0:cw_],
                              rwr[:, 1:2], None, ALU.mult)
                    rw_free[r] = [("sV", nv)]
                    for t2 in range(nct):
                        b3 = 2 + (t2 % 2)
                        w = P.bank_wait(b3)
                        P.op("tensor", "transpose", PS[b3][:, 0:128],
                             lgr[:, t2 * 128:(t2 + 1) * 128], ctx.IDNF[:, :],
                             wait=(w + [("sV", nv)] + pt_free[r]) if t2 < 2 else w,
                             inc="sT")
                        pt_free[r] = []
                        trv = P.cnt["sT"]
                        sv2 = P.op("vector", "tensor_copy", ptr[:, t2 * 128:(t2 + 1) * 128],
                                   PS[b3][:, 0:128], wait=[("sT", trv)])
                        P.bank_done(b3, "sV", sv2)
                    ptv = P.cnt["sV"]
                    lg_free[r] = [("sT", P.cnt["sT"])]
                    w4 = P.bank_wait(4)
                    for t2 in range(nct):
                        P.op("tensor", "matmul", PS[4][pb:pb + 64, 0:128],
                             vt[ctx0 // 128 + t2][:, hh * 64:(hh + 1) * 64],
                             ptr[:, t2 * 128:(t2 + 1) * 128],
                             start=(t2 == 0), stop=(t2 == nct - 1),
                             tile_position=(0, pb),
                             wait=(w4 + [("sV", ptv)]) if t2 == 0 else (),
                             inc="sT" if t2 == nct - 1 else None)
                    mmv = P.cnt["sT"]
                    pt_free[r] = [("sT", mmv)]
                    sv = P.op("scalar", "activation",
                              osb[ft][pb:pb + 64, qti * 128:(qti + 1) * 128],
                              PS[4][pb:pb + 64, 0:128], AF.Copy, wait=[("sT", mmv)])
                    P.bank_done(4, "sS", sv)

            ov = P.cnt["sS"]
            for ch in range(NCH):
                cs = tok(ch * CH)
                for m in range(8):
                    w = P.bank_wait(m)
                    if ch == 0 and m == 0:
                        w = w + [("qW", wvo), ("sS", ov)]
                    for k in range(4):
                        P.op("tensor", "matmul", PS[m][:, :],
                             WO[:, k * 1024 + m * 128: k * 1024 + (m + 1) * 128],
                             osb[k][:, ch * CH:(ch + 1) * CH], start=(k == 0),
                             stop=(k == 3), wait=w if k == 0 else (),
                             inc="sT" if k == 3 else None)
                    mmv = P.cnt["sT"]
                    sv = P.op("vector", "tensor_tensor", X[m][:, cs], PS[m][:, :],
                              X[m][:, cs], ALU.add, wait=[("sT", mmv)])
                    P.bank_done(m, "sV", sv)
            AR.pop()
        AR.pop()
        attn_done = P.cnt["sV"]

        if SKIP_FFN:
            break
        for e in ("vector", "gpsimd", "scalar"):
            P.wait(e, ("sV", attn_done))
        gb = [(lng[:, li * 8 + i:li * 8 + i + 1], lnb[:, li * 8 + i:li * 8 + i + 1])
              for i in range(8)]
        for th in range(2):
            t0 = th * 1024
            for e in ("vector", "gpsimd", "scalar"):
                P.wait(e, ("sV", P.cnt["sV"]))
            AR.push()
            xn2 = [AR.alloc(f"lfxn{i}", [128, 1024], BF16) for i in range(8)]
            lnw2 = emit_ln_xn(ctx, xn2, gamma_beta=gb, tok0=t0, ntok=1024,
                              shifts=[0, 0, 0, 0, -1, -1, -1, -1])
            if DBG.get("lf_dump_xn2"):
                for i in range(8):
                    P.op("vector", "tensor_copy", X[i][:, tok(t0, 1024)], xn2[i][:, :],
                         wait=lnw2 if i == 0 else ())
                return
            hbuf = [AR.alloc(f"lfh{k}", [128, 1024], BF16) for k in range(NKS)]
            W1A = AR.alloc("lfW1A", [128, 4096], BF16)
            W1G = AR.alloc("lfW1G", [128, 4096], BF16)
            W2F = AR.alloc("lfW2F", [128, NKS * 512], BF16)
            gel = [AR.alloc("lfgel0", [128, 512], F32), AR.alloc("lfgel1", [128, 512], F32)]
            gel_free = {0: [], 1: []}
            gi = 0

            w1base = li * 1024 * 5632

            def load_w1(jb, guard):
                jw = JBW[jb]
                a1 = fap_d(dram["lf_w1"], w1base + jb * 512,
                           [[5632, 128], [128 * 5632, 8], [1, jw]])
                P.dma("gpsimd", W1A[:, 0:8 * jw], a1, wait=guard, inc="qW")
                a2 = fap_d(dram["lf_w1"], w1base + 2816 + jb * 512,
                           [[5632, 128], [128 * 5632, 8], [1, jw]])
                return P.dma("gpsimd", W1G[:, 0:8 * jw], a2, inc="qW")

            def load_w2(mh, guard):
                return P.dma("gpsimd", W2F[:, :],
                             fap_d(dram["lf_w2"], li * 2816 * 1024 + mh * 512,
                                   [[1024, 128], [128 * 1024, NKS], [1, 512]]),
                             wait=guard, inc="qW")

            wv1 = load_w1(0, lnw2)
            ks0 = 0
            for jb in range(NJB):
                jw = JBW[jb]
                nmt = jw // 128
                for sub in range(2):
                    ss = slice(sub * 512, (sub + 1) * 512)
                    for mt in range(nmt):
                        ba = mt % 2
                        bgt = 2 + mt % 2
                        w = P.bank_wait(ba)
                        if sub == 0 and mt == 0:
                            w = w + [("qW", wv1)]
                        for k in range(8):
                            P.op("tensor", "matmul", PS[ba][:, :],
                                 W1A[:, k * jw + mt * 128: k * jw + (mt + 1) * 128],
                                 xn2[k][:, ss], start=(k == 0), stop=(k == 7),
                                 wait=w if k == 0 else ())
                        w = P.bank_wait(bgt)
                        for k in range(8):
                            P.op("tensor", "matmul", PS[bgt][:, :],
                                 W1G[:, k * jw + mt * 128: k * jw + (mt + 1) * 128],
                                 xn2[k][:, ss], start=(k == 0), stop=(k == 7),
                                 wait=w if k == 0 else (), inc="sT" if k == 7 else None)
                        mmv = P.cnt["sT"]
                        kst = ks0 + mt
                        gr = gi % 2
                        gi += 1
                        sv = P.op("scalar", "activation", gel[gr][:, :], PS[bgt][:, :],
                                  AF.Gelu,
                                  bias=b1c[:, li * 44 + 22 + kst:li * 44 + 23 + kst],
                                  wait=[("sT", mmv)] + gel_free[gr])
                        P.bank_done(bgt, "sS", sv)
                        hv = P.op("vector", "scalar_tensor_tensor", hbuf[kst][:, ss],
                                  PS[ba][:, :], b1c[:, li * 44 + kst:li * 44 + 1 + kst],
                                  gel[gr][:, :], ALU.add, ALU.mult, wait=[("sS", sv)])
                        gel_free[gr] = [("sV", hv)]
                        P.bank_done(ba, "sV", hv)
                if jb + 1 < NJB:
                    wv1 = load_w1(jb + 1, [("sT", P.cnt["sT"])])
                ks0 += nmt
            h_all = P.cnt["sV"]

            if DBG.get("lf_dump_h"):
                for i in range(8):
                    P.op("vector", "tensor_copy", X[i][:, tok(t0, 1024)],
                         hbuf[i + DBG.get("lf_dump_h_off", 0)][:, :],
                         wait=[("sV", h_all)] if i == 0 else ())
                return
            wv2 = load_w2(0, [("sV", h_all)] if th == 0 and li == 0 else
                          [("sT", P.cnt["sT"])])
            for mh in range(2):
                for k in range(NKS):
                    for sub in range(2):
                        for m4 in range(4):
                            bb = sub * 4 + m4
                            w = []
                            if k == 0:
                                w = P.bank_wait(bb)
                                if sub == 0 and m4 == 0:
                                    w = w + [("sV", h_all), ("qW", wv2)]
                            P.op("tensor", "matmul", PS[bb][:, :],
                                 W2F[:, k * 512 + m4 * 128:k * 512 + (m4 + 1) * 128],
                                 hbuf[k][:, sub * 512:(sub + 1) * 512],
                                 start=(k == 0), stop=(k == NKS - 1),
                                 wait=w, inc="sT" if (sub == 1 and m4 == 3) else None)
                mm_all = P.cnt["sT"]
                if mh == 0:
                    wv2 = load_w2(1, [("sT", mm_all)])
                for sub in range(2):
                    for m4 in range(4):
                        bb = sub * 4 + m4
                        m = mh * 4 + m4
                        cs = tok(t0 + sub * 512)
                        sv = P.op("vector", "scalar_tensor_tensor", X[m][:, cs],
                                  PS[bb][:, :], b2c[:, li * 8 + m:li * 8 + m + 1],
                                  X[m][:, cs], ALU.add, ALU.add,
                                  wait=[("sT", mm_all)] if (sub == 0 and m4 == 0) else ())
                        P.bank_done(bb, "sV", sv)
            AR.pop()
        prev_done = [("sV", P.cnt["sV"])]

    tmpx = AR.alloc("lftmpx", [128, L], F32)
    for m in range(8):
        wv = P.dma("sync", tmpx[:, :], fap_d(xs, m * 128 * L, [[L, 128], [1, L]]),
                   wait=[("sV", P.cnt["sV"])], inc="qD")
        P.op("vector", "scalar_tensor_tensor", X[m][:, tok(0, L)], X[m][:, tok(0, L)],
             scale, tmpx[:, :], ALU.mult, ALU.add, wait=[("qD", wv)])
    AR.pop()


@stage("s4")
def s4(ctx):
    emit_l1ffn(ctx)


@stage("s6")
def s6(ctx):
    emit_l1ffn(ctx)



def emit_l1attn(ctx):
    P, X, PS, AR, dram, cc = ctx.P, ctx.X, ctx.PS, ctx.AR, ctx.dram, ctx.const_col
    AR.push()
    xn = [AR.alloc(f"laxn{i}", [128, L], BF16) for i in range(8)]
    lnw = emit_ln_xn(ctx, xn)

    bo = AR.alloc("labo", [128, 8], F32)
    dtri = AR.alloc("ladtri", [128, 128], F32)
    P.dma("sync", bo[:, :], dram["att_bo"].ap(), wait=lnw, inc="qD")
    P.dma("sync", dtri[:, :], dram["dtri"].ap(), inc="qD")
    wcst = P.cnt["qD"]

    GW = 2824
    for half in range(2):
        AR.push()
        q = [AR.alloc(f"laq{m}", [128, L], BF16) for m in range(4)]
        kk = [AR.alloc(f"lak{m}", [128, L], BF16) for m in range(4)]
        vt = [AR.alloc(f"lav{t}", [128, 512], BF16) for t in range(16)]
        WA = AR.alloc("laWA", [128, 4096], BF16)
        WB = AR.alloc("laWB", [128, 4096], BF16)
        osb = [AR.alloc(f"lao{m}", [128, L], BF16) for m in range(4)]
        G = AR.alloc("laG", [128, GW], F32)
        lg = AR.alloc("lalg", [128, L], F32)
        PT = AR.alloc("laPT", [128, L], BF16)
        rowm = AR.alloc("larow", [128, 2], F32)

        def load_whalf(name, buf, guard):
            t = dram[name]
            ap = fap_d(t, half * 512, [[1024, 128], [128 * 1024, 8], [1, 512]])
            return P.dma("gpsimd", buf[:, :], ap, wait=guard, inc="qW")

        def load_wo_half(buf, guard):
            t = dram["att_wo"]
            ap = fap_d(t, half * 512 * 1024, [[1024, 128], [128 * 1024, 4], [1, 1024]])
            return P.dma("gpsimd", buf[:, :], ap, wait=guard, inc="qW")

        hguard = lnw if half == 0 else [("sT", P.cnt["sT"])]
        wva = load_whalf("att_wq", WA, hguard)
        wvb = load_whalf("att_wk", WB, [])

        def proj(buf, wv, out_tiles, sem_after):
            for ch in range(NCH):
                for m in range(4):
                    b = (ch * 4 + m) % 4
                    w = P.bank_wait(b)
                    if ch == 0 and m == 0:
                        w = w + [("qW", wv)]
                    for k in range(8):
                        P.op("tensor", "matmul", PS[b][:, :],
                             buf[:, k * 512 + m * 128: k * 512 + (m + 1) * 128],
                             xn[k][:, ch * CH:(ch + 1) * CH], start=(k == 0), stop=(k == 7),
                             wait=w if k == 0 else (), inc="sT" if k == 7 else None)
                    mmv = P.cnt["sT"]
                    sv = P.op("scalar", "activation", out_tiles[m][:, ch * CH:(ch + 1) * CH],
                              PS[b][:, :], AF.Copy, wait=[("sT", mmv)])
                    P.bank_done(b, "sS", sv)
            return P.cnt["sT"]

        qmm = proj(WA, wva, q, None)
        wva2 = load_whalf("att_wv", WA, [("sT", qmm)])
        kmm = proj(WB, wvb, kk, None)
        wvb2 = load_wo_half(WB, [("sT", kmm)])

        for tt in range(16):
            b = tt % 4
            w = P.bank_wait(b)
            if tt == 0:
                w = w + [("qW", wva2)]
            for k in range(8):
                P.op("tensor", "matmul", PS[b][:, :],
                     xn[k][:, tt * 128:(tt + 1) * 128],
                     WA[:, k * 512:(k + 1) * 512], start=(k == 0), stop=(k == 7),
                     wait=w if k == 0 else (), inc="sT" if k == 7 else None)
            mmv = P.cnt["sT"]
            sv = P.op("scalar", "activation", vt[tt][:, :], PS[b][:, :], AF.Copy,
                      wait=[("sT", mmv)])
            P.bank_done(b, "sS", sv)

        g_guard = [("qD", wcst)] if half == 0 else [("sV", P.cnt["sV"])]
        for hh in range(8):
            h = half * 8 + hh
            gv = P.dma("sync", G[:, :], fap_d(dram["att_T"], h * 2952, [[1, 128], [1, GW]]),
                       wait=g_guard, inc="qD")
            ft = hh // 2
            pb = (hh % 2) * 64
            for qt in range(16):
                kw = (qt + 1) * 128
                nkc = (kw + 511) // 512
                for kc in range(nkc):
                    ks = min(512, kw - kc * 512)
                    b = kc % 2
                    w = P.bank_wait(b)
                    P.op("tensor", "matmul", PS[b][:, 0:ks],
                         q[ft][pb:pb + 64, qt * 128:(qt + 1) * 128],
                         kk[ft][pb:pb + 64, kc * 512:kc * 512 + ks],
                         start=True, stop=True, wait=w + [("qD", gv)] if kc == 0 else w,
                         inc="sT")
                    mmv = P.cnt["sT"]
                    base = qt * 128 - kc * 512 + 896
                    sv = P.op("vector", "tensor_tensor", lg[:, kc * 512:kc * 512 + ks],
                              PS[b][:, 0:ks], fap(G, base, [[-1, ks]]), ALU.add,
                              wait=[("sT", mmv)])
                    P.bank_done(b, "sV", sv)
                P.op("vector", "tensor_tensor", lg[:, kw - 128:kw], lg[:, kw - 128:kw],
                     dtri[:, :], ALU.min,
                     wait=[("qD", wcst)] if (hh == 0 and qt == 0 and half == 0) else ())
                P.op("vector", "tensor_reduce", rowm[:, 0:1], lg[:, 0:kw], AX.X, ALU.max)
                nm = P.op("vector", "tensor_scalar", rowm[:, 1:2], rowm[:, 0:1],
                          -1.0 / 64.0, None, ALU.mult)
                se = P.op("scalar", "activation", lg[:, 0:kw], lg[:, 0:kw], AF.Exp,
                          scale=cc(1.0 / 64.0), bias=rowm[:, 1:2], accum_out=rowm[:, 0:1],
                          wait=[("sV", nm)])
                P.op("vector", "reciprocal", rowm[:, 1:2], rowm[:, 0:1], wait=[("sS", se)])
                nv = P.op("vector", "tensor_scalar", lg[:, 0:kw], lg[:, 0:kw],
                          rowm[:, 1:2], None, ALU.mult)
                for t2 in range(qt + 1):
                    b = 2 + (t2 % 2)
                    w = P.bank_wait(b)
                    P.op("tensor", "transpose", PS[b][:, 0:128], lg[:, t2 * 128:(t2 + 1) * 128],
                         ctx.IDNF[:, :], wait=(w + [("sV", nv)]) if t2 < 2 else w,
                         inc="sT")
                    trv = P.cnt["sT"]
                    sv = P.op("vector", "tensor_copy", PT[:, t2 * 128:(t2 + 1) * 128],
                              PS[b][:, 0:128], wait=[("sT", trv)])
                    P.bank_done(b, "sV", sv)
                ptv = P.cnt["sV"]
                w4 = P.bank_wait(4)
                for t2 in range(qt + 1):
                    P.op("tensor", "matmul", PS[4][pb:pb + 64, 0:128],
                         vt[t2][:, hh * 64:(hh + 1) * 64], PT[:, t2 * 128:(t2 + 1) * 128],
                         start=(t2 == 0), stop=(t2 == qt),
                         tile_position=(0, pb),
                         wait=(w4 + [("sV", ptv)]) if t2 == 0 else (),
                         inc="sT" if t2 == qt else None)
                mmv = P.cnt["sT"]
                sv = P.op("scalar", "activation", osb[ft][pb:pb + 64, qt * 128:(qt + 1) * 128],
                          PS[4][pb:pb + 64, 0:128], AF.Copy, wait=[("sT", mmv)])
                P.bank_done(4, "sS", sv)
            g_guard = [("sV", P.cnt["sV"])]

        ov = P.cnt["sS"]
        for ch in range(NCH):
            cs = tok(ch * CH)
            for m in range(8):
                w = P.bank_wait(m)
                if ch == 0 and m == 0:
                    w = w + [("qW", wvb2), ("sS", ov)]
                for k in range(4):
                    P.op("tensor", "matmul", PS[m][:, :],
                         WB[:, k * 1024 + m * 128: k * 1024 + (m + 1) * 128],
                         osb[k][:, ch * CH:(ch + 1) * CH], start=(k == 0), stop=(k == 3),
                         wait=w if k == 0 else (), inc="sT" if k == 3 else None)
                mmv = P.cnt["sT"]
                if half == 0:
                    sv = P.op("vector", "scalar_tensor_tensor", X[m][:, cs], PS[m][:, :],
                              bo[:, m:m + 1], X[m][:, cs], ALU.add, ALU.add,
                              wait=[("sT", mmv)])
                else:
                    sv = P.op("vector", "tensor_tensor", X[m][:, cs], PS[m][:, :],
                              X[m][:, cs], ALU.add, wait=[("sT", mmv)])
                P.bank_done(m, "sV", sv)
        AR.pop()
    AR.pop()


@stage("s5")
def s5(ctx):
    emit_l1attn(ctx)




def build_nc(weights_meta, stages):
    nc = bass.Bass("TRN2", target_bir_lowering=False,
                   detect_race_conditions=not DBG.get("no_race_detect", False))
    P = Prog(nc)

    dram = {}
    for name, (shape, dt) in weights_meta.items():
        dram[name] = nc.dram_tensor(name, list(shape), dt, kind="ExternalInput")
    xout = nc.dram_tensor("xout", [D, L], F32, kind="ExternalOutput")

    X = [nc.alloc_sbuf_tensor(f"x{i}", [128, XW], F32) for i in range(8)]
    CONST = nc.alloc_sbuf_tensor("consts", [128, 16], F32)
    IDN = nc.alloc_sbuf_tensor("sb_idn", [128, 128], BF16)
    IDNF = nc.alloc_sbuf_tensor("sb_idnf", [128, 128], F32)
    ONES = nc.alloc_sbuf_tensor("ones", [128, 128], BF16)

    ctx = Ctx()
    ctx.nc, ctx.P, ctx.dram, ctx.X = nc, P, dram, X
    ctx.IDN, ctx.IDNF, ctx.ONES = IDN, IDNF, ONES
    ctx.xout = xout
    ctx.scratch_cw = nc.dram_tensor("scratch_cw", [8, L], F32, kind="Internal")
    ctx.scratch_x = nc.dram_tensor("scratch_x", [D, L], F32, kind="Internal")
    ctx.scratch_bc = nc.dram_tensor("scratch_bc", [32, 256], F32, kind="Internal")
    ctx.AR = Arena(nc, 142 * 1024)
    ctx.PS = [nc.alloc_psum_tensor(f"ps{i}", [128, 512], F32) for i in range(8)]

    CONSTS = {}

    def const_col(val):
        v = float(val)
        if v not in CONSTS:
            idx = len(CONSTS)
            assert idx < 16
            CONSTS[v] = idx
            P.op("vector", "memset", CONST[:, idx:idx + 1], v)
        return CONST[:, CONSTS[v]:CONSTS[v] + 1]
    ctx.const_col = const_col
    const_col(EPS)
    const_col(-0.5)

    for i in range(8):
        P.op("vector", "memset", X[i][:, 0:GUARD], 0.0)
    P.op("gpsimd", "memset", ONES[:, :], 1.0)
    P.dma("sync", IDN[:, :], dram["idn"].ap(), inc="qS")
    P.dma("sync", IDNF[:, :], dram["idnf"].ap(), inc="qS")
    for i in range(8):
        P.dma("sync", X[i][:, GUARD:GUARD + L], dram["x"].ap()[i * 128:(i + 1) * 128, :],
              inc="qS")
    qv = P.cnt["qS"]
    vv = P.cnt["sV"]
    gv = P.cnt["sG"]
    for e in ENGINES:
        P.wait(e, ("qS", qv), ("sV", vv), ("sG", gv))

    for st in stages:
        STAGE_FNS[st](ctx)
        P.barrier()

    for i in range(8):
        P.dma("sync", xout.ap()[i * 128:(i + 1) * 128, :], X[i][:, GUARD:GUARD + L],
              inc="qOut")
    P.wait("sync", ("qOut", P.cnt["qOut"]))

    P.build()
    return nc



def prep_core_inputs(ins, core):
    o = {}
    o["x"] = np.ascontiguousarray(np.asarray(ins["L0_attn_src"][core]).T).astype(np.float32)
    o["idn"] = np.eye(128, dtype=bfnp)
    o["idnf"] = np.eye(128, dtype=np.float32)
    o["moe_gwT"] = np.ascontiguousarray(np.asarray(ins["L0_ffn_gate_weight"]).T)
    o["moe_bias16"] = np.ascontiguousarray(
        np.tile(np.asarray(ins["L0_ffn_e_score_correction_bias"]), 16)[None])
    o["moe_gpT"] = np.ascontiguousarray(np.swapaxes(np.asarray(ins["L0_ffn_gate_proj"]), 1, 2))
    o["moe_upT"] = np.ascontiguousarray(np.swapaxes(np.asarray(ins["L0_ffn_up_proj"]), 1, 2))
    o["moe_dnT"] = np.ascontiguousarray(np.swapaxes(np.asarray(ins["L0_ffn_down_proj"]), 1, 2))
    o["att_wq"] = np.ascontiguousarray(np.asarray(ins["L1_attn_Wq"]))
    o["att_wk"] = np.ascontiguousarray(np.asarray(ins["L1_attn_Wk"]))
    o["att_wv"] = np.ascontiguousarray(np.asarray(ins["L1_attn_Wv"]))
    o["att_wo"] = np.ascontiguousarray(np.asarray(ins["L1_attn_Wo"]))
    o["att_bo"] = np.ascontiguousarray(np.asarray(ins["L1_attn_bo"]).reshape(8, 128).T)
    rel = np.asarray(ins["L1_attn_rel_pos_bias"])
    s = np.arange(2952)
    o["att_T"] = np.ascontiguousarray(rel[:, np.clip(1023 - (s - 896), 0, 1023)])
    i_ = np.arange(128)[:, None]
    j_ = np.arange(128)[None, :]
    o["dtri"] = np.where(j_ <= i_, 3e38, -3e37).astype(np.float32)

    o["lf_wqkv"] = np.ascontiguousarray(np.asarray(ins["L1_ffn_Wqkv"]))
    o["lf_wo"] = np.ascontiguousarray(np.asarray(ins["L1_ffn_Wo"]))
    w1 = np.asarray(ins["L1_ffn_W1"])
    w1p = np.zeros((6, 1024, 5632), np.float32)
    w1p[:, :, 0:2730] = w1[:, :, 0:2730]
    w1p[:, :, 2816:2816 + 2730] = w1[:, :, 2730:5460]
    o["lf_w1"] = w1p
    w2 = np.asarray(ins["L1_ffn_W2"])
    w2p = np.zeros((6, 2816, 1024), np.float32)
    w2p[:, 0:2730] = w2
    o["lf_w2"] = w2p
    b1 = np.asarray(ins["L1_ffn_b1"])
    b1p = np.zeros((6, 5632), np.float32)
    b1p[:, 0:2730] = b1[:, 0:2730]
    b1p[:, 2816:2816 + 2730] = b1[:, 2730:5460]
    o["lf_b1"] = np.ascontiguousarray(b1p.reshape(6, 44, 128).transpose(2, 0, 1).reshape(128, 264))
    o["lf_b2"] = np.ascontiguousarray(np.asarray(ins["L1_ffn_b2"]).reshape(6, 8, 128).transpose(2, 0, 1).reshape(128, 48))
    o["lf_lng"] = np.ascontiguousarray(np.asarray(ins["L1_ffn_ln_g"]).reshape(6, 8, 128).transpose(2, 0, 1).reshape(128, 48))
    o["lf_lnb"] = np.ascontiguousarray(np.asarray(ins["L1_ffn_ln_b"]).reshape(6, 8, 128).transpose(2, 0, 1).reshape(128, 48))
    o["lf_qs"] = np.ascontiguousarray(np.tile(np.asarray(ins["L1_ffn_q_scale"]), (1, 2)).T)
    o["lf_ks"] = np.ascontiguousarray(np.tile(np.asarray(ins["L1_ffn_k_scale"]), (1, 2)).T)
    t_ = np.arange(2048, dtype=np.float64)
    halfd = np.arange(0, 64, 2, dtype=np.float64)
    inv_freq = 1.0 / 10000.0 ** (halfd / 64.0)
    freqs = np.concatenate([t_[:, None] * inv_freq[None, :]] * 2, axis=1)
    base_scale = np.concatenate([(halfd + 0.4 * 64) / (1.4 * 64)] * 2)
    power = (t_ - 1024) / 128.0
    scl = base_scale[None, :] ** power[:, None]
    cq_ = (np.cos(freqs) * scl).T
    sq_ = (np.sin(freqs) * scl).T
    ck_ = (np.cos(freqs) / scl).T
    sk_ = (np.sin(freqs) / scl).T
    o["rot_cq"] = np.ascontiguousarray(np.tile(cq_, (2, 1))).astype(bfnp)
    sqp_ = np.concatenate([sq_[32:64], sq_[0:32]], axis=0)
    o["rot_sqp"] = np.ascontiguousarray(np.tile(sqp_, (2, 1))).astype(bfnp)
    o["rot_ck"] = np.ascontiguousarray(np.tile(ck_, (2, 1))).astype(bfnp)
    skp_ = np.concatenate([sk_[32:64], sk_[0:32]], axis=0)
    o["rot_skp"] = np.ascontiguousarray(np.tile(skp_, (2, 1))).astype(bfnp)
    M = np.zeros((128, 128), np.float32)
    for b_ in (0, 64):
        for j in range(32):
            M[b_ + j, b_ + 32 + j] = -1.0
            M[b_ + 32 + j, b_ + j] = 1.0
    o["rotmat"] = np.ascontiguousarray(M.T).astype(bfnp)
    H2 = np.zeros((128, 128), np.float32)
    H2[0:64, 0:64] = 1.0
    H2[64:128, 64:128] = 1.0
    o["hones2"] = H2.astype(bfnp)
    p_ = np.arange(128)[:, None]
    jj = np.arange(384)[None, :]
    o["m384f"] = np.where((jj >= p_) & (jj <= p_ + 256), 0.0, -3e37).astype(np.float32)

    o["ma_lng"] = np.ascontiguousarray(np.asarray(ins["L0_attn_ln_g"]).reshape(8, 128).T)
    o["ma_lnb"] = np.ascontiguousarray(np.asarray(ins["L0_attn_ln_b"]).reshape(8, 128).T)
    o["ma_rms"] = np.ascontiguousarray(np.asarray(ins["L0_attn_rms_w"]).reshape(2, 4, 128).transpose(2, 0, 1).reshape(128, 8))
    cw = np.asarray(ins["L0_attn_conv_w"])[:, :, 0, :]
    o["ma_convw"] = np.ascontiguousarray(cw.reshape(2, 8, 128, 4).transpose(2, 0, 3, 1).reshape(128, 64))
    o["ma_convb"] = np.ascontiguousarray(np.asarray(ins["L0_attn_conv_b"]).reshape(2, 8, 128).transpose(2, 0, 1).reshape(128, 16))
    o["ma_dtb"] = np.ascontiguousarray(np.asarray(ins["L0_attn_dt_b"]).reshape(2, 8, 128).transpose(2, 0, 1).reshape(128, 16))
    o["ma_Alog"] = np.ascontiguousarray(np.asarray(ins["L0_attn_A_log"]).reshape(2, 8, 128, 16).transpose(2, 0, 1, 3).reshape(128, 256))
    o["ma_Dp"] = np.ascontiguousarray(np.asarray(ins["L0_attn_Dp"]).reshape(2, 8, 128).transpose(2, 0, 1).reshape(128, 16))
    o["ma_pob"] = np.ascontiguousarray(np.asarray(ins["L0_attn_po_b"]).reshape(8, 128).T)
    o["ma_inWT"] = np.ascontiguousarray(np.swapaxes(np.asarray(ins["L0_attn_in_W"]), 1, 2))
    o["ma_xpWT"] = np.ascontiguousarray(np.swapaxes(np.asarray(ins["L0_attn_xp_W"]), 1, 2))
    o["ma_dtWT"] = np.ascontiguousarray(np.swapaxes(np.asarray(ins["L0_attn_dt_W"]), 1, 2))
    o["ma_moWT"] = np.ascontiguousarray(np.swapaxes(np.asarray(ins["L0_attn_mo_W"]), 1, 2))
    o["ma_poWT"] = np.ascontiguousarray(np.asarray(ins["L0_attn_po_W"]).T)
    idx8 = np.tile(np.arange(8, dtype=np.float32), 16)
    idx4 = np.tile(np.arange(4, dtype=np.float32), 16)
    o["idx8"] = idx8[None]
    o["idx8b"] = (idx8 + 1e4)[None]
    o["idx4"] = idx4[None]
    o["idx4b"] = (idx4 + 1e4)[None]
    return o


DT_MAP = {np.dtype(np.float32): F32, np.dtype(bfnp): BF16}


def kernel(**inputs):
    in_maps = [prep_core_inputs(inputs, c) for c in range(8)]
    meta = {k: (v.shape, DT_MAP[v.dtype]) for k, v in in_maps[0].items()}
    nc = build_nc(meta, stages=STAGES_TO_RUN)
    res = run_bass_kernel_spmd(nc, in_maps, core_ids=list(range(8)))
    out = np.stack([np.ascontiguousarray(r["xout"].T) for r in res.results], axis=0)
    return out.astype(np.float32)


STAGES_TO_RUN = ["s1", "s2", "s3", "s4", "s5", "s6"]


# version 06
# speedup vs baseline: 1.3505x; 1.3505x over previous
import numpy as np
import ml_dtypes
import concourse.bass as bass
import concourse.mybir as mybir
from concourse.bass_utils import run_bass_kernel_spmd

DBG = {}
F32 = mybir.dt.float32
BF16 = mybir.dt.bfloat16
AF = mybir.ActivationFunctionType
ALU = mybir.AluOpType
AX = mybir.AxisListType
bfnp = ml_dtypes.bfloat16

L = 2048
D = 1024
CH = 512
NCH = L // CH
GUARD = 4
XW = GUARD + L
EPS = 1e-5

ENGINES = ("sync", "scalar", "tensor", "vector", "gpsimd")
SELF_SEM = {"vector": "sV", "scalar": "sS", "gpsimd": "sG"}


class Prog:
    def __init__(self, nc):
        self.nc = nc
        self.ops = []
        self.cnt = {}
        self.sems = {}
        self.bank = [[] for _ in range(8)]

    def sem(self, name):
        if name not in self.sems:
            self.sems[name] = self.nc.alloc_semaphore(name=name)
            self.cnt[name] = 0
        return self.sems[name]

    def raw(self, engine, fn):
        self.ops.append((engine, fn))

    def op(self, engine, method, *a, wait=(), inc=None, incn=1, **kw):
        waits = list(wait)
        sems = self.sems
        auto = SELF_SEM.get(engine) if method != "dma_start" else None
        if auto is not None:
            self.sem(auto)
            prev = self.cnt[auto]
            if prev > 0:
                waits.append((auto, prev))
            assert inc is None or inc == auto, \
                f"op on {engine} cannot carry extra inc {inc} (one sync update per inst)"
            inc = auto
        if inc is not None:
            self.sem(inc)
        for s, _ in waits:
            self.sem(s)
        waits = tuple(waits)

        def fn(eng, method=method, a=a, kw=kw, waits=waits, inc=inc, incn=incn):
            for s, v in waits:
                if v > 0:
                    eng.wait_ge(sems[s], v)
            inst = getattr(eng, method)(*a, **kw)
            if inc is not None:
                inst.then_inc(sems[inc], incn)
        self.raw(engine, fn)
        ret = None
        if inc is not None:
            self.cnt[inc] += incn
            ret = self.cnt[inc]
        return ret

    def wait(self, engine, *wait):
        waits = tuple(wait)
        sems = self.sems
        for s, _ in waits:
            self.sem(s)

        def fn(eng, waits=waits):
            for s, v in waits:
                if v > 0:
                    eng.wait_ge(sems[s], v)
        self.raw(engine, fn)

    def dma(self, queue, out, in_, wait=(), inc=None, **kw):
        return self.op(queue, "dma_start", out, in_, wait=wait, inc=inc, incn=16, **kw)

    def barrier(self):
        vals = {}
        for e in ENGINES:
            n = {"tensor": "sT", "sync": "bar_sync"}.get(e, SELF_SEM.get(e))
            self.sem(n)
            self.op(e, "nop", inc=n if e in ("tensor", "sync") else None)
            vals[e] = (n, self.cnt[n])
        for e in ENGINES:
            self.wait(e, *[vals[x] for x in ENGINES])
        self.bank = [[] for _ in range(8)]

    def bank_wait(self, i):
        w = self.bank[i]
        self.bank[i] = []
        return w

    def bank_done(self, i, sem, val):
        self.bank[i].append((sem, val))

    def build(self):
        with self.nc.Block() as block:
            for en in ENGINES:
                ops = [fn for (e, fn) in self.ops if e == en]

                def body(eng, ops=ops):
                    for fn in ops:
                        fn(eng)
                getattr(block, en)(body)


class Arena:
    def __init__(self, nc, nbytes):
        self.nc = nc
        self.base, _ = nc.bump_sbuf(nbytes)
        self.nbytes = nbytes
        self.off = 0
        self.marks = []
        self.n = 0

    def alloc(self, name, shape, dtype):
        esz = mybir.dt.size(dtype)
        need = int(np.prod(shape[1:])) * esz
        off = (self.off + 31) // 32 * 32
        assert off + need <= self.nbytes, f"arena overflow: {name}: {off}+{need} > {self.nbytes}"
        self.off = off + need
        self.n += 1
        return self.nc.alloc_sbuf_tensor_at(f"ar_{name}_{self.n}", list(shape), dtype,
                                            offset=self.base + off)

    def push(self):
        self.marks.append(self.off)

    def pop(self):
        self.off = self.marks.pop()


def fap(t, off, dims):
    a = t.ap()
    return bass.AP(a.tensor, a.offset + off, [list(a.ap[0])] + [list(d) for d in dims])


def fap_d(t, off, dims):
    return bass.AP(t, off, [list(d) for d in dims])


STAGE_FNS = {}


def stage(name):
    def deco(fn):
        STAGE_FNS[name] = fn
        return fn
    return deco


class Ctx:
    pass


def tok(c0, n=CH):
    return slice(GUARD + c0, GUARD + c0 + n)



def emit_ln_xn(ctx, xn, gamma_beta=None, tok0=0, ntok=L, shifts=None, src=None,
               xn_slice=None):
    P, X, PS, AR, ONES, cc = ctx.P, ctx.X, ctx.PS, ctx.AR, ctx.ONES, ctx.const_col
    if src is None:
        src = X
    if shifts is None:
        shifts = [0] * 8
    inplace = xn is None
    xoff = 0 if xn_slice is None else xn_slice

    def ssl(i, c0, n):
        return slice(GUARD + tok0 + c0 + shifts[i], GUARD + tok0 + c0 + shifts[i] + n)

    nch = (ntok + CH - 1) // CH
    AR.push()
    xb = [AR.alloc(f"lnxb{i}", [128, ntok], BF16) for i in range(8)]
    sq = [AR.alloc(f"lnsq{i}", [128, ntok], BF16) for i in range(8)]
    rstd = AR.alloc("lnrstd", [128, ntok], F32)
    arow = AR.alloc("lnarow", [128, ntok], F32)
    mu = AR.alloc("lnmu", [128, CH], F32)
    var = AR.alloc("lnvar", [128, CH], F32)

    for i in range(8):
        eng = "vector" if i % 2 == 0 else "gpsimd"
        P.op(eng, "tensor_copy", xb[i][:, :], src[i][:, ssl(i, 0, ntok)])
        P.op("scalar", "activation", sq[i][:, :], src[i][:, ssl(i, 0, ntok)], AF.Square)
    vcast = P.cnt["sV"]
    gcast = P.cnt["sG"]
    scast = P.cnt["sS"]

    prev_ps_read_v = 0
    prev_var_read_s = 0
    for c in range(nch):
        cw = min(CH, ntok - c * CH)
        cs = slice(c * CH, c * CH + cw)
        w0 = P.bank_wait(0) + ([("sV", vcast), ("sG", gcast)] if c == 0
                               else [("sV", prev_ps_read_v)])
        for i in range(8):
            P.op("tensor", "matmul", PS[0][:, 0:cw], ONES[:, :], xb[i][:, cs],
                 start=(i == 0), stop=(i == 7), wait=w0 if i == 0 else ())
        w1 = P.bank_wait(1) + ([("sS", scast)] if c == 0 else [])
        for i in range(8):
            P.op("tensor", "matmul", PS[1][:, 0:cw], ONES[:, :], sq[i][:, cs],
                 start=(i == 0), stop=(i == 7), wait=w1 if i == 0 else (),
                 inc="sT" if i == 7 else None)
        mmv = P.cnt["sT"]
        P.op("vector", "tensor_scalar", mu[:, 0:cw], PS[0][:, 0:cw], 1.0 / D, None, ALU.mult,
             wait=[("sT", mmv)])
        P.op("vector", "tensor_scalar", var[:, 0:cw], PS[1][:, 0:cw], 1.0 / D, None, ALU.mult,
             wait=[("sS", prev_var_read_s)] if prev_var_read_s else ())
        prev_ps_read_v = P.cnt["sV"]
        P.bank_done(0, "sV", prev_ps_read_v)
        P.bank_done(1, "sV", prev_ps_read_v)
        P.op("vector", "tensor_tensor", rstd[:, cs], mu[:, 0:cw], mu[:, 0:cw], ALU.mult)
        vmm = P.op("vector", "tensor_tensor", var[:, 0:cw], var[:, 0:cw], rstd[:, cs],
                   ALU.subtract)
        P.op("scalar", "activation", var[:, 0:cw], var[:, 0:cw], AF.Ln, bias=cc(EPS),
             wait=[("sV", vmm)])
        prev_var_read_s = P.cnt["sS"]
        srow = P.op("scalar", "activation", rstd[:, cs], var[:, 0:cw], AF.Exp, scale=cc(-0.5))
        varow = P.op("vector", "tensor_tensor", arow[:, cs], mu[:, 0:cw], rstd[:, cs],
                     ALU.mult, wait=[("sS", srow)])
        prev_ps_read_v = varow
        for i in range(8):
            eng = "vector" if i % 2 == 0 else "gpsimd"
            wt = [("sV", varow)] if eng == "gpsimd" else ()
            if inplace:
                P.op(eng, "tensor_tensor", src[i][:, ssl(i, c * CH, cw)],
                     src[i][:, ssl(i, c * CH, cw)], rstd[:, cs], ALU.mult, wait=wt)
                P.op(eng, "tensor_tensor", src[i][:, ssl(i, c * CH, cw)],
                     src[i][:, ssl(i, c * CH, cw)], arow[:, cs], ALU.subtract)
                continue
            xsl = slice(xoff + c * CH, xoff + c * CH + cw)
            P.op(eng, "tensor_tensor", xb[i][:, cs], src[i][:, ssl(i, c * CH, cw)],
                 rstd[:, cs], ALU.mult, wait=wt)
            if gamma_beta is None:
                P.op(eng, "tensor_tensor", xn[i][:, xsl], xb[i][:, cs], arow[:, cs],
                     ALU.subtract)
            else:
                P.op(eng, "tensor_tensor", xb[i][:, cs], xb[i][:, cs], arow[:, cs],
                     ALU.subtract)
                g_ap, b_ap = gamma_beta[i]
                P.op(eng, "tensor_scalar", xn[i][:, xsl], xb[i][:, cs], g_ap, b_ap,
                     ALU.mult, op1=ALU.add)
    AR.pop()
    return [("sV", P.cnt["sV"]), ("sG", P.cnt["sG"])]



def emit_top2_sel(P, x, sel, width, gsz, ng, m, eq, cand, IDX, IDXB):
    for p in range(2):
        P.op("vector", "tensor_reduce", m[:, 0:ng], fap(x, 0, [[gsz, ng], [1, gsz]]),
             AX.X, ALU.max)
        P.op("vector", "tensor_tensor", eq[:, 0:width], x[:, 0:width],
             fap(m, 0, [[1, ng], [0, gsz]]), ALU.is_ge)
        P.op("vector", "scalar_tensor_tensor", cand[:, 0:width], eq[:, 0:width], -1e4,
             IDXB[:, 0:width], ALU.mult, ALU.add)
        P.op("vector", "tensor_reduce", m[:, 0:ng], fap(cand, 0, [[gsz, ng], [1, gsz]]),
             AX.X, ALU.min)
        tgt = sel if p == 0 else eq
        P.op("vector", "tensor_tensor", tgt[:, 0:width], IDX[:, 0:width],
             fap(m, 0, [[1, ng], [0, gsz]]), ALU.is_equal)
        if p == 0:
            P.op("vector", "scalar_tensor_tensor", x[:, 0:width], sel[:, 0:width], -1e30,
                 x[:, 0:width], ALU.mult, ALU.add)
        else:
            P.op("vector", "tensor_tensor", sel[:, 0:width], sel[:, 0:width],
                 eq[:, 0:width], ALU.add)


def emit_moe(ctx):
    P, X, PS, AR, dram, cc = ctx.P, ctx.X, ctx.PS, ctx.AR, ctx.dram, ctx.const_col
    E = 8

    AR.push()
    xn = [AR.alloc(f"moxn{i}", [128, L], BF16) for i in range(8)]
    lnw = emit_ln_xn(ctx, xn)

    IDX8 = AR.alloc("moidx8", [128, 128], F32)
    IDX8B = AR.alloc("moidx8b", [128, 128], F32)
    IDX4 = AR.alloc("moidx4", [128, 64], F32)
    IDX4B = AR.alloc("moidx4b", [128, 64], F32)
    gw = AR.alloc("mogw", [128, 64], BF16)
    biasb = AR.alloc("mobias", [128, 128], F32)
    ltm = AR.alloc("moltm", [128, 128], F32)
    scc = AR.alloc("moscc", [128, 128], F32)
    gsb = AR.alloc("mogsb", [128, 64], F32)
    gsel = AR.alloc("mogsel", [128, 64], F32)
    smask = AR.alloc("mosmask", [128, 128], F32)
    sel = AR.alloc("mosel", [128, 128], F32)
    eqb = AR.alloc("moeq", [128, 128], F32)
    cnd = AR.alloc("mocnd", [128, 128], F32)
    mrow = AR.alloc("momrow", [128, 32], F32)
    den = AR.alloc("moden", [128, 16], F32)
    cwt = AR.alloc("mocwt", [128, 128], F32)
    cwR = AR.alloc("mocwR", [128, L], F32)
    gpb = [AR.alloc(f"mogp{b}", [128, 4096], BF16) for b in range(2)]
    upb = [AR.alloc(f"moup{b}", [128, 4096], BF16) for b in range(2)]
    dnb = [AR.alloc(f"modn{b}", [128, 4096], BF16) for b in range(2)]
    hall = [AR.alloc(f"moh{e}", [128, 4 * CH], BF16) for e in range(E)]
    cwb = [AR.alloc(f"mocwb{b}", [128, CH], F32) for b in range(2)]
    sil = AR.alloc("mosil", [128, CH], F32)

    P.dma("gpsimd", gw[:, :], fap_d(dram["moe_gwT"], 0, [[8, 128], [8 * 128, 8], [1, 8]]),
          wait=lnw, inc="qW")
    wgw = P.cnt["qW"]
    P.dma("sync", biasb[:, :], fap_d(dram["moe_bias16"], 0, [[0, 128], [1, 128]]),
          wait=lnw, inc="qD")
    wbias = P.cnt["qD"]

    for tt in range(16):
        b = tt % 2
        w = P.bank_wait(b)
        if tt == 0:
            w = w + [("qW", wgw)]
        for k in range(8):
            P.op("tensor", "matmul", PS[b][0:128, 0:8], xn[k][:, tt * 128:(tt + 1) * 128],
                 gw[:, k * 8:(k + 1) * 8], start=(k == 0), stop=(k == 7),
                 wait=w if k == 0 else (), inc="sT" if k == 7 else None)
        mmv = P.cnt["sT"]
        sv = P.op("scalar", "activation", ltm[:, tt * 8:(tt + 1) * 8], PS[b][0:128, 0:8],
                  AF.Sigmoid, wait=[("sT", mmv)])
        P.bank_done(b, "sS", sv)
    sigv = P.cnt["sS"]

    for nm, t, wdt in [("idx8", IDX8, 128), ("idx8b", IDX8B, 128), ("idx4", IDX4, 64),
                       ("idx4b", IDX4B, 64)]:
        P.dma("sync", t[:, :], fap_d(dram[nm], 0, [[0, 128], [1, wdt]]), wait=lnw, inc="qD")
    widx = P.cnt["qD"]
    P.op("vector", "tensor_tensor", scc[:, :], ltm[:, :], biasb[:, :], ALU.add,
         wait=[("sS", sigv), ("qD", widx)])
    P.op("vector", "tensor_tensor", gsb[:, :], fap(scc, 0, [[2, 64]]),
         fap(scc, 1, [[2, 64]]), ALU.add)
    emit_top2_sel(P, gsb, gsel, 64, 4, 16, mrow, eqb, cnd, IDX4, IDX4B)
    P.op("vector", "tensor_copy", smask[:, :], fap(gsel, 0, [[4, 16], [1, 4], [0, 2]]))
    P.op("vector", "tensor_tensor", scc[:, :], scc[:, :], smask[:, :], ALU.mult)
    emit_top2_sel(P, scc, sel, 128, 8, 16, mrow, eqb, cnd, IDX8, IDX8B)
    P.op("vector", "tensor_tensor", cwt[:, :], ltm[:, :], sel[:, :], ALU.mult)
    P.op("vector", "tensor_reduce", den[:, 0:16], fap(cwt, 0, [[8, 16], [1, 8]]),
         AX.X, ALU.add)
    P.op("vector", "tensor_single_scalar", den[:, 0:16], den[:, 0:16], 1e-20, ALU.add)
    P.op("vector", "reciprocal", den[:, 0:16], den[:, 0:16])
    gatev = P.op("vector", "tensor_tensor", cwt[:, :], cwt[:, :],
                 fap(den, 0, [[1, 16], [0, 8]]), ALU.mult)

    for blk in range(4):
        w = P.bank_wait(2)
        for j in range(4):
            tt = blk * 4 + j
            P.op("tensor", "transpose", PS[2][0:8, j * 128:(j + 1) * 128],
                 cwt[:, tt * 8:(tt + 1) * 8], ctx.IDNF[:, :],
                 wait=(w + [("sV", gatev)]) if j == 0 else (),
                 inc="sT" if j == 3 else None)
        trv = P.cnt["sT"]
        sv = P.op("vector", "tensor_copy", cwR[0:8, blk * 512:(blk + 1) * 512],
                  PS[2][0:8, 0:512], wait=[("sT", trv)])
        P.bank_done(2, "sV", sv)
    cwrv = P.cnt["sV"]
    cwd = ctx.scratch_cw
    P.dma("sync", cwd.ap(), cwR[0:8, :], wait=[("sV", cwrv)], inc="qD")
    wcw = P.cnt["qD"]

    def load_gu(e, b, guard):
        t = dram["moe_gpT"]
        ap = fap_d(t, e * 1024 * 512, [[512, 128], [128 * 512, 8], [1, 512]])
        P.dma("gpsimd", gpb[b][:, :], ap, wait=guard, inc="qW")
        t = dram["moe_upT"]
        ap = fap_d(t, e * 1024 * 512, [[512, 128], [128 * 512, 8], [1, 512]])
        return P.dma("gpsimd", upb[b][:, :], ap, inc="qW")

    def load_dn(e, b, guard):
        t = dram["moe_dnT"]
        ap = fap_d(t, e * 512 * 1024, [[1024, 128], [128 * 1024, 4], [1, 1024]])
        return P.dma("gpsimd", dnb[b][:, :], ap, wait=guard, inc="qW")

    gufree = {0: [], 1: []}
    dnfree = {0: [], 1: []}
    cwfree = {0: [], 1: []}
    wv_gu = {0: load_gu(0, 0, [])}
    wv_dn = {}

    for c in range(NCH):
        cs = tok(c * CH)
        hv = {}
        for e in range(E):
            pb = e % 2
            nxt = e + 1 if e + 1 < E else (0 if c + 1 < NCH else None)
            if nxt is not None:
                nb = (e + 1) % 2
                wv_gu[nxt] = load_gu(nxt, nb, gufree[nb])
                gufree[nb] = []
            cwv = P.dma("sync", cwb[pb][:, :],
                        fap_d(cwd, e * L + c * CH, [[0, 128], [1, CH]]),
                        wait=[("qD", wcw)] + cwfree[pb], inc="qD")
            cwfree[pb] = []
            for m in range(4):
                bg = 4 + (m % 2)
                bu = 6 + (m % 2)
                w = P.bank_wait(bg)
                if m == 0:
                    w = w + [("qW", wv_gu[e])]
                for k in range(8):
                    P.op("tensor", "matmul", PS[bg][:, :],
                         gpb[pb][:, k * 512 + m * 128: k * 512 + (m + 1) * 128],
                         xn[k][:, c * CH:(c + 1) * CH], start=(k == 0), stop=(k == 7),
                         wait=w if k == 0 else ())
                w = P.bank_wait(bu)
                for k in range(8):
                    P.op("tensor", "matmul", PS[bu][:, :],
                         upb[pb][:, k * 512 + m * 128: k * 512 + (m + 1) * 128],
                         xn[k][:, c * CH:(c + 1) * CH], start=(k == 0), stop=(k == 7),
                         wait=w if k == 0 else (), inc="sT" if k == 7 else None)
                mmv = P.cnt["sT"]
                if m == 3:
                    gufree[pb] = [("sT", mmv)]
                sv_ = P.op("scalar", "activation", sil[:, :], PS[bg][:, :], AF.Silu,
                           wait=[("sT", mmv)])
                P.bank_done(bg, "sS", sv_)
                hv1 = P.op("vector", "tensor_tensor", hall[e][:, m * CH:(m + 1) * CH],
                           sil[:, :], PS[bu][:, :], ALU.mult,
                           wait=[("sS", sv_), ("qD", cwv)])
                P.bank_done(bu, "sV", hv1)
                hv[e] = P.op("vector", "tensor_tensor", hall[e][:, m * CH:(m + 1) * CH],
                             hall[e][:, m * CH:(m + 1) * CH], cwb[pb][:, :], ALU.mult)
            cwfree[pb] = [("sV", hv[e])]

        wv_dn[0] = load_dn(0, 0, dnfree[0])
        dnfree[0] = []
        for e in range(E):
            db = e % 2
            if e + 1 < E:
                wv_dn[e + 1] = load_dn(e + 1, 1 - db, dnfree[1 - db])
                dnfree[1 - db] = []
            for m in range(8):
                for k in range(4):
                    w = []
                    if e == 0 and k == 0:
                        w = P.bank_wait(m) + [("sV", hv[E - 1])]
                    if k == 0:
                        w = list(w) + [("qW", wv_dn[e])]
                    P.op("tensor", "matmul", PS[m][:, :],
                         dnb[db][:, k * 1024 + m * 128: k * 1024 + (m + 1) * 128],
                         hall[e][:, k * CH:(k + 1) * CH],
                         start=(e == 0 and k == 0), stop=(e == E - 1 and k == 3),
                         wait=w, inc="sT" if (m == 7 and k == 3) else None)
            dnfree[db] = [("sT", P.cnt["sT"])]
        dn_all = P.cnt["sT"]
        for m in range(8):
            sv = P.op("vector", "scalar_tensor_tensor", X[m][:, cs], PS[m][:, :], 0.5,
                      X[m][:, cs], ALU.mult, ALU.add, wait=[("sT", dn_all)] if m == 0 else ())
            P.bank_done(m, "sV", sv)
    AR.pop()


@stage("s1")
def s1(ctx):
    emit_moe(ctx)


@stage("s3")
def s3(ctx):
    emit_moe(ctx)








def emit_ln_rows(ctx, rtot, arow, gelem=None, double=True):
    P, X, PS, AR, ONES, cc = ctx.P, ctx.X, ctx.PS, ctx.AR, ctx.ONES, ctx.const_col
    AR.push()
    xb = [AR.alloc(f"lrxb{i}", [128, L], BF16) for i in range(8)]
    sq = [AR.alloc(f"lrsq{i}", [128, L], BF16) for i in range(8)]
    mu = AR.alloc("lrmu", [128, CH], F32)
    var = AR.alloc("lrvar", [128, CH], F32)
    t1 = AR.alloc("lrt1", [128, CH], F32)

    for i in range(8):
        eng = "vector" if i % 2 == 0 else "gpsimd"
        P.op(eng, "tensor_copy", xb[i][:, :], X[i][:, tok(0, L)])
        P.op("scalar", "activation", sq[i][:, :], X[i][:, tok(0, L)], AF.Square)
    vcast, gcast, scast = P.cnt["sV"], P.cnt["sG"], P.cnt["sS"]

    prev_v = 0
    prev_s = 0
    for c in range(NCH):
        cs = slice(c * CH, (c + 1) * CH)
        w0 = P.bank_wait(0) + ([("sV", vcast), ("sG", gcast)] if c == 0 else [("sV", prev_v)])
        for i in range(8):
            P.op("tensor", "matmul", PS[0][:, :], ONES[:, :], xb[i][:, cs],
                 start=(i == 0), stop=(i == 7), wait=w0 if i == 0 else ())
        w1 = P.bank_wait(1) + ([("sS", scast)] if c == 0 else [])
        for i in range(8):
            P.op("tensor", "matmul", PS[1][:, :], ONES[:, :], sq[i][:, cs],
                 start=(i == 0), stop=(i == 7), wait=w1 if i == 0 else (),
                 inc="sT" if i == 7 else None)
        mmv = P.cnt["sT"]
        P.op("vector", "tensor_scalar", mu[:, :], PS[0][:, :], 1.0 / D, None, ALU.mult,
             wait=[("sT", mmv), ("sS", prev_s)] if prev_s else [("sT", mmv)])
        P.op("vector", "tensor_scalar", var[:, :], PS[1][:, :], 1.0 / D, None, ALU.mult)
        prev_v = P.cnt["sV"]
        P.bank_done(0, "sV", prev_v)
        P.bank_done(1, "sV", prev_v)
        P.op("vector", "tensor_tensor", t1[:, :], mu[:, :], mu[:, :], ALU.mult)
        vmm = P.op("vector", "tensor_tensor", var[:, :], var[:, :], t1[:, :], ALU.subtract)
        P.op("scalar", "activation", t1[:, :], var[:, :], AF.Ln, bias=cc(EPS),
             wait=[("sV", vmm)])
        s1 = P.op("scalar", "activation", rtot[:, cs], t1[:, :], AF.Exp, scale=cc(-0.5))
        prev_s = s1
        if double:
            P.op("vector", "tensor_tensor", t1[:, :], rtot[:, cs], rtot[:, cs], ALU.mult,
                 wait=[("sS", s1)])
            v2 = P.op("vector", "tensor_tensor", t1[:, :], t1[:, :], var[:, :], ALU.mult)
            P.op("scalar", "activation", t1[:, :], t1[:, :], AF.Ln, bias=cc(EPS),
                 wait=[("sV", v2)])
            s2_ = P.op("scalar", "activation", t1[:, :], t1[:, :], AF.Exp, scale=cc(-0.5))
            prev_s = s2_
            P.op("vector", "tensor_tensor", rtot[:, cs], rtot[:, cs], t1[:, :],
                 ALU.mult, wait=[("sS", s2_)])
            P.op("vector", "tensor_tensor", arow[:, cs], mu[:, :], rtot[:, cs], ALU.mult)
        else:
            P.op("vector", "tensor_tensor", arow[:, cs], mu[:, :], rtot[:, cs], ALU.mult,
                 wait=[("sS", s1)])
        prev_v = P.cnt["sV"]
    AR.pop()
    return [("sV", P.cnt["sV"]), ("sS", P.cnt["sS"])]


def emit_l0attn(ctx):
    P, X, PS, AR, dram, cc = ctx.P, ctx.X, ctx.PS, ctx.AR, ctx.dram, ctx.const_col
    NL, NS, DI, DC, DTR = 2, 16, 1024, 512, 32
    MC = 256
    NMC = L // MC
    scr = ctx.scratch_bc

    AR.push()
    rtot = AR.alloc("martot", [128, L], F32)
    arow = AR.alloc("maarow", [128, L], F32)
    lnw = emit_ln_rows(ctx, rtot, arow, double=True)

    lng = AR.alloc("malng", [128, 8], F32)
    lnb = AR.alloc("malnb", [128, 8], F32)
    rms = AR.alloc("marms", [128, 8], F32)
    cwc = AR.alloc("macwc", [128, 64], F32)
    cbc = AR.alloc("macbc", [128, 16], F32)
    dtb = AR.alloc("madtb", [128, 16], F32)
    Alog = AR.alloc("maAlog", [128, 256], F32)
    Dpc = AR.alloc("maDpc", [128, 16], F32)
    pob = AR.alloc("mapob", [128, 8], F32)
    H = [AR.alloc(f"maH{g}", [128, L], F32) for g in range(4)]
    hstate = AR.alloc("mahst", [128, 128], F32)
    for nm, t in [("ma_lng", lng), ("ma_lnb", lnb), ("ma_rms", rms), ("ma_convw", cwc),
                  ("ma_convb", cbc), ("ma_dtb", dtb), ("ma_Alog", Alog), ("ma_Dp", Dpc),
                  ("ma_pob", pob)]:
        P.dma("sync", t[:, :], dram[nm].ap(), wait=lnw if nm == "ma_lng" else (), inc="qD")
    wcst = P.cnt["qD"]
    av = P.op("scalar", "activation", Alog[:, :], Alog[:, :], AF.Exp, wait=[("qD", wcst)])
    av = P.op("vector", "tensor_scalar", Alog[:, :], Alog[:, :], -1.0, None, ALU.mult,
              wait=[("sS", av)])

    for g in range(4):
        wt = [("sV", av)] if g == 0 else ()
        P.op("vector", "tensor_tensor", H[g][:, :], X[g][:, tok(0, L)], rtot[:, :],
             ALU.mult, wait=wt)
        P.op("vector", "tensor_tensor", H[g][:, :], H[g][:, :], arow[:, :], ALU.subtract)
        P.op("vector", "tensor_scalar", H[g][:, :], H[g][:, :], lng[:, g:g + 1],
             lnb[:, g:g + 1], ALU.mult, op1=ALU.add)
    hiv = [("sV", P.cnt["sV"])]

    WIN = AR.alloc("maWIN", [128, 4 * 2048], BF16)
    WXP = AR.alloc("maWXP", [128, 8 * 64], BF16)
    WDT = AR.alloc("maWDT", [128, 1024], BF16)
    WMO = AR.alloc("maWMO", [128, 8 * 512], BF16)

    for li in range(DBG.get('ma_layers', NL)):
        guard = [("sT", P.cnt["sT"])] if li > 0 else list(hiv)
        P.dma("gpsimd", WIN[:, :], fap_d(dram["ma_inWT"], li * 512 * 2048,
              [[2048, 128], [128 * 2048, 4], [1, 2048]]), wait=guard, inc="qW")
        P.dma("gpsimd", WXP[:, :], fap_d(dram["ma_xpWT"], li * 1024 * 64,
              [[64, 128], [128 * 64, 8], [1, 64]]), inc="qW")
        P.dma("gpsimd", WDT[0:32, :], fap_d(dram["ma_dtWT"], li * 32 * 1024,
              [[1024, 32], [1, 1024]]), inc="qW")
        wlv = P.dma("gpsimd", WMO[:, :], fap_d(dram["ma_moWT"], li * 1024 * 512,
                    [[512, 128], [128 * 512, 8], [1, 512]]), inc="qW")
        zs = P.op("vector", "memset", hstate[:, :], 0.0,
                  wait=[("sG", P.cnt["sG"])])

        scr_free = []
        bb_free = {0: [], 1: []}
        da_free = {r: [] for r in range(3)}
        sc_free = {r: [] for r in range(3)}
        db_free = {r: [] for r in range(3)}
        tv_free = {r: [] for r in range(3)}
        for mc in range(DBG.get('ma_chunks', NMC)):
            ts0 = mc * MC
            for e in ("vector", "gpsimd", "scalar"):
                P.wait(e, ("sT", P.cnt["sT"]), ("sV", P.cnt["sV"]), ("sG", P.cnt["sG"]))
            AR.push()
            hn = [AR.alloc(f"mahn{g}", [128, MC], BF16) for g in range(4)]
            rr = AR.alloc("marr", [128, MC], F32)
            XC = [AR.alloc(f"maXC{g}", [128, 4 + MC], F32) for g in range(8)]
            XTL = AR.alloc("maXTL", [128, 4 * 8], F32)
            XS = [AR.alloc(f"maXS{g}", [128, MC], F32) for g in range(2)]
            XSF = [AR.alloc(f"maXSF{g}", [128, MC], F32) for g in range(2)]
            XSB = [AR.alloc(f"maXSB{g}", [128, MC], BF16) for g in range(8)]
            SZ = [AR.alloc(f"maSZ{g}", [128, MC], BF16) for g in range(8)]
            DT_ = [AR.alloc(f"maDT{g}", [128, MC], F32) for g in range(8)]
            UU = [AR.alloc(f"maUU{g}", [128, MC], BF16) for g in range(8)]
            YG = [AR.alloc(f"maYG{g}", [128, MC], F32) for g in range(8)]
            YB = [AR.alloc(f"maYB{g}", [128, MC], BF16) for g in range(8)]
            dtt = AR.alloc("madtt", [32, MC], BF16)
            NR = 3
            DA = [AR.alloc(f"maDA{r}", [128, MC], F32) for r in range(NR)]
            DB = [AR.alloc(f"maDB{r}", [128, MC], F32) for r in range(NR)]
            SC = [AR.alloc(f"maSC{r}", [128, MC], F32) for r in range(NR)]
            TV = [AR.alloc(f"maTV{r}", [128, MC], F32) for r in range(NR)]
            Bb = [AR.alloc(f"maBb{r}", [128, MC], F32) for r in range(2)]
            Cb = [AR.alloc(f"maCb{r}", [128, MC], F32) for r in range(2)]
            tmp = AR.alloc("matmp", [128, MC], F32)

            for g in range(4):
                P.op("scalar", "activation", hn[g][:, :], H[g][:, ts0:ts0 + MC], AF.Square)
            sqv = P.cnt["sS"]
            w = P.bank_wait(0) + [("sS", sqv)]
            for g in range(4):
                P.op("tensor", "matmul", PS[0][:, 0:MC], ctx.ONES[:, :], hn[g][:, :],
                     start=(g == 0), stop=(g == 3), wait=w if g == 0 else (),
                     inc="sT" if g == 3 else None)
            mmv = P.cnt["sT"]
            P.op("scalar", "activation", rr[:, :], PS[0][:, 0:MC], AF.Ln, bias=cc(EPS),
                 scale=cc(1.0 / DC), wait=[("sT", mmv)])
            sv = P.op("scalar", "activation", rr[:, :], rr[:, :], AF.Exp, scale=cc(-0.5))
            P.bank_done(0, "sS", sv)
            for g in range(4):
                P.op("vector", "tensor_tensor", tmp[:, :], H[g][:, ts0:ts0 + MC],
                     rr[:, :], ALU.mult, wait=[("sS", sv)] if g == 0 else ())
                P.op("vector", "tensor_scalar", hn[g][:, :], tmp[:, :],
                     rms[:, li * 4 + g:li * 4 + g + 1], None, ALU.mult)
            hnv = [("sV", P.cnt["sV"])]

            for m in range(16):
                b = m % 4
                w = P.bank_wait(b)
                if m == 0:
                    w = w + [("qW", wlv)] + hnv
                for k in range(4):
                    P.op("tensor", "matmul", PS[b][:, 0:MC],
                         WIN[:, k * 2048 + m * 128:k * 2048 + (m + 1) * 128],
                         hn[k][:, :], start=(k == 0), stop=(k == 3),
                         wait=w if k == 0 else (), inc="sT" if k == 3 else None)
                mmv = P.cnt["sT"]
                if m < 8:
                    sv = P.op("scalar", "activation", XC[m][:, 4:4 + MC], PS[b][:, 0:MC],
                              AF.Copy, wait=[("sT", mmv)])
                else:
                    sv = P.op("scalar", "activation", SZ[m - 8][:, :], PS[b][:, 0:MC],
                              AF.Silu, wait=[("sT", mmv)])
                P.bank_done(b, "sS", sv)
            xzv = P.cnt["sS"]

            if mc == 0:
                tlv = P.op("vector", "memset", XTL[:, :], 0.0)
            for g in range(8):
                eng = "vector" if g % 2 == 0 else "gpsimd"
                P.op(eng, "tensor_copy", XC[g][:, 0:4], XTL[:, g * 4:(g + 1) * 4],
                     wait=[("sS", xzv)] if g < 2 else ())
            tcv = [("sV", P.cnt["sV"]), ("sG", P.cnt["sG"])]
            for g in range(8):
                eng = "vector" if g % 2 == 0 else "gpsimd"
                P.op(eng, "tensor_copy", XTL[:, g * 4:(g + 1) * 4], XC[g][:, MC:MC + 4])
            tsv = [("sV", P.cnt["sV"]), ("sG", P.cnt["sG"])]

            xs_free = {0: [], 1: []}
            xsf_free = {0: [], 1: []}
            for g in range(8):
                xr = g % 2
                wt = (tcv + tsv if g == 0 else []) + xs_free[xr]
                P.op("vector", "tensor_scalar", XS[xr][:, :], XC[g][:, 1:1 + MC],
                     cwc[:, li * 32 + 0 * 8 + g:li * 32 + 0 * 8 + g + 1], None, ALU.mult,
                     wait=wt)
                for kk2 in range(1, 4):
                    P.op("vector", "scalar_tensor_tensor", XS[xr][:, :],
                         XC[g][:, 1 + kk2:1 + kk2 + MC],
                         cwc[:, li * 32 + kk2 * 8 + g:li * 32 + kk2 * 8 + g + 1],
                         XS[xr][:, :], ALU.mult, ALU.add)
                cvg = P.cnt["sV"]
                sx = P.op("scalar", "activation", XSF[xr][:, :], XS[xr][:, :], AF.Silu,
                          bias=cbc[:, li * 8 + g:li * 8 + g + 1],
                          wait=[("sV", cvg)] + xsf_free[xr])
                xs_free[xr] = [("sS", sx)]
                sx2 = P.op("scalar", "activation", XSB[g][:, :], XSF[xr][:, :], AF.Copy)
                ygi = P.op("vector", "tensor_scalar", YG[g][:, :], XSF[xr][:, :],
                           Dpc[:, li * 8 + g:li * 8 + g + 1], None, ALU.mult,
                           wait=[("sS", sx2)])
                xsf_free[xr] = [("sV", ygi)]
            xsv = P.cnt["sS"]
            ygv = P.cnt["sV"]

            w = P.bank_wait(4) + [("sS", xsv)]
            for k in range(8):
                P.op("tensor", "matmul", PS[4][0:64, 0:MC], WXP[:, k * 64:(k + 1) * 64],
                     XSB[k][:, :], start=(k == 0), stop=(k == 7),
                     wait=w if k == 0 else (), inc="sT" if k == 7 else None)
            mmv = P.cnt["sT"]
            dte = P.op("vector", "tensor_copy", dtt[0:32, :], PS[4][0:32, 0:MC],
                       wait=[("sT", mmv)])
            bce = P.op("vector", "tensor_copy", tmp[32:64, :], PS[4][32:64, 0:MC])
            P.bank_done(4, "sV", bce)
            bcd = P.dma("sync", fap_d(scr, 0, [[MC, 32], [1, MC]]),
                        tmp[32:64, :], wait=[("sV", bce)] + scr_free, inc="qD")
            scr_free = []

            for mg in range(8):
                b = mg % 2
                w = P.bank_wait(b)
                P.op("tensor", "matmul", PS[b][:, 0:MC],
                     WDT[0:32, mg * 128:(mg + 1) * 128], dtt[0:32, :],
                     start=True, stop=True, wait=w + ([("sV", dte)] if mg == 0 else []),
                     inc="sT")
                mmv = P.cnt["sT"]
                sv = P.op("scalar", "activation", DT_[mg][:, :], PS[b][:, 0:MC], AF.Abs,
                          bias=dtb[:, li * 8 + mg:li * 8 + mg + 1], wait=[("sT", mmv)])
                sv = P.op("scalar", "activation", tmp[:, :], DT_[mg][:, :], AF.Exp,
                          scale=cc(-1.0),
                          wait=[("sV", P.cnt["sV"])] if mg > 0 else [("qD", bcd)])
                sv = P.op("scalar", "activation", tmp[:, :], tmp[:, :], AF.Ln, bias=cc(1.0))
                rv = P.op("vector", "tensor_scalar", DT_[mg][:, :], PS[b][:, 0:MC],
                          dtb[:, li * 8 + mg:li * 8 + mg + 1], 0.0, ALU.add, op1=ALU.max,
                          wait=[("sS", sv)])
                rv = P.op("vector", "tensor_tensor", DT_[mg][:, :], DT_[mg][:, :],
                          tmp[:, :], ALU.add)
                P.bank_done(b, "sV", rv)
                P.op("gpsimd", "tensor_tensor", UU[mg][:, :], DT_[mg][:, :], XSB[mg][:, :],
                     ALU.mult, wait=[("sV", rv), ("sS", xsv)] if mg == 0 else [("sV", rv)])
            duv = [("sV", P.cnt["sV"]), ("sG", P.cnt["sG"])]


            for n in range(NS):
                r = n % 2
                bv = P.dma("sync", Bb[r][:, :], fap_d(scr, n * MC, [[0, 128], [1, MC]]),
                           wait=[("qD", bcd)] + bb_free[r], inc="qD")
                cv2 = P.dma("sync", Cb[r][:, :], fap_d(scr, (16 + n) * MC, [[0, 128], [1, MC]]),
                            inc="qD")
                if n == NS - 1:
                    scr_free = [("qD", cv2)]
                for g in range(8):
                    rg = (n * 8 + g) % 3
                    col = g * 16 + n
                    da = P.op("scalar", "activation", DA[rg][:, :], DT_[g][:, :], AF.Exp,
                              scale=Alog[:, li * 128 + col:li * 128 + col + 1],
                              wait=da_free[rg] + (duv if n == 0 and g == 0 else []))
                    da_free[rg] = []
                    db = P.op("gpsimd", "tensor_tensor", DB[rg][:, :], UU[g][:, :],
                              Bb[r][:, :], ALU.mult,
                              wait=([("qD", cv2)] if g == 0 else []) + db_free[rg])
                    db_free[rg] = []
                    scv = P.op("vector", "tensor_tensor_scan", SC[rg][:, :], DA[rg][:, :],
                               DB[rg][:, :], hstate[:, col:col + 1], ALU.mult, ALU.add,
                               wait=[("sS", da), ("sG", db)] + sc_free[rg] +
                                    ([("sV", ygv)] if n == 0 and g == 0 else []))
                    da_free[rg] = [("sV", scv)]
                    hsv = P.op("gpsimd", "tensor_copy", hstate[:, col:col + 1],
                               SC[rg][:, MC - 1:MC], wait=[("sV", scv)])
                    tv = P.op("gpsimd", "tensor_tensor", TV[rg][:, :], SC[rg][:, :],
                              Cb[r][:, :], ALU.mult, wait=tv_free[rg])
                    db_free[rg] = []
                    yv = P.op("vector", "tensor_tensor", YG[g][:, :], YG[g][:, :],
                              TV[rg][:, :], ALU.add, wait=[("sG", tv)])
                    tv_free[rg] = [("sV", yv)]
                    sc_free[rg] = [("sV", P.cnt["sV"]), ("sG", tv)]
                    if g == 7:
                        bb_free[r] = [("sG", tv), ("sV", yv)]
            scan_done = P.cnt["sV"]

            for g in range(8):
                eng = "vector" if g % 2 == 0 else "gpsimd"
                wt = [("sV", scan_done)] if g < 2 else ()
                P.op(eng, "tensor_tensor", YB[g][:, :], YG[g][:, :], SZ[g][:, :],
                     ALU.mult, wait=wt)
            ybv = [("sV", P.cnt["sV"]), ("sG", P.cnt["sG"])]

            for m in range(4):
                b = 4 + m % 2
                w = P.bank_wait(b)
                if m == 0:
                    w = w + ybv
                for k in range(8):
                    P.op("tensor", "matmul", PS[b][:, 0:MC],
                         WMO[:, k * 512 + m * 128:k * 512 + (m + 1) * 128],
                         YB[k][:, :], start=(k == 0), stop=(k == 7),
                         wait=w if k == 0 else (), inc="sT" if k == 7 else None)
                mmv = P.cnt["sT"]
                hv2 = P.op("vector", "tensor_tensor", H[m][:, ts0:ts0 + MC],
                           PS[b][:, 0:MC], H[m][:, ts0:ts0 + MC], ALU.add,
                           wait=[("sT", mmv)])
                P.bank_done(b, "sV", hv2)
            xc_tail_guard = []
            AR.pop()

    if DBG.get("ma_skip_final"):
        AR.pop()
        return
    WPO = AR.alloc("maWPO", [128, 8 * 1024], BF16)
    CAT = [AR.alloc(f"maCAT{g}", [128, L], BF16) for g in range(8)]
    stmp = AR.alloc("mastmp", [128, CH], F32)
    wpo = P.dma("gpsimd", WPO[:, :], fap_d(dram["ma_poWT"], 0,
                [[1024, 128], [128 * 1024, 8], [1, 1024]]),
                wait=[("sT", P.cnt["sT"]), ("sV", P.cnt["sV"])], inc="qW")
    hdone = P.cnt["sV"]
    for g in range(8):
        wt = [("sV", hdone)] if g == 0 else ()
        P.op("vector", "tensor_tensor", CAT[g][:, :], X[g][:, tok(0, L)], rtot[:, :],
             ALU.mult, wait=wt)
        P.op("vector", "tensor_tensor", CAT[g][:, :], CAT[g][:, :], arow[:, :], ALU.subtract)
        P.op("vector", "tensor_scalar", CAT[g][:, :], CAT[g][:, :], lng[:, g:g + 1],
             lnb[:, g:g + 1], ALU.mult, op1=ALU.add)
        if g >= 4:
            P.op("vector", "tensor_tensor", CAT[g][:, :], CAT[g][:, :], H[g - 4][:, :],
                 ALU.add)
    catv = [("sV", P.cnt["sV"])]

    for ch in range(NCH):
        cs = tok(ch * CH)
        for m in range(8):
            w = P.bank_wait(m)
            if ch == 0 and m == 0:
                w = w + [("qW", wpo)] + catv
            for k in range(8):
                P.op("tensor", "matmul", PS[m][:, :],
                     WPO[:, k * 1024 + m * 128:k * 1024 + (m + 1) * 128],
                     CAT[k][:, ch * CH:(ch + 1) * CH], start=(k == 0), stop=(k == 7),
                     wait=w if k == 0 else (), inc="sT" if k == 7 else None)
            mmv = P.cnt["sT"]
            c0 = ch * CH
            P.op("vector", "tensor_tensor", stmp[:, :], X[m][:, cs], rtot[:, c0:c0 + CH],
                 ALU.mult, wait=[("sT", mmv)])
            P.op("vector", "tensor_tensor", stmp[:, :], stmp[:, :], arow[:, c0:c0 + CH],
                 ALU.subtract)
            P.op("vector", "tensor_scalar", stmp[:, :], stmp[:, :], lng[:, m:m + 1],
                 lnb[:, m:m + 1], ALU.mult, op1=ALU.add)
            P.op("vector", "tensor_tensor", stmp[:, :], stmp[:, :], X[m][:, cs], ALU.add)
            sv = P.op("vector", "scalar_tensor_tensor", X[m][:, cs], PS[m][:, :],
                      pob[:, m:m + 1], stmp[:, :], ALU.add, ALU.add)
            P.bank_done(m, "sV", sv)
    AR.pop()


@stage("s2")
def s2(ctx):
    emit_l0attn(ctx)



def emit_l1ffn(ctx, scale=0.5):
    P, X, PS, AR, dram, cc = ctx.P, ctx.X, ctx.PS, ctx.AR, ctx.dram, ctx.const_col
    DEPTH = DBG.get("lf_depth", 6)
    SKIP_FFN = DBG.get("lf_skip_ffn", False)
    SKIP_ATTN = DBG.get("lf_skip_attn", False)
    HID = 2816
    NJB = 6
    JBW = [512, 512, 512, 512, 512, 256]
    NKS = HID // 128

    xs = ctx.scratch_x
    for i in range(8):
        P.dma("sync", fap_d(xs, i * 128 * L, [[L, 128], [1, L]]), X[i][:, tok(0, L)],
              inc="qD")
    xsv = P.cnt["qD"]
    for e in ("vector", "gpsimd", "scalar"):
        P.wait(e, ("qD", xsv))
    lnw = emit_ln_xn(ctx, None)

    AR.push()
    lng = AR.alloc("lflng", [128, 48], F32)
    lnb = AR.alloc("lflnb", [128, 48], F32)
    b1c = AR.alloc("lfb1", [128, 264], F32)
    b2c = AR.alloc("lfb2", [128, 48], F32)
    qsc = AR.alloc("lfqs", [128, 6], F32)
    ksc = AR.alloc("lfks", [128, 6], F32)
    for nm, t in [("lf_lng", lng), ("lf_lnb", lnb), ("lf_b1", b1c),
                  ("lf_b2", b2c), ("lf_qs", qsc), ("lf_ks", ksc)]:
        P.dma("sync", t[:, :], dram[nm].ap(), wait=lnw if nm == "lf_lng" else (), inc="qD")
    cstv = P.cnt["qD"]

    prev_done = [("qD", cstv)]
    for li in range(DEPTH):
        AR.push()
        ysb = [AR.alloc(f"lfys{i}", [128, L], BF16) for i in range(8)]
        cq = AR.alloc("lfcq", [128, L], BF16)
        sq_t = AR.alloc("lfsqt", [128, L], BF16)
        ck = AR.alloc("lfck", [128, L], BF16)
        sk = AR.alloc("lfsk", [128, L], BF16)
        ROT = AR.alloc("lfrot", [128, 128], BF16)
        HON = AR.alloc("lfhon", [128, 128], BF16)
        M384 = AR.alloc("lfm384", [128, 384], F32)
        cguard = list(prev_done) if li > 0 else [("sT", P.cnt["sT"])]
        for nm, t in [("rot_cq", cq), ("rot_sqp", sq_t), ("rot_ck", ck), ("rot_skp", sk),
                      ("rotmat", ROT), ("hones2", HON), ("m384f", M384)]:
            P.dma("sync", t[:, :], dram[nm].ap(),
                  wait=cguard if nm == "rot_cq" else (), inc="qD")
        cstv2 = P.cnt["qD"]
        for i in range(8):
            eng = "vector" if i % 2 == 0 else "gpsimd"
            wt = []
            if i < 2:
                wt = list(prev_done) + [("qD", cstv2)]
                if li == 0:
                    wt += lnw
            sh = GUARD - (1 if i >= 4 else 0)
            P.op(eng, "tensor_copy", ysb[i][:, :], X[i][:, sh:sh + L], wait=wt)
        ysv = [("sV", P.cnt["sV"]), ("sG", P.cnt["sG"])]

        for half in range(2 if not SKIP_ATTN else 0):
            AR.push()
            W0 = AR.alloc("lfW0", [128, 4096], BF16)
            W1_ = AR.alloc("lfW1x", [128, 4096], BF16)
            qt_ = [AR.alloc(f"lfq{m}", [128, L], BF16) for m in range(4)]
            kt_ = [AR.alloc(f"lfk{m}", [128, L], BF16) for m in range(4)]
            vt = [AR.alloc(f"lfv{t}", [128, 512], BF16) for t in range(16)]
            osb = [AR.alloc(f"lfo{m}", [128, L], BF16) for m in range(4)]
            mark = AR.off
            qq = AR.alloc("lfqq", [128, L], BF16)
            rq = AR.alloc("lfrq", [128, 512], F32)
            qq_end = AR.off
            AR.off = mark
            lgs = [AR.alloc(f"lflg{r}", [128, 384], F32) for r in range(3)]
            PTs = [AR.alloc(f"lfPT{r}", [128, 384], BF16) for r in range(3)]
            rws = [AR.alloc(f"lfrw{r}", [128, 2], F32) for r in range(3)]
            lg = lgs[0]
            AR.off = max(AR.off, qq_end)
            sqb = vt

            base = li * 1024 * 3072 + half * 512
            guard = [("sT", P.cnt["sT"])]
            wvq = P.dma("gpsimd", W0[:, :], fap_d(dram["lf_wqkv"], base,
                        [[3072, 128], [128 * 3072, 8], [1, 512]]), wait=guard, inc="qW")
            wvk = P.dma("gpsimd", W1_[:, :], fap_d(dram["lf_wqkv"], base + 1024,
                        [[3072, 128], [128 * 3072, 8], [1, 512]]), inc="qW")
            WQ, WK, WV, WO = W0, W1_, W0, W1_

            def qk_pipeline(Wbuf, wv, out_tiles, rtab_c, rtab_s, sc_col, hon_guard):
                for m in range(4):
                    qq_guard = [("sG", P.cnt["sG"])]
                    for ch in range(NCH):
                        b = (m * NCH + ch) % 4
                        w = P.bank_wait(b)
                        if m == 0 and ch == 0:
                            w = w + [("qW", wv)] + ysv
                        for k in range(8):
                            P.op("tensor", "matmul", PS[b][:, :],
                                 Wbuf[:, k * 512 + m * 128: k * 512 + (m + 1) * 128],
                                 ysb[k][:, ch * CH:(ch + 1) * CH], start=(k == 0),
                                 stop=(k == 7), wait=w if k == 0 else (),
                                 inc="sT" if k == 7 else None)
                        mmv = P.cnt["sT"]
                        sv = P.op("scalar", "activation", qq[:, ch * CH:(ch + 1) * CH],
                                  PS[b][:, :], AF.Copy,
                                  wait=[("sT", mmv)] + (qq_guard if ch == 0 else []))
                        P.bank_done(b, "sS", sv)
                    for ch in range(NCH):
                        P.op("scalar", "activation", sqb[ch][:, :],
                             qq[:, ch * CH:(ch + 1) * CH], AF.Square,
                             wait=[("sT", hon_guard[0])] if ch == 0 else ())
                    sqv = P.cnt["sS"]
                    for ch in range(NCH):
                        b2_ = 4 + (ch % 2)
                        w = P.bank_wait(b2_)
                        P.op("tensor", "matmul", PS[b2_][:, :], HON[:, :],
                             sqb[ch][:, :], start=True, stop=True,
                             wait=w + [("sS", sqv)], inc="sT")
                        hon_guard[0] = P.cnt["sT"]
                        mmv = P.cnt["sT"]
                        P.op("scalar", "activation", rq[:, :], PS[b2_][:, :], AF.Ln,
                             bias=cc(1e-30),
                             wait=[("sT", mmv), ("sV", P.cnt["sV"])])
                        sv = P.op("scalar", "activation", rq[:, :], rq[:, :], AF.Exp,
                                  scale=cc(-0.5))
                        P.bank_done(b2_, "sS", sv)
                        P.op("vector", "tensor_scalar", rq[:, :], rq[:, :], 1e12, sc_col,
                             ALU.min, op1=ALU.mult, wait=[("sS", sv)])
                        P.op("vector", "tensor_tensor", qq[:, ch * CH:(ch + 1) * CH],
                             qq[:, ch * CH:(ch + 1) * CH], rq[:, :], ALU.mult)
                        qrv = P.op("vector", "tensor_tensor",
                                   out_tiles[m][:, ch * CH:(ch + 1) * CH],
                                   qq[:, ch * CH:(ch + 1) * CH],
                                   rtab_s[:, ch * CH:(ch + 1) * CH], ALU.mult)
                        b3 = 6 + (ch % 2)
                        w = P.bank_wait(b3)
                        P.op("tensor", "matmul", PS[b3][:, :], ROT[:, :],
                             out_tiles[m][:, ch * CH:(ch + 1) * CH], start=True,
                             stop=True, wait=w + [("sV", qrv)], inc="sT")
                        mmv = P.cnt["sT"]
                        gv1 = P.op("gpsimd", "tensor_tensor", qq[:, ch * CH:(ch + 1) * CH],
                                   qq[:, ch * CH:(ch + 1) * CH],
                                   rtab_c[:, ch * CH:(ch + 1) * CH], ALU.mult,
                                   wait=[("sV", qrv)])
                        vv1 = P.op("vector", "tensor_tensor",
                                   out_tiles[m][:, ch * CH:(ch + 1) * CH],
                                   PS[b3][:, :], qq[:, ch * CH:(ch + 1) * CH], ALU.add,
                                   wait=[("sT", mmv), ("sG", gv1)])
                        P.bank_done(b3, "sV", vv1)

            hon_guard = [0]
            qk_pipeline(WQ, wvq, qt_, cq, sq_t, qsc[:, li:li + 1], hon_guard)
            wvv = P.dma("gpsimd", WV[:, :], fap_d(dram["lf_wqkv"], base + 2048,
                        [[3072, 128], [128 * 3072, 8], [1, 512]]),
                        wait=[("sT", P.cnt["sT"])], inc="qW")
            qk_pipeline(WK, wvk, kt_, ck, sk, ksc[:, li:li + 1], hon_guard)
            wvo = P.dma("gpsimd", WO[:, :], fap_d(dram["lf_wo"],
                        li * 1024 * 1024 + half * 512 * 1024,
                        [[1024, 128], [128 * 1024, 4], [1, 1024]]),
                        wait=[("sT", P.cnt["sT"])], inc="qW")

            for tt in range(16):
                b = tt % 4
                w = P.bank_wait(b)
                if tt == 0:
                    w = w + [("qW", wvv)]
                for k in range(8):
                    P.op("tensor", "matmul", PS[b][:, :],
                         ysb[k][:, tt * 128:(tt + 1) * 128],
                         WV[:, k * 512:(k + 1) * 512], start=(k == 0), stop=(k == 7),
                         wait=w if k == 0 else (), inc="sT" if k == 7 else None)
                mmv = P.cnt["sT"]
                sv = P.op("scalar", "activation", vt[tt][:, :], PS[b][:, :], AF.Copy,
                          wait=[("sT", mmv)])
                P.bank_done(b, "sS", sv)

            QKS = 8.0
            NROT = 3
            qk_done = [("sG", P.cnt["sG"]), ("sV", P.cnt["sV"]), ("sT", P.cnt["sT"])]
            lg_free = {r: list(qk_done) for r in range(NROT)}
            pt_free = {r: list(qk_done) for r in range(NROT)}
            rw_free = {r: list(qk_done) for r in range(NROT)}
            QKB = [0, 1, 5]
            insts = [(hh, qti) for hh in range(8) for qti in range(16)]
            mmq = {}

            def emit_qk(i):
                hh, qti = insts[i]
                ft = hh // 2
                pb = (hh % 2) * 64
                ctx0 = max(0, qti * 128 - 256)
                cw_ = qti * 128 + 128 - ctx0
                bq = QKB[i % 3]
                w = P.bank_wait(bq)
                P.op("tensor", "matmul", PS[bq][:, 0:cw_],
                     qt_[ft][pb:pb + 64, qti * 128:(qti + 1) * 128],
                     kt_[ft][pb:pb + 64, ctx0:ctx0 + cw_],
                     start=True, stop=True, wait=w, inc="sT")
                mmq[i] = P.cnt["sT"]

            emit_qk(0)
            emit_qk(1)
            for i, (hh, qti) in enumerate(insts):
                ft = hh // 2
                pb = (hh % 2) * 64
                ctx0 = max(0, qti * 128 - 256)
                cw_ = qti * 128 + 128 - ctx0
                nct = cw_ // 128
                r = i % NROT
                bq = QKB[i % 3]
                lgr, ptr, rwr = lgs[r], PTs[r], rws[r]
                sv = P.op("vector", "scalar_tensor_tensor", lgr[:, 0:cw_],
                          PS[bq][:, 0:cw_], QKS, M384[:, 384 - cw_:384],
                          ALU.mult, ALU.add, wait=[("sT", mmq[i])] + lg_free[r])
                lg_free[r] = []
                P.bank_done(bq, "sV", sv)
                P.op("vector", "tensor_reduce", rwr[:, 0:1], lgr[:, 0:cw_],
                     AX.X, ALU.max, wait=rw_free[r])
                rw_free[r] = []
                nm = P.op("vector", "tensor_scalar", rwr[:, 1:2], rwr[:, 0:1],
                          -1.0, None, ALU.mult)
                se = P.op("scalar", "activation", lgr[:, 0:cw_], lgr[:, 0:cw_],
                          AF.Exp, bias=rwr[:, 1:2], accum_out=rwr[:, 0:1],
                          wait=[("sV", nm)])
                P.op("vector", "reciprocal", rwr[:, 1:2], rwr[:, 0:1], wait=[("sS", se)])
                nv = P.op("vector", "tensor_scalar", lgr[:, 0:cw_], lgr[:, 0:cw_],
                          rwr[:, 1:2], None, ALU.mult)
                rw_free[r] = [("sV", nv)]
                if i + 2 < len(insts):
                    emit_qk(i + 2)
                for t2 in range(nct):
                    b3 = 2 + (t2 % 2)
                    w = P.bank_wait(b3)
                    P.op("tensor", "transpose", PS[b3][:, 0:128],
                         lgr[:, t2 * 128:(t2 + 1) * 128], ctx.IDNF[:, :],
                         wait=(w + [("sV", nv)] + pt_free[r]) if t2 < 2 else w,
                         inc="sT")
                    pt_free[r] = []
                    trv = P.cnt["sT"]
                    sv2 = P.op("vector", "tensor_copy", ptr[:, t2 * 128:(t2 + 1) * 128],
                               PS[b3][:, 0:128], wait=[("sT", trv)])
                    P.bank_done(b3, "sV", sv2)
                ptv = P.cnt["sV"]
                lg_free[r] = [("sT", P.cnt["sT"])]
                w4 = P.bank_wait(4)
                for t2 in range(nct):
                    P.op("tensor", "matmul", PS[4][pb:pb + 64, 0:128],
                         vt[ctx0 // 128 + t2][:, hh * 64:(hh + 1) * 64],
                         ptr[:, t2 * 128:(t2 + 1) * 128],
                         start=(t2 == 0), stop=(t2 == nct - 1),
                         tile_position=(0, pb),
                         wait=(w4 + [("sV", ptv)]) if t2 == 0 else (),
                         inc="sT" if t2 == nct - 1 else None)
                mmv = P.cnt["sT"]
                pt_free[r] = [("sT", mmv)]
                sv = P.op("scalar", "activation",
                          osb[ft][pb:pb + 64, qti * 128:(qti + 1) * 128],
                          PS[4][pb:pb + 64, 0:128], AF.Copy, wait=[("sT", mmv)])
                P.bank_done(4, "sS", sv)

            ov = P.cnt["sS"]
            for ch in range(NCH):
                cs = tok(ch * CH)
                for m in range(8):
                    w = P.bank_wait(m)
                    if ch == 0 and m == 0:
                        w = w + [("qW", wvo), ("sS", ov)]
                    for k in range(4):
                        P.op("tensor", "matmul", PS[m][:, :],
                             WO[:, k * 1024 + m * 128: k * 1024 + (m + 1) * 128],
                             osb[k][:, ch * CH:(ch + 1) * CH], start=(k == 0),
                             stop=(k == 3), wait=w if k == 0 else (),
                             inc="sT" if k == 3 else None)
                    mmv = P.cnt["sT"]
                    sv = P.op("vector", "tensor_tensor", X[m][:, cs], PS[m][:, :],
                              X[m][:, cs], ALU.add, wait=[("sT", mmv)])
                    P.bank_done(m, "sV", sv)
            AR.pop()
        AR.pop()
        attn_done = P.cnt["sV"]

        if SKIP_FFN:
            break
        for e in ("vector", "gpsimd", "scalar"):
            P.wait(e, ("sV", attn_done))
        gb = [(lng[:, li * 8 + i:li * 8 + i + 1], lnb[:, li * 8 + i:li * 8 + i + 1])
              for i in range(8)]
        for th in range(2):
            t0 = th * 1024
            for e in ("vector", "gpsimd", "scalar"):
                P.wait(e, ("sV", P.cnt["sV"]))
            AR.push()
            xn2 = [AR.alloc(f"lfxn{i}", [128, 1024], BF16) for i in range(8)]
            lnw2 = emit_ln_xn(ctx, xn2, gamma_beta=gb, tok0=t0, ntok=1024,
                              shifts=[0, 0, 0, 0, -1, -1, -1, -1])
            if DBG.get("lf_dump_xn2"):
                for i in range(8):
                    P.op("vector", "tensor_copy", X[i][:, tok(t0, 1024)], xn2[i][:, :],
                         wait=lnw2 if i == 0 else ())
                return
            hbuf = [AR.alloc(f"lfh{k}", [128, 1024], BF16) for k in range(NKS)]
            W1A = AR.alloc("lfW1A", [128, 4096], BF16)
            W1G = AR.alloc("lfW1G", [128, 4096], BF16)
            W2F = AR.alloc("lfW2F", [128, NKS * 512], BF16)
            gel = [AR.alloc("lfgel0", [128, 512], F32), AR.alloc("lfgel1", [128, 512], F32)]
            gel_free = {0: [], 1: []}
            gi = 0

            w1base = li * 1024 * 5632

            def load_w1(jb, guard):
                jw = JBW[jb]
                a1 = fap_d(dram["lf_w1"], w1base + jb * 512,
                           [[5632, 128], [128 * 5632, 8], [1, jw]])
                P.dma("gpsimd", W1A[:, 0:8 * jw], a1, wait=guard, inc="qW")
                a2 = fap_d(dram["lf_w1"], w1base + 2816 + jb * 512,
                           [[5632, 128], [128 * 5632, 8], [1, jw]])
                return P.dma("gpsimd", W1G[:, 0:8 * jw], a2, inc="qW")

            def load_w2(mh, guard):
                return P.dma("gpsimd", W2F[:, :],
                             fap_d(dram["lf_w2"], li * 2816 * 1024 + mh * 512,
                                   [[1024, 128], [128 * 1024, NKS], [1, 512]]),
                             wait=guard, inc="qW")

            wv1 = load_w1(0, lnw2)
            ks0 = 0
            for jb in range(NJB):
                jw = JBW[jb]
                nmt = jw // 128
                for sub in range(2):
                    ss = slice(sub * 512, (sub + 1) * 512)
                    for mt in range(nmt):
                        ba = mt % 2
                        bgt = 2 + mt % 2
                        w = P.bank_wait(ba)
                        if sub == 0 and mt == 0:
                            w = w + [("qW", wv1)]
                        for k in range(8):
                            P.op("tensor", "matmul", PS[ba][:, :],
                                 W1A[:, k * jw + mt * 128: k * jw + (mt + 1) * 128],
                                 xn2[k][:, ss], start=(k == 0), stop=(k == 7),
                                 wait=w if k == 0 else ())
                        w = P.bank_wait(bgt)
                        for k in range(8):
                            P.op("tensor", "matmul", PS[bgt][:, :],
                                 W1G[:, k * jw + mt * 128: k * jw + (mt + 1) * 128],
                                 xn2[k][:, ss], start=(k == 0), stop=(k == 7),
                                 wait=w if k == 0 else (), inc="sT" if k == 7 else None)
                        mmv = P.cnt["sT"]
                        kst = ks0 + mt
                        gr = gi % 2
                        gi += 1
                        sv = P.op("scalar", "activation", gel[gr][:, :], PS[bgt][:, :],
                                  AF.Gelu,
                                  bias=b1c[:, li * 44 + 22 + kst:li * 44 + 23 + kst],
                                  wait=[("sT", mmv)] + gel_free[gr])
                        P.bank_done(bgt, "sS", sv)
                        hv = P.op("vector", "scalar_tensor_tensor", hbuf[kst][:, ss],
                                  PS[ba][:, :], b1c[:, li * 44 + kst:li * 44 + 1 + kst],
                                  gel[gr][:, :], ALU.add, ALU.mult, wait=[("sS", sv)])
                        gel_free[gr] = [("sV", hv)]
                        P.bank_done(ba, "sV", hv)
                if jb + 1 < NJB:
                    wv1 = load_w1(jb + 1, [("sT", P.cnt["sT"])])
                ks0 += nmt
            h_all = P.cnt["sV"]

            if DBG.get("lf_dump_h"):
                for i in range(8):
                    P.op("vector", "tensor_copy", X[i][:, tok(t0, 1024)],
                         hbuf[i + DBG.get("lf_dump_h_off", 0)][:, :],
                         wait=[("sV", h_all)] if i == 0 else ())
                return
            wv2 = load_w2(0, [("sV", h_all)] if th == 0 and li == 0 else
                          [("sT", P.cnt["sT"])])
            for mh in range(2):
                for k in range(NKS):
                    for sub in range(2):
                        for m4 in range(4):
                            bb = sub * 4 + m4
                            w = []
                            if k == 0:
                                w = P.bank_wait(bb)
                                if sub == 0 and m4 == 0:
                                    w = w + [("sV", h_all), ("qW", wv2)]
                            P.op("tensor", "matmul", PS[bb][:, :],
                                 W2F[:, k * 512 + m4 * 128:k * 512 + (m4 + 1) * 128],
                                 hbuf[k][:, sub * 512:(sub + 1) * 512],
                                 start=(k == 0), stop=(k == NKS - 1),
                                 wait=w, inc="sT" if (sub == 1 and m4 == 3) else None)
                mm_all = P.cnt["sT"]
                if mh == 0:
                    wv2 = load_w2(1, [("sT", mm_all)])
                for sub in range(2):
                    for m4 in range(4):
                        bb = sub * 4 + m4
                        m = mh * 4 + m4
                        cs = tok(t0 + sub * 512)
                        sv = P.op("vector", "scalar_tensor_tensor", X[m][:, cs],
                                  PS[bb][:, :], b2c[:, li * 8 + m:li * 8 + m + 1],
                                  X[m][:, cs], ALU.add, ALU.add,
                                  wait=[("sT", mm_all)] if (sub == 0 and m4 == 0) else ())
                        P.bank_done(bb, "sV", sv)
            AR.pop()
        prev_done = [("sV", P.cnt["sV"])]

    tmpx = AR.alloc("lftmpx", [128, L], F32)
    for m in range(8):
        wv = P.dma("sync", tmpx[:, :], fap_d(xs, m * 128 * L, [[L, 128], [1, L]]),
                   wait=[("sV", P.cnt["sV"])], inc="qD")
        P.op("vector", "scalar_tensor_tensor", X[m][:, tok(0, L)], X[m][:, tok(0, L)],
             scale, tmpx[:, :], ALU.mult, ALU.add, wait=[("qD", wv)])
    AR.pop()


@stage("s4")
def s4(ctx):
    emit_l1ffn(ctx)


@stage("s6")
def s6(ctx):
    emit_l1ffn(ctx)



def emit_l1attn(ctx):
    P, X, PS, AR, dram, cc = ctx.P, ctx.X, ctx.PS, ctx.AR, ctx.dram, ctx.const_col
    AR.push()
    xn = [AR.alloc(f"laxn{i}", [128, L], BF16) for i in range(8)]
    lnw = emit_ln_xn(ctx, xn)

    bo = AR.alloc("labo", [128, 8], F32)
    dtri = AR.alloc("ladtri", [128, 128], F32)
    P.dma("sync", bo[:, :], dram["att_bo"].ap(), wait=lnw, inc="qD")
    P.dma("sync", dtri[:, :], dram["dtri"].ap(), inc="qD")
    wcst = P.cnt["qD"]

    GW = 2824
    for half in range(2):
        AR.push()
        q = [AR.alloc(f"laq{m}", [128, L], BF16) for m in range(4)]
        kk = [AR.alloc(f"lak{m}", [128, L], BF16) for m in range(4)]
        vt = [AR.alloc(f"lav{t}", [128, 512], BF16) for t in range(16)]
        WA = AR.alloc("laWA", [128, 4096], BF16)
        WB = AR.alloc("laWB", [128, 4096], BF16)
        osb = [AR.alloc(f"lao{m}", [128, L], BF16) for m in range(4)]
        G = AR.alloc("laG", [128, GW], F32)
        lg = AR.alloc("lalg", [128, L], F32)
        PT = AR.alloc("laPT", [128, L], BF16)
        rowm = AR.alloc("larow", [128, 2], F32)

        def load_whalf(name, buf, guard):
            t = dram[name]
            ap = fap_d(t, half * 512, [[1024, 128], [128 * 1024, 8], [1, 512]])
            return P.dma("gpsimd", buf[:, :], ap, wait=guard, inc="qW")

        def load_wo_half(buf, guard):
            t = dram["att_wo"]
            ap = fap_d(t, half * 512 * 1024, [[1024, 128], [128 * 1024, 4], [1, 1024]])
            return P.dma("gpsimd", buf[:, :], ap, wait=guard, inc="qW")

        hguard = lnw if half == 0 else [("sT", P.cnt["sT"])]
        wva = load_whalf("att_wq", WA, hguard)
        wvb = load_whalf("att_wk", WB, [])

        def proj(buf, wv, out_tiles, sem_after):
            for ch in range(NCH):
                for m in range(4):
                    b = (ch * 4 + m) % 4
                    w = P.bank_wait(b)
                    if ch == 0 and m == 0:
                        w = w + [("qW", wv)]
                    for k in range(8):
                        P.op("tensor", "matmul", PS[b][:, :],
                             buf[:, k * 512 + m * 128: k * 512 + (m + 1) * 128],
                             xn[k][:, ch * CH:(ch + 1) * CH], start=(k == 0), stop=(k == 7),
                             wait=w if k == 0 else (), inc="sT" if k == 7 else None)
                    mmv = P.cnt["sT"]
                    sv = P.op("scalar", "activation", out_tiles[m][:, ch * CH:(ch + 1) * CH],
                              PS[b][:, :], AF.Copy, wait=[("sT", mmv)])
                    P.bank_done(b, "sS", sv)
            return P.cnt["sT"]

        qmm = proj(WA, wva, q, None)
        wva2 = load_whalf("att_wv", WA, [("sT", qmm)])
        kmm = proj(WB, wvb, kk, None)
        wvb2 = load_wo_half(WB, [("sT", kmm)])

        for tt in range(16):
            b = tt % 4
            w = P.bank_wait(b)
            if tt == 0:
                w = w + [("qW", wva2)]
            for k in range(8):
                P.op("tensor", "matmul", PS[b][:, :],
                     xn[k][:, tt * 128:(tt + 1) * 128],
                     WA[:, k * 512:(k + 1) * 512], start=(k == 0), stop=(k == 7),
                     wait=w if k == 0 else (), inc="sT" if k == 7 else None)
            mmv = P.cnt["sT"]
            sv = P.op("scalar", "activation", vt[tt][:, :], PS[b][:, :], AF.Copy,
                      wait=[("sT", mmv)])
            P.bank_done(b, "sS", sv)

        g_guard = [("qD", wcst)] if half == 0 else [("sV", P.cnt["sV"])]
        for hh in range(8):
            h = half * 8 + hh
            gv = P.dma("sync", G[:, :], fap_d(dram["att_T"], h * 2952, [[1, 128], [1, GW]]),
                       wait=g_guard, inc="qD")
            ft = hh // 2
            pb = (hh % 2) * 64
            for qt in range(16):
                kw = (qt + 1) * 128
                nkc = (kw + 511) // 512
                for kc in range(nkc):
                    ks = min(512, kw - kc * 512)
                    b = kc % 2
                    w = P.bank_wait(b)
                    P.op("tensor", "matmul", PS[b][:, 0:ks],
                         q[ft][pb:pb + 64, qt * 128:(qt + 1) * 128],
                         kk[ft][pb:pb + 64, kc * 512:kc * 512 + ks],
                         start=True, stop=True, wait=w + [("qD", gv)] if kc == 0 else w,
                         inc="sT")
                    mmv = P.cnt["sT"]
                    base = qt * 128 - kc * 512 + 896
                    sv = P.op("vector", "tensor_tensor", lg[:, kc * 512:kc * 512 + ks],
                              PS[b][:, 0:ks], fap(G, base, [[-1, ks]]), ALU.add,
                              wait=[("sT", mmv)])
                    P.bank_done(b, "sV", sv)
                P.op("vector", "tensor_tensor", lg[:, kw - 128:kw], lg[:, kw - 128:kw],
                     dtri[:, :], ALU.min,
                     wait=[("qD", wcst)] if (hh == 0 and qt == 0 and half == 0) else ())
                P.op("vector", "tensor_reduce", rowm[:, 0:1], lg[:, 0:kw], AX.X, ALU.max)
                nm = P.op("vector", "tensor_scalar", rowm[:, 1:2], rowm[:, 0:1],
                          -1.0 / 64.0, None, ALU.mult)
                se = P.op("scalar", "activation", lg[:, 0:kw], lg[:, 0:kw], AF.Exp,
                          scale=cc(1.0 / 64.0), bias=rowm[:, 1:2], accum_out=rowm[:, 0:1],
                          wait=[("sV", nm)])
                P.op("vector", "reciprocal", rowm[:, 1:2], rowm[:, 0:1], wait=[("sS", se)])
                nv = P.op("vector", "tensor_scalar", lg[:, 0:kw], lg[:, 0:kw],
                          rowm[:, 1:2], None, ALU.mult)
                for t2 in range(qt + 1):
                    b = 2 + (t2 % 2)
                    w = P.bank_wait(b)
                    P.op("tensor", "transpose", PS[b][:, 0:128], lg[:, t2 * 128:(t2 + 1) * 128],
                         ctx.IDNF[:, :], wait=(w + [("sV", nv)]) if t2 < 2 else w,
                         inc="sT")
                    trv = P.cnt["sT"]
                    sv = P.op("vector", "tensor_copy", PT[:, t2 * 128:(t2 + 1) * 128],
                              PS[b][:, 0:128], wait=[("sT", trv)])
                    P.bank_done(b, "sV", sv)
                ptv = P.cnt["sV"]
                w4 = P.bank_wait(4)
                for t2 in range(qt + 1):
                    P.op("tensor", "matmul", PS[4][pb:pb + 64, 0:128],
                         vt[t2][:, hh * 64:(hh + 1) * 64], PT[:, t2 * 128:(t2 + 1) * 128],
                         start=(t2 == 0), stop=(t2 == qt),
                         tile_position=(0, pb),
                         wait=(w4 + [("sV", ptv)]) if t2 == 0 else (),
                         inc="sT" if t2 == qt else None)
                mmv = P.cnt["sT"]
                sv = P.op("scalar", "activation", osb[ft][pb:pb + 64, qt * 128:(qt + 1) * 128],
                          PS[4][pb:pb + 64, 0:128], AF.Copy, wait=[("sT", mmv)])
                P.bank_done(4, "sS", sv)
            g_guard = [("sV", P.cnt["sV"])]

        ov = P.cnt["sS"]
        for ch in range(NCH):
            cs = tok(ch * CH)
            for m in range(8):
                w = P.bank_wait(m)
                if ch == 0 and m == 0:
                    w = w + [("qW", wvb2), ("sS", ov)]
                for k in range(4):
                    P.op("tensor", "matmul", PS[m][:, :],
                         WB[:, k * 1024 + m * 128: k * 1024 + (m + 1) * 128],
                         osb[k][:, ch * CH:(ch + 1) * CH], start=(k == 0), stop=(k == 3),
                         wait=w if k == 0 else (), inc="sT" if k == 3 else None)
                mmv = P.cnt["sT"]
                if half == 0:
                    sv = P.op("vector", "scalar_tensor_tensor", X[m][:, cs], PS[m][:, :],
                              bo[:, m:m + 1], X[m][:, cs], ALU.add, ALU.add,
                              wait=[("sT", mmv)])
                else:
                    sv = P.op("vector", "tensor_tensor", X[m][:, cs], PS[m][:, :],
                              X[m][:, cs], ALU.add, wait=[("sT", mmv)])
                P.bank_done(m, "sV", sv)
        AR.pop()
    AR.pop()


@stage("s5")
def s5(ctx):
    emit_l1attn(ctx)




def build_nc(weights_meta, stages):
    nc = bass.Bass("TRN2", target_bir_lowering=False,
                   detect_race_conditions=not DBG.get("no_race_detect", False))
    P = Prog(nc)

    dram = {}
    for name, (shape, dt) in weights_meta.items():
        dram[name] = nc.dram_tensor(name, list(shape), dt, kind="ExternalInput")
    xout = nc.dram_tensor("xout", [D, L], F32, kind="ExternalOutput")

    X = [nc.alloc_sbuf_tensor(f"x{i}", [128, XW], F32) for i in range(8)]
    CONST = nc.alloc_sbuf_tensor("consts", [128, 16], F32)
    IDN = nc.alloc_sbuf_tensor("sb_idn", [128, 128], BF16)
    IDNF = nc.alloc_sbuf_tensor("sb_idnf", [128, 128], F32)
    ONES = nc.alloc_sbuf_tensor("ones", [128, 128], BF16)

    ctx = Ctx()
    ctx.nc, ctx.P, ctx.dram, ctx.X = nc, P, dram, X
    ctx.IDN, ctx.IDNF, ctx.ONES = IDN, IDNF, ONES
    ctx.xout = xout
    ctx.scratch_cw = nc.dram_tensor("scratch_cw", [8, L], F32, kind="Internal")
    ctx.scratch_x = nc.dram_tensor("scratch_x", [D, L], F32, kind="Internal")
    ctx.scratch_bc = nc.dram_tensor("scratch_bc", [32, 256], F32, kind="Internal")
    ctx.AR = Arena(nc, 142 * 1024)
    ctx.PS = [nc.alloc_psum_tensor(f"ps{i}", [128, 512], F32) for i in range(8)]

    CONSTS = {}

    def const_col(val):
        v = float(val)
        if v not in CONSTS:
            idx = len(CONSTS)
            assert idx < 16
            CONSTS[v] = idx
            P.op("vector", "memset", CONST[:, idx:idx + 1], v)
        return CONST[:, CONSTS[v]:CONSTS[v] + 1]
    ctx.const_col = const_col
    const_col(EPS)
    const_col(-0.5)

    for i in range(8):
        P.op("vector", "memset", X[i][:, 0:GUARD], 0.0)
    P.op("gpsimd", "memset", ONES[:, :], 1.0)
    P.dma("sync", IDN[:, :], dram["idn"].ap(), inc="qS")
    P.dma("sync", IDNF[:, :], dram["idnf"].ap(), inc="qS")
    for i in range(8):
        P.dma("sync", X[i][:, GUARD:GUARD + L], dram["x"].ap()[i * 128:(i + 1) * 128, :],
              inc="qS")
    qv = P.cnt["qS"]
    vv = P.cnt["sV"]
    gv = P.cnt["sG"]
    for e in ENGINES:
        P.wait(e, ("qS", qv), ("sV", vv), ("sG", gv))

    for st in stages:
        STAGE_FNS[st](ctx)
        P.barrier()

    for i in range(8):
        P.dma("sync", xout.ap()[i * 128:(i + 1) * 128, :], X[i][:, GUARD:GUARD + L],
              inc="qOut")
    P.wait("sync", ("qOut", P.cnt["qOut"]))

    P.build()
    return nc



def prep_core_inputs(ins, core):
    o = {}
    o["x"] = np.ascontiguousarray(np.asarray(ins["L0_attn_src"][core]).T).astype(np.float32)
    o["idn"] = np.eye(128, dtype=bfnp)
    o["idnf"] = np.eye(128, dtype=np.float32)
    o["moe_gwT"] = np.ascontiguousarray(np.asarray(ins["L0_ffn_gate_weight"]).T)
    o["moe_bias16"] = np.ascontiguousarray(
        np.tile(np.asarray(ins["L0_ffn_e_score_correction_bias"]), 16)[None])
    o["moe_gpT"] = np.ascontiguousarray(np.swapaxes(np.asarray(ins["L0_ffn_gate_proj"]), 1, 2))
    o["moe_upT"] = np.ascontiguousarray(np.swapaxes(np.asarray(ins["L0_ffn_up_proj"]), 1, 2))
    o["moe_dnT"] = np.ascontiguousarray(np.swapaxes(np.asarray(ins["L0_ffn_down_proj"]), 1, 2))
    o["att_wq"] = np.ascontiguousarray(np.asarray(ins["L1_attn_Wq"]))
    o["att_wk"] = np.ascontiguousarray(np.asarray(ins["L1_attn_Wk"]))
    o["att_wv"] = np.ascontiguousarray(np.asarray(ins["L1_attn_Wv"]))
    o["att_wo"] = np.ascontiguousarray(np.asarray(ins["L1_attn_Wo"]))
    o["att_bo"] = np.ascontiguousarray(np.asarray(ins["L1_attn_bo"]).reshape(8, 128).T)
    rel = np.asarray(ins["L1_attn_rel_pos_bias"])
    s = np.arange(2952)
    o["att_T"] = np.ascontiguousarray(rel[:, np.clip(1023 - (s - 896), 0, 1023)])
    i_ = np.arange(128)[:, None]
    j_ = np.arange(128)[None, :]
    o["dtri"] = np.where(j_ <= i_, 3e38, -3e37).astype(np.float32)

    o["lf_wqkv"] = np.ascontiguousarray(np.asarray(ins["L1_ffn_Wqkv"]))
    o["lf_wo"] = np.ascontiguousarray(np.asarray(ins["L1_ffn_Wo"]))
    w1 = np.asarray(ins["L1_ffn_W1"])
    w1p = np.zeros((6, 1024, 5632), np.float32)
    w1p[:, :, 0:2730] = w1[:, :, 0:2730]
    w1p[:, :, 2816:2816 + 2730] = w1[:, :, 2730:5460]
    o["lf_w1"] = w1p
    w2 = np.asarray(ins["L1_ffn_W2"])
    w2p = np.zeros((6, 2816, 1024), np.float32)
    w2p[:, 0:2730] = w2
    o["lf_w2"] = w2p
    b1 = np.asarray(ins["L1_ffn_b1"])
    b1p = np.zeros((6, 5632), np.float32)
    b1p[:, 0:2730] = b1[:, 0:2730]
    b1p[:, 2816:2816 + 2730] = b1[:, 2730:5460]
    o["lf_b1"] = np.ascontiguousarray(b1p.reshape(6, 44, 128).transpose(2, 0, 1).reshape(128, 264))
    o["lf_b2"] = np.ascontiguousarray(np.asarray(ins["L1_ffn_b2"]).reshape(6, 8, 128).transpose(2, 0, 1).reshape(128, 48))
    o["lf_lng"] = np.ascontiguousarray(np.asarray(ins["L1_ffn_ln_g"]).reshape(6, 8, 128).transpose(2, 0, 1).reshape(128, 48))
    o["lf_lnb"] = np.ascontiguousarray(np.asarray(ins["L1_ffn_ln_b"]).reshape(6, 8, 128).transpose(2, 0, 1).reshape(128, 48))
    o["lf_qs"] = np.ascontiguousarray(np.tile(np.asarray(ins["L1_ffn_q_scale"]), (1, 2)).T)
    o["lf_ks"] = np.ascontiguousarray(np.tile(np.asarray(ins["L1_ffn_k_scale"]), (1, 2)).T)
    t_ = np.arange(2048, dtype=np.float64)
    halfd = np.arange(0, 64, 2, dtype=np.float64)
    inv_freq = 1.0 / 10000.0 ** (halfd / 64.0)
    freqs = np.concatenate([t_[:, None] * inv_freq[None, :]] * 2, axis=1)
    base_scale = np.concatenate([(halfd + 0.4 * 64) / (1.4 * 64)] * 2)
    power = (t_ - 1024) / 128.0
    scl = base_scale[None, :] ** power[:, None]
    cq_ = (np.cos(freqs) * scl).T
    sq_ = (np.sin(freqs) * scl).T
    ck_ = (np.cos(freqs) / scl).T
    sk_ = (np.sin(freqs) / scl).T
    o["rot_cq"] = np.ascontiguousarray(np.tile(cq_, (2, 1))).astype(bfnp)
    sqp_ = np.concatenate([sq_[32:64], sq_[0:32]], axis=0)
    o["rot_sqp"] = np.ascontiguousarray(np.tile(sqp_, (2, 1))).astype(bfnp)
    o["rot_ck"] = np.ascontiguousarray(np.tile(ck_, (2, 1))).astype(bfnp)
    skp_ = np.concatenate([sk_[32:64], sk_[0:32]], axis=0)
    o["rot_skp"] = np.ascontiguousarray(np.tile(skp_, (2, 1))).astype(bfnp)
    M = np.zeros((128, 128), np.float32)
    for b_ in (0, 64):
        for j in range(32):
            M[b_ + j, b_ + 32 + j] = -1.0
            M[b_ + 32 + j, b_ + j] = 1.0
    o["rotmat"] = np.ascontiguousarray(M.T).astype(bfnp)
    H2 = np.zeros((128, 128), np.float32)
    H2[0:64, 0:64] = 1.0
    H2[64:128, 64:128] = 1.0
    o["hones2"] = H2.astype(bfnp)
    p_ = np.arange(128)[:, None]
    jj = np.arange(384)[None, :]
    o["m384f"] = np.where((jj >= p_) & (jj <= p_ + 256), 0.0, -3e37).astype(np.float32)

    o["ma_lng"] = np.ascontiguousarray(np.asarray(ins["L0_attn_ln_g"]).reshape(8, 128).T)
    o["ma_lnb"] = np.ascontiguousarray(np.asarray(ins["L0_attn_ln_b"]).reshape(8, 128).T)
    o["ma_rms"] = np.ascontiguousarray(np.asarray(ins["L0_attn_rms_w"]).reshape(2, 4, 128).transpose(2, 0, 1).reshape(128, 8))
    cw = np.asarray(ins["L0_attn_conv_w"])[:, :, 0, :]
    o["ma_convw"] = np.ascontiguousarray(cw.reshape(2, 8, 128, 4).transpose(2, 0, 3, 1).reshape(128, 64))
    o["ma_convb"] = np.ascontiguousarray(np.asarray(ins["L0_attn_conv_b"]).reshape(2, 8, 128).transpose(2, 0, 1).reshape(128, 16))
    o["ma_dtb"] = np.ascontiguousarray(np.asarray(ins["L0_attn_dt_b"]).reshape(2, 8, 128).transpose(2, 0, 1).reshape(128, 16))
    o["ma_Alog"] = np.ascontiguousarray(np.asarray(ins["L0_attn_A_log"]).reshape(2, 8, 128, 16).transpose(2, 0, 1, 3).reshape(128, 256))
    o["ma_Dp"] = np.ascontiguousarray(np.asarray(ins["L0_attn_Dp"]).reshape(2, 8, 128).transpose(2, 0, 1).reshape(128, 16))
    o["ma_pob"] = np.ascontiguousarray(np.asarray(ins["L0_attn_po_b"]).reshape(8, 128).T)
    o["ma_inWT"] = np.ascontiguousarray(np.swapaxes(np.asarray(ins["L0_attn_in_W"]), 1, 2))
    o["ma_xpWT"] = np.ascontiguousarray(np.swapaxes(np.asarray(ins["L0_attn_xp_W"]), 1, 2))
    o["ma_dtWT"] = np.ascontiguousarray(np.swapaxes(np.asarray(ins["L0_attn_dt_W"]), 1, 2))
    o["ma_moWT"] = np.ascontiguousarray(np.swapaxes(np.asarray(ins["L0_attn_mo_W"]), 1, 2))
    o["ma_poWT"] = np.ascontiguousarray(np.asarray(ins["L0_attn_po_W"]).T)
    idx8 = np.tile(np.arange(8, dtype=np.float32), 16)
    idx4 = np.tile(np.arange(4, dtype=np.float32), 16)
    o["idx8"] = idx8[None]
    o["idx8b"] = (idx8 + 1e4)[None]
    o["idx4"] = idx4[None]
    o["idx4b"] = (idx4 + 1e4)[None]
    return o


DT_MAP = {np.dtype(np.float32): F32, np.dtype(bfnp): BF16}


def kernel(**inputs):
    in_maps = [prep_core_inputs(inputs, c) for c in range(8)]
    meta = {k: (v.shape, DT_MAP[v.dtype]) for k, v in in_maps[0].items()}
    nc = build_nc(meta, stages=STAGES_TO_RUN)
    res = run_bass_kernel_spmd(nc, in_maps, core_ids=list(range(8)))
    out = np.stack([np.ascontiguousarray(r["xout"].T) for r in res.results], axis=0)
    return out.astype(np.float32)


STAGES_TO_RUN = ["s1", "s2", "s3", "s4", "s5", "s6"]


# version 07
# speedup vs baseline: 1.3933x; 1.0317x over previous
import numpy as np
import ml_dtypes
import concourse.bass as bass
import concourse.mybir as mybir
from concourse.bass_utils import run_bass_kernel_spmd

DBG = {}
F32 = mybir.dt.float32
BF16 = mybir.dt.bfloat16
AF = mybir.ActivationFunctionType
ALU = mybir.AluOpType
AX = mybir.AxisListType
bfnp = ml_dtypes.bfloat16

L = 2048
D = 1024
CH = 512
NCH = L // CH
GUARD = 4
XW = GUARD + L
EPS = 1e-5

ENGINES = ("sync", "scalar", "tensor", "vector", "gpsimd")
SELF_SEM = {"vector": "sV", "scalar": "sS", "gpsimd": "sG"}


class Prog:
    def __init__(self, nc):
        self.nc = nc
        self.ops = []
        self.cnt = {}
        self.sems = {}
        self.bank = [[] for _ in range(8)]

    def sem(self, name):
        if name not in self.sems:
            self.sems[name] = self.nc.alloc_semaphore(name=name)
            self.cnt[name] = 0
        return self.sems[name]

    def raw(self, engine, fn):
        self.ops.append((engine, fn))

    def op(self, engine, method, *a, wait=(), inc=None, incn=1, **kw):
        waits = list(wait)
        sems = self.sems
        auto = SELF_SEM.get(engine) if method != "dma_start" else None
        if auto is not None:
            self.sem(auto)
            prev = self.cnt[auto]
            if prev > 0:
                waits.append((auto, prev))
            assert inc is None or inc == auto, \
                f"op on {engine} cannot carry extra inc {inc} (one sync update per inst)"
            inc = auto
        if inc is not None:
            self.sem(inc)
        for s, _ in waits:
            self.sem(s)
        waits = tuple(waits)

        def fn(eng, method=method, a=a, kw=kw, waits=waits, inc=inc, incn=incn):
            for s, v in waits:
                if v > 0:
                    eng.wait_ge(sems[s], v)
            inst = getattr(eng, method)(*a, **kw)
            if inc is not None:
                inst.then_inc(sems[inc], incn)
        self.raw(engine, fn)
        ret = None
        if inc is not None:
            self.cnt[inc] += incn
            ret = self.cnt[inc]
        return ret

    def wait(self, engine, *wait):
        waits = tuple(wait)
        sems = self.sems
        for s, _ in waits:
            self.sem(s)

        def fn(eng, waits=waits):
            for s, v in waits:
                if v > 0:
                    eng.wait_ge(sems[s], v)
        self.raw(engine, fn)

    def dma(self, queue, out, in_, wait=(), inc=None, **kw):
        return self.op(queue, "dma_start", out, in_, wait=wait, inc=inc, incn=16, **kw)

    def barrier(self):
        vals = {}
        for e in ENGINES:
            n = {"tensor": "sT", "sync": "bar_sync"}.get(e, SELF_SEM.get(e))
            self.sem(n)
            self.op(e, "nop", inc=n if e in ("tensor", "sync") else None)
            vals[e] = (n, self.cnt[n])
        for e in ENGINES:
            self.wait(e, *[vals[x] for x in ENGINES])
        self.bank = [[] for _ in range(8)]

    def bank_wait(self, i):
        w = self.bank[i]
        self.bank[i] = []
        return w

    def bank_done(self, i, sem, val):
        self.bank[i].append((sem, val))

    def build(self):
        with self.nc.Block() as block:
            for en in ENGINES:
                ops = [fn for (e, fn) in self.ops if e == en]

                def body(eng, ops=ops):
                    for fn in ops:
                        fn(eng)
                getattr(block, en)(body)


class Arena:
    def __init__(self, nc, nbytes):
        self.nc = nc
        self.base, _ = nc.bump_sbuf(nbytes)
        self.nbytes = nbytes
        self.off = 0
        self.marks = []
        self.n = 0

    def alloc(self, name, shape, dtype):
        esz = mybir.dt.size(dtype)
        need = int(np.prod(shape[1:])) * esz
        off = (self.off + 31) // 32 * 32
        assert off + need <= self.nbytes, f"arena overflow: {name}: {off}+{need} > {self.nbytes}"
        self.off = off + need
        self.n += 1
        return self.nc.alloc_sbuf_tensor_at(f"ar_{name}_{self.n}", list(shape), dtype,
                                            offset=self.base + off)

    def push(self):
        self.marks.append(self.off)

    def pop(self):
        self.off = self.marks.pop()


def fap(t, off, dims):
    a = t.ap()
    return bass.AP(a.tensor, a.offset + off, [list(a.ap[0])] + [list(d) for d in dims])


def fap_d(t, off, dims):
    return bass.AP(t, off, [list(d) for d in dims])


STAGE_FNS = {}


def stage(name):
    def deco(fn):
        STAGE_FNS[name] = fn
        return fn
    return deco


class Ctx:
    pass


def tok(c0, n=CH):
    return slice(GUARD + c0, GUARD + c0 + n)



def emit_ln_xn(ctx, xn, gamma_beta=None, tok0=0, ntok=L, shifts=None, src=None,
               xn_slice=None):
    P, X, PS, AR, ONES, cc = ctx.P, ctx.X, ctx.PS, ctx.AR, ctx.ONES, ctx.const_col
    if src is None:
        src = X
    if shifts is None:
        shifts = [0] * 8
    inplace = xn is None
    xoff = 0 if xn_slice is None else xn_slice

    def ssl(i, c0, n):
        return slice(GUARD + tok0 + c0 + shifts[i], GUARD + tok0 + c0 + shifts[i] + n)

    nch = (ntok + CH - 1) // CH
    AR.push()
    xb = [AR.alloc(f"lnxb{i}", [128, ntok], BF16) for i in range(8)]
    sq = [AR.alloc(f"lnsq{i}", [128, ntok], BF16) for i in range(8)]
    rstd = AR.alloc("lnrstd", [128, ntok], F32)
    arow = AR.alloc("lnarow", [128, ntok], F32)
    mu = AR.alloc("lnmu", [128, CH], F32)
    var = AR.alloc("lnvar", [128, CH], F32)

    for i in range(8):
        eng = "vector" if i % 2 == 0 else "gpsimd"
        P.op(eng, "tensor_copy", xb[i][:, :], src[i][:, ssl(i, 0, ntok)])
        P.op("scalar", "activation", sq[i][:, :], src[i][:, ssl(i, 0, ntok)], AF.Square)
    vcast = P.cnt["sV"]
    gcast = P.cnt["sG"]
    scast = P.cnt["sS"]

    prev_ps_read_v = 0
    prev_var_read_s = 0
    for c in range(nch):
        cw = min(CH, ntok - c * CH)
        cs = slice(c * CH, c * CH + cw)
        w0 = P.bank_wait(0) + ([("sV", vcast), ("sG", gcast)] if c == 0
                               else [("sV", prev_ps_read_v)])
        for i in range(8):
            P.op("tensor", "matmul", PS[0][:, 0:cw], ONES[:, :], xb[i][:, cs],
                 start=(i == 0), stop=(i == 7), wait=w0 if i == 0 else ())
        w1 = P.bank_wait(1) + ([("sS", scast)] if c == 0 else [])
        for i in range(8):
            P.op("tensor", "matmul", PS[1][:, 0:cw], ONES[:, :], sq[i][:, cs],
                 start=(i == 0), stop=(i == 7), wait=w1 if i == 0 else (),
                 inc="sT" if i == 7 else None)
        mmv = P.cnt["sT"]
        P.op("vector", "tensor_scalar", mu[:, 0:cw], PS[0][:, 0:cw], 1.0 / D, None, ALU.mult,
             wait=[("sT", mmv)])
        P.op("vector", "tensor_scalar", var[:, 0:cw], PS[1][:, 0:cw], 1.0 / D, None, ALU.mult,
             wait=[("sS", prev_var_read_s)] if prev_var_read_s else ())
        prev_ps_read_v = P.cnt["sV"]
        P.bank_done(0, "sV", prev_ps_read_v)
        P.bank_done(1, "sV", prev_ps_read_v)
        P.op("vector", "tensor_tensor", rstd[:, cs], mu[:, 0:cw], mu[:, 0:cw], ALU.mult)
        vmm = P.op("vector", "tensor_tensor", var[:, 0:cw], var[:, 0:cw], rstd[:, cs],
                   ALU.subtract)
        P.op("scalar", "activation", var[:, 0:cw], var[:, 0:cw], AF.Ln, bias=cc(EPS),
             wait=[("sV", vmm)])
        prev_var_read_s = P.cnt["sS"]
        srow = P.op("scalar", "activation", rstd[:, cs], var[:, 0:cw], AF.Exp, scale=cc(-0.5))
        varow = P.op("vector", "tensor_tensor", arow[:, cs], mu[:, 0:cw], rstd[:, cs],
                     ALU.mult, wait=[("sS", srow)])
        prev_ps_read_v = varow
        for i in range(8):
            eng = "vector" if i % 2 == 0 else "gpsimd"
            wt = [("sV", varow)] if eng == "gpsimd" else ()
            if inplace:
                P.op(eng, "tensor_tensor", src[i][:, ssl(i, c * CH, cw)],
                     src[i][:, ssl(i, c * CH, cw)], rstd[:, cs], ALU.mult, wait=wt)
                P.op(eng, "tensor_tensor", src[i][:, ssl(i, c * CH, cw)],
                     src[i][:, ssl(i, c * CH, cw)], arow[:, cs], ALU.subtract)
                continue
            xsl = slice(xoff + c * CH, xoff + c * CH + cw)
            P.op(eng, "tensor_tensor", xb[i][:, cs], src[i][:, ssl(i, c * CH, cw)],
                 rstd[:, cs], ALU.mult, wait=wt)
            if gamma_beta is None:
                P.op(eng, "tensor_tensor", xn[i][:, xsl], xb[i][:, cs], arow[:, cs],
                     ALU.subtract)
            else:
                P.op(eng, "tensor_tensor", xb[i][:, cs], xb[i][:, cs], arow[:, cs],
                     ALU.subtract)
                g_ap, b_ap = gamma_beta[i]
                P.op(eng, "tensor_scalar", xn[i][:, xsl], xb[i][:, cs], g_ap, b_ap,
                     ALU.mult, op1=ALU.add)
    AR.pop()
    return [("sV", P.cnt["sV"]), ("sG", P.cnt["sG"])]



def emit_top2_sel(P, x, sel, width, gsz, ng, m, eq, cand, IDX, IDXB):
    for p in range(2):
        P.op("vector", "tensor_reduce", m[:, 0:ng], fap(x, 0, [[gsz, ng], [1, gsz]]),
             AX.X, ALU.max)
        P.op("vector", "tensor_tensor", eq[:, 0:width], x[:, 0:width],
             fap(m, 0, [[1, ng], [0, gsz]]), ALU.is_ge)
        P.op("vector", "scalar_tensor_tensor", cand[:, 0:width], eq[:, 0:width], -1e4,
             IDXB[:, 0:width], ALU.mult, ALU.add)
        P.op("vector", "tensor_reduce", m[:, 0:ng], fap(cand, 0, [[gsz, ng], [1, gsz]]),
             AX.X, ALU.min)
        tgt = sel if p == 0 else eq
        P.op("vector", "tensor_tensor", tgt[:, 0:width], IDX[:, 0:width],
             fap(m, 0, [[1, ng], [0, gsz]]), ALU.is_equal)
        if p == 0:
            P.op("vector", "scalar_tensor_tensor", x[:, 0:width], sel[:, 0:width], -1e30,
                 x[:, 0:width], ALU.mult, ALU.add)
        else:
            P.op("vector", "tensor_tensor", sel[:, 0:width], sel[:, 0:width],
                 eq[:, 0:width], ALU.add)


def emit_moe(ctx):
    P, X, PS, AR, dram, cc = ctx.P, ctx.X, ctx.PS, ctx.AR, ctx.dram, ctx.const_col
    E = 8

    AR.push()
    xn = [AR.alloc(f"moxn{i}", [128, L], BF16) for i in range(8)]
    lnw = emit_ln_xn(ctx, xn)

    IDX8 = AR.alloc("moidx8", [128, 128], F32)
    IDX8B = AR.alloc("moidx8b", [128, 128], F32)
    IDX4 = AR.alloc("moidx4", [128, 64], F32)
    IDX4B = AR.alloc("moidx4b", [128, 64], F32)
    gw = AR.alloc("mogw", [128, 64], BF16)
    biasb = AR.alloc("mobias", [128, 128], F32)
    ltm = AR.alloc("moltm", [128, 128], F32)
    scc = AR.alloc("moscc", [128, 128], F32)
    gsb = AR.alloc("mogsb", [128, 64], F32)
    gsel = AR.alloc("mogsel", [128, 64], F32)
    smask = AR.alloc("mosmask", [128, 128], F32)
    sel = AR.alloc("mosel", [128, 128], F32)
    eqb = AR.alloc("moeq", [128, 128], F32)
    cnd = AR.alloc("mocnd", [128, 128], F32)
    mrow = AR.alloc("momrow", [128, 32], F32)
    den = AR.alloc("moden", [128, 16], F32)
    cwt = AR.alloc("mocwt", [128, 128], F32)
    cwR = AR.alloc("mocwR", [128, L], F32)
    gpb = [AR.alloc(f"mogp{b}", [128, 4096], BF16) for b in range(2)]
    upb = [AR.alloc(f"moup{b}", [128, 4096], BF16) for b in range(2)]
    dnb = [AR.alloc(f"modn{b}", [128, 4096], BF16) for b in range(2)]
    hall = [AR.alloc(f"moh{e}", [128, 4 * CH], BF16) for e in range(E)]
    cwb = [AR.alloc(f"mocwb{b}", [128, CH], F32) for b in range(2)]
    sil = AR.alloc("mosil", [128, CH], F32)

    P.dma("gpsimd", gw[:, :], fap_d(dram["moe_gwT"], 0, [[8, 128], [8 * 128, 8], [1, 8]]),
          wait=lnw, inc="qW")
    wgw = P.cnt["qW"]
    P.dma("sync", biasb[:, :], fap_d(dram["moe_bias16"], 0, [[0, 128], [1, 128]]),
          wait=lnw, inc="qD")
    wbias = P.cnt["qD"]

    for tt in range(16):
        b = tt % 2
        w = P.bank_wait(b)
        if tt == 0:
            w = w + [("qW", wgw)]
        for k in range(8):
            P.op("tensor", "matmul", PS[b][0:128, 0:8], xn[k][:, tt * 128:(tt + 1) * 128],
                 gw[:, k * 8:(k + 1) * 8], start=(k == 0), stop=(k == 7),
                 wait=w if k == 0 else (), inc="sT" if k == 7 else None)
        mmv = P.cnt["sT"]
        sv = P.op("scalar", "activation", ltm[:, tt * 8:(tt + 1) * 8], PS[b][0:128, 0:8],
                  AF.Sigmoid, wait=[("sT", mmv)])
        P.bank_done(b, "sS", sv)
    sigv = P.cnt["sS"]

    for nm, t, wdt in [("idx8", IDX8, 128), ("idx8b", IDX8B, 128), ("idx4", IDX4, 64),
                       ("idx4b", IDX4B, 64)]:
        P.dma("sync", t[:, :], fap_d(dram[nm], 0, [[0, 128], [1, wdt]]), wait=lnw, inc="qD")
    widx = P.cnt["qD"]
    P.op("vector", "tensor_tensor", scc[:, :], ltm[:, :], biasb[:, :], ALU.add,
         wait=[("sS", sigv), ("qD", widx)])
    P.op("vector", "tensor_tensor", gsb[:, :], fap(scc, 0, [[2, 64]]),
         fap(scc, 1, [[2, 64]]), ALU.add)
    emit_top2_sel(P, gsb, gsel, 64, 4, 16, mrow, eqb, cnd, IDX4, IDX4B)
    P.op("vector", "tensor_copy", smask[:, :], fap(gsel, 0, [[4, 16], [1, 4], [0, 2]]))
    P.op("vector", "tensor_tensor", scc[:, :], scc[:, :], smask[:, :], ALU.mult)
    emit_top2_sel(P, scc, sel, 128, 8, 16, mrow, eqb, cnd, IDX8, IDX8B)
    P.op("vector", "tensor_tensor", cwt[:, :], ltm[:, :], sel[:, :], ALU.mult)
    P.op("vector", "tensor_reduce", den[:, 0:16], fap(cwt, 0, [[8, 16], [1, 8]]),
         AX.X, ALU.add)
    P.op("vector", "tensor_single_scalar", den[:, 0:16], den[:, 0:16], 1e-20, ALU.add)
    P.op("vector", "reciprocal", den[:, 0:16], den[:, 0:16])
    gatev = P.op("vector", "tensor_tensor", cwt[:, :], cwt[:, :],
                 fap(den, 0, [[1, 16], [0, 8]]), ALU.mult)

    for blk in range(4):
        w = P.bank_wait(2)
        for j in range(4):
            tt = blk * 4 + j
            P.op("tensor", "transpose", PS[2][0:8, j * 128:(j + 1) * 128],
                 cwt[:, tt * 8:(tt + 1) * 8], ctx.IDNF[:, :],
                 wait=(w + [("sV", gatev)]) if j == 0 else (),
                 inc="sT" if j == 3 else None)
        trv = P.cnt["sT"]
        sv = P.op("vector", "tensor_copy", cwR[0:8, blk * 512:(blk + 1) * 512],
                  PS[2][0:8, 0:512], wait=[("sT", trv)])
        P.bank_done(2, "sV", sv)
    cwrv = P.cnt["sV"]
    cwd = ctx.scratch_cw
    P.dma("sync", cwd.ap(), cwR[0:8, :], wait=[("sV", cwrv)], inc="qD")
    wcw = P.cnt["qD"]

    def load_gu(e, b, guard):
        t = dram["moe_gpT"]
        ap = fap_d(t, e * 1024 * 512, [[512, 128], [128 * 512, 8], [1, 512]])
        P.dma("gpsimd", gpb[b][:, :], ap, wait=guard, inc="qW")
        t = dram["moe_upT"]
        ap = fap_d(t, e * 1024 * 512, [[512, 128], [128 * 512, 8], [1, 512]])
        return P.dma("gpsimd", upb[b][:, :], ap, inc="qW")

    def load_dn(e, b, guard):
        t = dram["moe_dnT"]
        ap = fap_d(t, e * 512 * 1024, [[1024, 128], [128 * 1024, 4], [1, 1024]])
        return P.dma("gpsimd", dnb[b][:, :], ap, wait=guard, inc="qW")

    gufree = {0: [], 1: []}
    dnfree = {0: [], 1: []}
    cwfree = {0: [], 1: []}
    wv_gu = {0: load_gu(0, 0, [])}
    wv_dn = {}

    for c in range(NCH):
        cs = tok(c * CH)
        hv = {}
        for e in range(E):
            pb = e % 2
            nxt = e + 1 if e + 1 < E else (0 if c + 1 < NCH else None)
            if nxt is not None:
                nb = (e + 1) % 2
                wv_gu[nxt] = load_gu(nxt, nb, gufree[nb])
                gufree[nb] = []
            cwv = P.dma("sync", cwb[pb][:, :],
                        fap_d(cwd, e * L + c * CH, [[0, 128], [1, CH]]),
                        wait=[("qD", wcw)] + cwfree[pb], inc="qD")
            cwfree[pb] = []
            for m in range(4):
                bg = 4 + (m % 2)
                bu = 6 + (m % 2)
                w = P.bank_wait(bg)
                if m == 0:
                    w = w + [("qW", wv_gu[e])]
                for k in range(8):
                    P.op("tensor", "matmul", PS[bg][:, :],
                         gpb[pb][:, k * 512 + m * 128: k * 512 + (m + 1) * 128],
                         xn[k][:, c * CH:(c + 1) * CH], start=(k == 0), stop=(k == 7),
                         wait=w if k == 0 else ())
                w = P.bank_wait(bu)
                for k in range(8):
                    P.op("tensor", "matmul", PS[bu][:, :],
                         upb[pb][:, k * 512 + m * 128: k * 512 + (m + 1) * 128],
                         xn[k][:, c * CH:(c + 1) * CH], start=(k == 0), stop=(k == 7),
                         wait=w if k == 0 else (), inc="sT" if k == 7 else None)
                mmv = P.cnt["sT"]
                if m == 3:
                    gufree[pb] = [("sT", mmv)]
                sv_ = P.op("scalar", "activation", sil[:, :], PS[bg][:, :], AF.Silu,
                           wait=[("sT", mmv)])
                P.bank_done(bg, "sS", sv_)
                hv1 = P.op("vector", "tensor_tensor", hall[e][:, m * CH:(m + 1) * CH],
                           sil[:, :], PS[bu][:, :], ALU.mult,
                           wait=[("sS", sv_), ("qD", cwv)])
                P.bank_done(bu, "sV", hv1)
                hv[e] = P.op("vector", "tensor_tensor", hall[e][:, m * CH:(m + 1) * CH],
                             hall[e][:, m * CH:(m + 1) * CH], cwb[pb][:, :], ALU.mult)
            cwfree[pb] = [("sV", hv[e])]

        wv_dn[0] = load_dn(0, 0, dnfree[0])
        dnfree[0] = []
        for e in range(E):
            db = e % 2
            if e + 1 < E:
                wv_dn[e + 1] = load_dn(e + 1, 1 - db, dnfree[1 - db])
                dnfree[1 - db] = []
            for m in range(8):
                for k in range(4):
                    w = []
                    if e == 0 and k == 0:
                        w = P.bank_wait(m) + [("sV", hv[E - 1])]
                    if k == 0:
                        w = list(w) + [("qW", wv_dn[e])]
                    P.op("tensor", "matmul", PS[m][:, :],
                         dnb[db][:, k * 1024 + m * 128: k * 1024 + (m + 1) * 128],
                         hall[e][:, k * CH:(k + 1) * CH],
                         start=(e == 0 and k == 0), stop=(e == E - 1 and k == 3),
                         wait=w, inc="sT" if (m == 7 and k == 3) else None)
            dnfree[db] = [("sT", P.cnt["sT"])]
        dn_all = P.cnt["sT"]
        for m in range(8):
            sv = P.op("vector", "scalar_tensor_tensor", X[m][:, cs], PS[m][:, :], 0.5,
                      X[m][:, cs], ALU.mult, ALU.add, wait=[("sT", dn_all)] if m == 0 else ())
            P.bank_done(m, "sV", sv)
    AR.pop()


@stage("s1")
def s1(ctx):
    emit_moe(ctx)


@stage("s3")
def s3(ctx):
    emit_moe(ctx)








def emit_ln_rows(ctx, rtot, arow, gelem=None, double=True):
    P, X, PS, AR, ONES, cc = ctx.P, ctx.X, ctx.PS, ctx.AR, ctx.ONES, ctx.const_col
    AR.push()
    xb = [AR.alloc(f"lrxb{i}", [128, L], BF16) for i in range(8)]
    sq = [AR.alloc(f"lrsq{i}", [128, L], BF16) for i in range(8)]
    mu = AR.alloc("lrmu", [128, CH], F32)
    var = AR.alloc("lrvar", [128, CH], F32)
    t1 = AR.alloc("lrt1", [128, CH], F32)

    for i in range(8):
        eng = "vector" if i % 2 == 0 else "gpsimd"
        P.op(eng, "tensor_copy", xb[i][:, :], X[i][:, tok(0, L)])
        P.op("scalar", "activation", sq[i][:, :], X[i][:, tok(0, L)], AF.Square)
    vcast, gcast, scast = P.cnt["sV"], P.cnt["sG"], P.cnt["sS"]

    prev_v = 0
    prev_s = 0
    for c in range(NCH):
        cs = slice(c * CH, (c + 1) * CH)
        w0 = P.bank_wait(0) + ([("sV", vcast), ("sG", gcast)] if c == 0 else [("sV", prev_v)])
        for i in range(8):
            P.op("tensor", "matmul", PS[0][:, :], ONES[:, :], xb[i][:, cs],
                 start=(i == 0), stop=(i == 7), wait=w0 if i == 0 else ())
        w1 = P.bank_wait(1) + ([("sS", scast)] if c == 0 else [])
        for i in range(8):
            P.op("tensor", "matmul", PS[1][:, :], ONES[:, :], sq[i][:, cs],
                 start=(i == 0), stop=(i == 7), wait=w1 if i == 0 else (),
                 inc="sT" if i == 7 else None)
        mmv = P.cnt["sT"]
        P.op("vector", "tensor_scalar", mu[:, :], PS[0][:, :], 1.0 / D, None, ALU.mult,
             wait=[("sT", mmv), ("sS", prev_s)] if prev_s else [("sT", mmv)])
        P.op("vector", "tensor_scalar", var[:, :], PS[1][:, :], 1.0 / D, None, ALU.mult)
        prev_v = P.cnt["sV"]
        P.bank_done(0, "sV", prev_v)
        P.bank_done(1, "sV", prev_v)
        P.op("vector", "tensor_tensor", t1[:, :], mu[:, :], mu[:, :], ALU.mult)
        vmm = P.op("vector", "tensor_tensor", var[:, :], var[:, :], t1[:, :], ALU.subtract)
        P.op("scalar", "activation", t1[:, :], var[:, :], AF.Ln, bias=cc(EPS),
             wait=[("sV", vmm)])
        s1 = P.op("scalar", "activation", rtot[:, cs], t1[:, :], AF.Exp, scale=cc(-0.5))
        prev_s = s1
        if double:
            P.op("vector", "tensor_tensor", t1[:, :], rtot[:, cs], rtot[:, cs], ALU.mult,
                 wait=[("sS", s1)])
            v2 = P.op("vector", "tensor_tensor", t1[:, :], t1[:, :], var[:, :], ALU.mult)
            P.op("scalar", "activation", t1[:, :], t1[:, :], AF.Ln, bias=cc(EPS),
                 wait=[("sV", v2)])
            s2_ = P.op("scalar", "activation", t1[:, :], t1[:, :], AF.Exp, scale=cc(-0.5))
            prev_s = s2_
            P.op("vector", "tensor_tensor", rtot[:, cs], rtot[:, cs], t1[:, :],
                 ALU.mult, wait=[("sS", s2_)])
            P.op("vector", "tensor_tensor", arow[:, cs], mu[:, :], rtot[:, cs], ALU.mult)
        else:
            P.op("vector", "tensor_tensor", arow[:, cs], mu[:, :], rtot[:, cs], ALU.mult,
                 wait=[("sS", s1)])
        prev_v = P.cnt["sV"]
    AR.pop()
    return [("sV", P.cnt["sV"]), ("sS", P.cnt["sS"])]


def emit_l0attn(ctx):
    P, X, PS, AR, dram, cc = ctx.P, ctx.X, ctx.PS, ctx.AR, ctx.dram, ctx.const_col
    NL, NS, DI, DC, DTR = 2, 16, 1024, 512, 32
    MC = 256
    NMC = L // MC
    scr = ctx.scratch_bc

    AR.push()
    rtot = AR.alloc("martot", [128, L], F32)
    arow = AR.alloc("maarow", [128, L], F32)
    lnw = emit_ln_rows(ctx, rtot, arow, double=True)

    lng = AR.alloc("malng", [128, 8], F32)
    lnb = AR.alloc("malnb", [128, 8], F32)
    rms = AR.alloc("marms", [128, 8], F32)
    cwc = AR.alloc("macwc", [128, 64], F32)
    cbc = AR.alloc("macbc", [128, 16], F32)
    dtb = AR.alloc("madtb", [128, 16], F32)
    Alog = AR.alloc("maAlog", [128, 256], F32)
    Dpc = AR.alloc("maDpc", [128, 16], F32)
    pob = AR.alloc("mapob", [128, 8], F32)
    H = [AR.alloc(f"maH{g}", [128, L], F32) for g in range(4)]
    hstate = AR.alloc("mahst", [128, 128], F32)
    for nm, t in [("ma_lng", lng), ("ma_lnb", lnb), ("ma_rms", rms), ("ma_convw", cwc),
                  ("ma_convb", cbc), ("ma_dtb", dtb), ("ma_Alog", Alog), ("ma_Dp", Dpc),
                  ("ma_pob", pob)]:
        P.dma("sync", t[:, :], dram[nm].ap(), wait=lnw if nm == "ma_lng" else (), inc="qD")
    wcst = P.cnt["qD"]
    av = P.op("scalar", "activation", Alog[:, :], Alog[:, :], AF.Exp, wait=[("qD", wcst)])
    av = P.op("vector", "tensor_scalar", Alog[:, :], Alog[:, :], -1.0, None, ALU.mult,
              wait=[("sS", av)])

    for g in range(4):
        wt = [("sV", av)] if g == 0 else ()
        P.op("vector", "tensor_tensor", H[g][:, :], X[g][:, tok(0, L)], rtot[:, :],
             ALU.mult, wait=wt)
        P.op("vector", "tensor_tensor", H[g][:, :], H[g][:, :], arow[:, :], ALU.subtract)
        P.op("vector", "tensor_scalar", H[g][:, :], H[g][:, :], lng[:, g:g + 1],
             lnb[:, g:g + 1], ALU.mult, op1=ALU.add)
    hiv = [("sV", P.cnt["sV"])]

    WIN = AR.alloc("maWIN", [128, 4 * 2048], BF16)
    WXP = AR.alloc("maWXP", [128, 8 * 64], BF16)
    WDT = AR.alloc("maWDT", [128, 1024], BF16)
    WMO = AR.alloc("maWMO", [128, 8 * 512], BF16)

    for li in range(DBG.get('ma_layers', NL)):
        guard = [("sT", P.cnt["sT"])] if li > 0 else list(hiv)
        P.dma("gpsimd", WIN[:, :], fap_d(dram["ma_inWT"], li * 512 * 2048,
              [[2048, 128], [128 * 2048, 4], [1, 2048]]), wait=guard, inc="qW")
        P.dma("gpsimd", WXP[:, :], fap_d(dram["ma_xpWT"], li * 1024 * 64,
              [[64, 128], [128 * 64, 8], [1, 64]]), inc="qW")
        P.dma("gpsimd", WDT[0:32, :], fap_d(dram["ma_dtWT"], li * 32 * 1024,
              [[1024, 32], [1, 1024]]), inc="qW")
        wlv = P.dma("gpsimd", WMO[:, :], fap_d(dram["ma_moWT"], li * 1024 * 512,
                    [[512, 128], [128 * 512, 8], [1, 512]]), inc="qW")
        zs = P.op("vector", "memset", hstate[:, :], 0.0,
                  wait=[("sG", P.cnt["sG"])])

        scr_free = []
        bb_free = {0: [], 1: []}
        da_free = {r: [] for r in range(3)}
        sc_free = {r: [] for r in range(3)}
        db_free = {r: [] for r in range(3)}
        tv_free = {r: [] for r in range(3)}
        for mc in range(DBG.get('ma_chunks', NMC)):
            ts0 = mc * MC
            for e in ("vector", "gpsimd", "scalar"):
                P.wait(e, ("sT", P.cnt["sT"]), ("sV", P.cnt["sV"]), ("sG", P.cnt["sG"]))
            AR.push()
            hn = [AR.alloc(f"mahn{g}", [128, MC], BF16) for g in range(4)]
            rr = AR.alloc("marr", [128, MC], F32)
            XC = [AR.alloc(f"maXC{g}", [128, 4 + MC], F32) for g in range(8)]
            XTL = AR.alloc("maXTL", [128, 4 * 8], F32)
            XS = [AR.alloc(f"maXS{g}", [128, MC], F32) for g in range(2)]
            XSF = [AR.alloc(f"maXSF{g}", [128, MC], F32) for g in range(2)]
            XSB = [AR.alloc(f"maXSB{g}", [128, MC], BF16) for g in range(8)]
            SZ = [AR.alloc(f"maSZ{g}", [128, MC], BF16) for g in range(8)]
            DT_ = [AR.alloc(f"maDT{g}", [128, MC], F32) for g in range(8)]
            UU = [AR.alloc(f"maUU{g}", [128, MC], BF16) for g in range(8)]
            YG = [AR.alloc(f"maYG{g}", [128, MC], F32) for g in range(8)]
            YB = [AR.alloc(f"maYB{g}", [128, MC], BF16) for g in range(8)]
            dtt = AR.alloc("madtt", [32, MC], BF16)
            NR = 3
            DA = [AR.alloc(f"maDA{r}", [128, MC], F32) for r in range(NR)]
            DB = [AR.alloc(f"maDB{r}", [128, MC], F32) for r in range(NR)]
            SC = [AR.alloc(f"maSC{r}", [128, MC], F32) for r in range(NR)]
            TV = [AR.alloc(f"maTV{r}", [128, MC], F32) for r in range(NR)]
            Bb = [AR.alloc(f"maBb{r}", [128, MC], F32) for r in range(2)]
            Cb = [AR.alloc(f"maCb{r}", [128, MC], F32) for r in range(2)]
            tmp = AR.alloc("matmp", [128, MC], F32)

            for g in range(4):
                P.op("scalar", "activation", hn[g][:, :], H[g][:, ts0:ts0 + MC], AF.Square)
            sqv = P.cnt["sS"]
            w = P.bank_wait(0) + [("sS", sqv)]
            for g in range(4):
                P.op("tensor", "matmul", PS[0][:, 0:MC], ctx.ONES[:, :], hn[g][:, :],
                     start=(g == 0), stop=(g == 3), wait=w if g == 0 else (),
                     inc="sT" if g == 3 else None)
            mmv = P.cnt["sT"]
            P.op("scalar", "activation", rr[:, :], PS[0][:, 0:MC], AF.Ln, bias=cc(EPS),
                 scale=cc(1.0 / DC), wait=[("sT", mmv)])
            sv = P.op("scalar", "activation", rr[:, :], rr[:, :], AF.Exp, scale=cc(-0.5))
            P.bank_done(0, "sS", sv)
            for g in range(4):
                P.op("vector", "tensor_tensor", tmp[:, :], H[g][:, ts0:ts0 + MC],
                     rr[:, :], ALU.mult, wait=[("sS", sv)] if g == 0 else ())
                P.op("vector", "tensor_scalar", hn[g][:, :], tmp[:, :],
                     rms[:, li * 4 + g:li * 4 + g + 1], None, ALU.mult)
            hnv = [("sV", P.cnt["sV"])]

            for m in range(16):
                b = m % 4
                w = P.bank_wait(b)
                if m == 0:
                    w = w + [("qW", wlv)] + hnv
                for k in range(4):
                    P.op("tensor", "matmul", PS[b][:, 0:MC],
                         WIN[:, k * 2048 + m * 128:k * 2048 + (m + 1) * 128],
                         hn[k][:, :], start=(k == 0), stop=(k == 3),
                         wait=w if k == 0 else (), inc="sT" if k == 3 else None)
                mmv = P.cnt["sT"]
                if m < 8:
                    sv = P.op("scalar", "activation", XC[m][:, 4:4 + MC], PS[b][:, 0:MC],
                              AF.Copy, wait=[("sT", mmv)])
                else:
                    sv = P.op("scalar", "activation", SZ[m - 8][:, :], PS[b][:, 0:MC],
                              AF.Silu, wait=[("sT", mmv)])
                P.bank_done(b, "sS", sv)
            xzv = P.cnt["sS"]

            if mc == 0:
                tlv = P.op("vector", "memset", XTL[:, :], 0.0)
            for g in range(8):
                eng = "vector" if g % 2 == 0 else "gpsimd"
                P.op(eng, "tensor_copy", XC[g][:, 0:4], XTL[:, g * 4:(g + 1) * 4],
                     wait=[("sS", xzv)] if g < 2 else ())
            tcv = [("sV", P.cnt["sV"]), ("sG", P.cnt["sG"])]
            for g in range(8):
                eng = "vector" if g % 2 == 0 else "gpsimd"
                P.op(eng, "tensor_copy", XTL[:, g * 4:(g + 1) * 4], XC[g][:, MC:MC + 4])
            tsv = [("sV", P.cnt["sV"]), ("sG", P.cnt["sG"])]

            xs_free = {0: [], 1: []}
            xsf_free = {0: [], 1: []}
            for g in range(8):
                xr = g % 2
                wt = (tcv + tsv if g == 0 else []) + xs_free[xr]
                P.op("vector", "tensor_scalar", XS[xr][:, :], XC[g][:, 1:1 + MC],
                     cwc[:, li * 32 + 0 * 8 + g:li * 32 + 0 * 8 + g + 1], None, ALU.mult,
                     wait=wt)
                for kk2 in range(1, 4):
                    P.op("vector", "scalar_tensor_tensor", XS[xr][:, :],
                         XC[g][:, 1 + kk2:1 + kk2 + MC],
                         cwc[:, li * 32 + kk2 * 8 + g:li * 32 + kk2 * 8 + g + 1],
                         XS[xr][:, :], ALU.mult, ALU.add)
                cvg = P.cnt["sV"]
                sx = P.op("scalar", "activation", XSF[xr][:, :], XS[xr][:, :], AF.Silu,
                          bias=cbc[:, li * 8 + g:li * 8 + g + 1],
                          wait=[("sV", cvg)] + xsf_free[xr])
                xs_free[xr] = [("sS", sx)]
                sx2 = P.op("scalar", "activation", XSB[g][:, :], XSF[xr][:, :], AF.Copy)
                ygi = P.op("vector", "tensor_scalar", YG[g][:, :], XSF[xr][:, :],
                           Dpc[:, li * 8 + g:li * 8 + g + 1], None, ALU.mult,
                           wait=[("sS", sx2)])
                xsf_free[xr] = [("sV", ygi)]
            xsv = P.cnt["sS"]
            ygv = P.cnt["sV"]

            w = P.bank_wait(4) + [("sS", xsv)]
            for k in range(8):
                P.op("tensor", "matmul", PS[4][0:64, 0:MC], WXP[:, k * 64:(k + 1) * 64],
                     XSB[k][:, :], start=(k == 0), stop=(k == 7),
                     wait=w if k == 0 else (), inc="sT" if k == 7 else None)
            mmv = P.cnt["sT"]
            dte = P.op("vector", "tensor_copy", dtt[0:32, :], PS[4][0:32, 0:MC],
                       wait=[("sT", mmv)])
            bce = P.op("vector", "tensor_copy", tmp[32:64, :], PS[4][32:64, 0:MC])
            P.bank_done(4, "sV", bce)
            bcd = P.dma("sync", fap_d(scr, 0, [[MC, 32], [1, MC]]),
                        tmp[32:64, :], wait=[("sV", bce)] + scr_free, inc="qD")
            scr_free = []

            for mg in range(8):
                b = mg % 2
                w = P.bank_wait(b)
                P.op("tensor", "matmul", PS[b][:, 0:MC],
                     WDT[0:32, mg * 128:(mg + 1) * 128], dtt[0:32, :],
                     start=True, stop=True, wait=w + ([("sV", dte)] if mg == 0 else []),
                     inc="sT")
                mmv = P.cnt["sT"]
                sv = P.op("scalar", "activation", DT_[mg][:, :], PS[b][:, 0:MC], AF.Abs,
                          bias=dtb[:, li * 8 + mg:li * 8 + mg + 1], wait=[("sT", mmv)])
                sv = P.op("scalar", "activation", tmp[:, :], DT_[mg][:, :], AF.Exp,
                          scale=cc(-1.0),
                          wait=[("sV", P.cnt["sV"])] if mg > 0 else [("qD", bcd)])
                sv = P.op("scalar", "activation", tmp[:, :], tmp[:, :], AF.Ln, bias=cc(1.0))
                rv = P.op("vector", "tensor_scalar", DT_[mg][:, :], PS[b][:, 0:MC],
                          dtb[:, li * 8 + mg:li * 8 + mg + 1], 0.0, ALU.add, op1=ALU.max,
                          wait=[("sS", sv)])
                rv = P.op("vector", "tensor_tensor", DT_[mg][:, :], DT_[mg][:, :],
                          tmp[:, :], ALU.add)
                P.bank_done(b, "sV", rv)
                P.op("gpsimd", "tensor_tensor", UU[mg][:, :], DT_[mg][:, :], XSB[mg][:, :],
                     ALU.mult, wait=[("sV", rv), ("sS", xsv)] if mg == 0 else [("sV", rv)])
            duv = [("sV", P.cnt["sV"]), ("sG", P.cnt["sG"])]


            for n in range(NS):
                r = n % 2
                bv = P.dma("sync", Bb[r][:, :], fap_d(scr, n * MC, [[0, 128], [1, MC]]),
                           wait=[("qD", bcd)] + bb_free[r], inc="qD")
                cv2 = P.dma("sync", Cb[r][:, :], fap_d(scr, (16 + n) * MC, [[0, 128], [1, MC]]),
                            inc="qD")
                if n == NS - 1:
                    scr_free = [("qD", cv2)]
                for g in range(8):
                    rg = (n * 8 + g) % 3
                    col = g * 16 + n
                    da = P.op("scalar", "activation", DA[rg][:, :], DT_[g][:, :], AF.Exp,
                              scale=Alog[:, li * 128 + col:li * 128 + col + 1],
                              wait=da_free[rg] + (duv if n == 0 and g == 0 else []))
                    da_free[rg] = []
                    db = P.op("gpsimd", "tensor_tensor", DB[rg][:, :], UU[g][:, :],
                              Bb[r][:, :], ALU.mult,
                              wait=([("qD", cv2)] if g == 0 else []) + db_free[rg])
                    db_free[rg] = []
                    scv = P.op("vector", "tensor_tensor_scan", SC[rg][:, :], DA[rg][:, :],
                               DB[rg][:, :], hstate[:, col:col + 1], ALU.mult, ALU.add,
                               wait=[("sS", da), ("sG", db)] + sc_free[rg] +
                                    ([("sV", ygv)] if n == 0 and g == 0 else []))
                    da_free[rg] = [("sV", scv)]
                    hsv = P.op("gpsimd", "tensor_copy", hstate[:, col:col + 1],
                               SC[rg][:, MC - 1:MC], wait=[("sV", scv)])
                    tv = P.op("gpsimd", "tensor_tensor", TV[rg][:, :], SC[rg][:, :],
                              Cb[r][:, :], ALU.mult, wait=tv_free[rg])
                    db_free[rg] = []
                    yv = P.op("vector", "tensor_tensor", YG[g][:, :], YG[g][:, :],
                              TV[rg][:, :], ALU.add, wait=[("sG", tv)])
                    tv_free[rg] = [("sV", yv)]
                    sc_free[rg] = [("sV", P.cnt["sV"]), ("sG", tv)]
                    if g == 7:
                        bb_free[r] = [("sG", tv), ("sV", yv)]
            scan_done = P.cnt["sV"]

            for g in range(8):
                eng = "vector" if g % 2 == 0 else "gpsimd"
                wt = [("sV", scan_done)] if g < 2 else ()
                P.op(eng, "tensor_tensor", YB[g][:, :], YG[g][:, :], SZ[g][:, :],
                     ALU.mult, wait=wt)
            ybv = [("sV", P.cnt["sV"]), ("sG", P.cnt["sG"])]

            for m in range(4):
                b = 4 + m % 2
                w = P.bank_wait(b)
                if m == 0:
                    w = w + ybv
                for k in range(8):
                    P.op("tensor", "matmul", PS[b][:, 0:MC],
                         WMO[:, k * 512 + m * 128:k * 512 + (m + 1) * 128],
                         YB[k][:, :], start=(k == 0), stop=(k == 7),
                         wait=w if k == 0 else (), inc="sT" if k == 7 else None)
                mmv = P.cnt["sT"]
                hv2 = P.op("vector", "tensor_tensor", H[m][:, ts0:ts0 + MC],
                           PS[b][:, 0:MC], H[m][:, ts0:ts0 + MC], ALU.add,
                           wait=[("sT", mmv)])
                P.bank_done(b, "sV", hv2)
            xc_tail_guard = []
            AR.pop()

    if DBG.get("ma_skip_final"):
        AR.pop()
        return
    WPO = AR.alloc("maWPO", [128, 8 * 1024], BF16)
    CAT = [AR.alloc(f"maCAT{g}", [128, L], BF16) for g in range(8)]
    stmp = AR.alloc("mastmp", [128, CH], F32)
    wpo = P.dma("gpsimd", WPO[:, :], fap_d(dram["ma_poWT"], 0,
                [[1024, 128], [128 * 1024, 8], [1, 1024]]),
                wait=[("sT", P.cnt["sT"]), ("sV", P.cnt["sV"])], inc="qW")
    hdone = P.cnt["sV"]
    for g in range(8):
        wt = [("sV", hdone)] if g == 0 else ()
        P.op("vector", "tensor_tensor", CAT[g][:, :], X[g][:, tok(0, L)], rtot[:, :],
             ALU.mult, wait=wt)
        P.op("vector", "tensor_tensor", CAT[g][:, :], CAT[g][:, :], arow[:, :], ALU.subtract)
        P.op("vector", "tensor_scalar", CAT[g][:, :], CAT[g][:, :], lng[:, g:g + 1],
             lnb[:, g:g + 1], ALU.mult, op1=ALU.add)
        if g >= 4:
            P.op("vector", "tensor_tensor", CAT[g][:, :], CAT[g][:, :], H[g - 4][:, :],
                 ALU.add)
    catv = [("sV", P.cnt["sV"])]

    for ch in range(NCH):
        cs = tok(ch * CH)
        for m in range(8):
            w = P.bank_wait(m)
            if ch == 0 and m == 0:
                w = w + [("qW", wpo)] + catv
            for k in range(8):
                P.op("tensor", "matmul", PS[m][:, :],
                     WPO[:, k * 1024 + m * 128:k * 1024 + (m + 1) * 128],
                     CAT[k][:, ch * CH:(ch + 1) * CH], start=(k == 0), stop=(k == 7),
                     wait=w if k == 0 else (), inc="sT" if k == 7 else None)
            mmv = P.cnt["sT"]
            c0 = ch * CH
            P.op("vector", "tensor_tensor", stmp[:, :], X[m][:, cs], rtot[:, c0:c0 + CH],
                 ALU.mult, wait=[("sT", mmv)])
            P.op("vector", "tensor_tensor", stmp[:, :], stmp[:, :], arow[:, c0:c0 + CH],
                 ALU.subtract)
            P.op("vector", "tensor_scalar", stmp[:, :], stmp[:, :], lng[:, m:m + 1],
                 lnb[:, m:m + 1], ALU.mult, op1=ALU.add)
            P.op("vector", "tensor_tensor", stmp[:, :], stmp[:, :], X[m][:, cs], ALU.add)
            sv = P.op("vector", "scalar_tensor_tensor", X[m][:, cs], PS[m][:, :],
                      pob[:, m:m + 1], stmp[:, :], ALU.add, ALU.add)
            P.bank_done(m, "sV", sv)
    AR.pop()


@stage("s2")
def s2(ctx):
    emit_l0attn(ctx)



def emit_l1ffn(ctx, scale=0.5):
    P, X, PS, AR, dram, cc = ctx.P, ctx.X, ctx.PS, ctx.AR, ctx.dram, ctx.const_col
    DEPTH = DBG.get("lf_depth", 6)
    SKIP_FFN = DBG.get("lf_skip_ffn", False)
    SKIP_ATTN = DBG.get("lf_skip_attn", False)
    HID = 2816
    NJB = 6
    JBW = [512, 512, 512, 512, 512, 256]
    NKS = HID // 128

    xs = ctx.scratch_x
    for i in range(8):
        P.dma("sync", fap_d(xs, i * 128 * L, [[L, 128], [1, L]]), X[i][:, tok(0, L)],
              inc="qD")
    xsv = P.cnt["qD"]
    for e in ("vector", "gpsimd", "scalar"):
        P.wait(e, ("qD", xsv))
    lnw = emit_ln_xn(ctx, None)

    AR.push()
    lng = AR.alloc("lflng", [128, 48], F32)
    lnb = AR.alloc("lflnb", [128, 48], F32)
    b1c = AR.alloc("lfb1", [128, 264], F32)
    b2c = AR.alloc("lfb2", [128, 48], F32)
    qsc = AR.alloc("lfqs", [128, 6], F32)
    ksc = AR.alloc("lfks", [128, 6], F32)
    for nm, t in [("lf_lng", lng), ("lf_lnb", lnb), ("lf_b1", b1c),
                  ("lf_b2", b2c), ("lf_qs", qsc), ("lf_ks", ksc)]:
        P.dma("sync", t[:, :], dram[nm].ap(), wait=lnw if nm == "lf_lng" else (), inc="qD")
    cstv = P.cnt["qD"]

    prev_done = [("qD", cstv)]
    for li in range(DEPTH):
        AR.push()
        ysb = [AR.alloc(f"lfys{i}", [128, L], BF16) for i in range(8)]
        cq = AR.alloc("lfcq", [128, L], BF16)
        sq_t = AR.alloc("lfsqt", [128, L], BF16)
        ck = AR.alloc("lfck", [128, L], BF16)
        sk = AR.alloc("lfsk", [128, L], BF16)
        ROT = AR.alloc("lfrot", [128, 128], BF16)
        HON = AR.alloc("lfhon", [128, 128], BF16)
        M384 = AR.alloc("lfm384", [128, 384], F32)
        cguard = list(prev_done) if li > 0 else [("sT", P.cnt["sT"])]
        for nm, t in [("rot_cq", cq), ("rot_sqp", sq_t), ("rot_ck", ck), ("rot_skp", sk),
                      ("rotmat", ROT), ("hones2", HON), ("m384f", M384)]:
            P.dma("sync", t[:, :], dram[nm].ap(),
                  wait=cguard if nm == "rot_cq" else (), inc="qD")
        cstv2 = P.cnt["qD"]
        for i in range(8):
            eng = "vector" if i % 2 == 0 else "gpsimd"
            wt = []
            if i < 2:
                wt = list(prev_done) + [("qD", cstv2)]
                if li == 0:
                    wt += lnw
            sh = GUARD - (1 if i >= 4 else 0)
            P.op(eng, "tensor_copy", ysb[i][:, :], X[i][:, sh:sh + L], wait=wt)
        ysv = [("sV", P.cnt["sV"]), ("sG", P.cnt["sG"])]

        for half in range(2 if not SKIP_ATTN else 0):
            AR.push()
            W0 = AR.alloc("lfW0", [128, 4096], BF16)
            W1_ = AR.alloc("lfW1x", [128, 4096], BF16)
            qt_ = [AR.alloc(f"lfq{m}", [128, L], BF16) for m in range(4)]
            kt_ = [AR.alloc(f"lfk{m}", [128, L], BF16) for m in range(4)]
            vt = [AR.alloc(f"lfv{t}", [128, 512], BF16) for t in range(16)]
            osb = [AR.alloc(f"lfo{m}", [128, L], BF16) for m in range(4)]
            mark = AR.off
            qq = AR.alloc("lfqq", [128, L], BF16)
            rq = AR.alloc("lfrq", [128, 512], F32)
            qq_end = AR.off
            AR.off = mark
            lgs = [AR.alloc(f"lflg{r}", [128, 384], F32) for r in range(3)]
            PTs = [AR.alloc(f"lfPT{r}", [128, 384], BF16) for r in range(3)]
            rws = [AR.alloc(f"lfrw{r}", [128, 2], F32) for r in range(3)]
            lg = lgs[0]
            AR.off = max(AR.off, qq_end)
            sqb = vt

            base = li * 1024 * 3072 + half * 512
            guard = [("sT", P.cnt["sT"])]
            wvq = P.dma("gpsimd", W0[:, :], fap_d(dram["lf_wqkv"], base,
                        [[3072, 128], [128 * 3072, 8], [1, 512]]), wait=guard, inc="qW")
            wvk = P.dma("gpsimd", W1_[:, :], fap_d(dram["lf_wqkv"], base + 1024,
                        [[3072, 128], [128 * 3072, 8], [1, 512]]), inc="qW")
            WQ, WK, WV, WO = W0, W1_, W0, W1_

            def qk_pipeline(Wbuf, wv, out_tiles, rtab_c, rtab_s, sc_col, hon_guard):
                for m in range(4):
                    qq_guard = [("sG", P.cnt["sG"])]
                    for ch in range(NCH):
                        b = (m * NCH + ch) % 4
                        w = P.bank_wait(b)
                        if m == 0 and ch == 0:
                            w = w + [("qW", wv)] + ysv
                        for k in range(8):
                            P.op("tensor", "matmul", PS[b][:, :],
                                 Wbuf[:, k * 512 + m * 128: k * 512 + (m + 1) * 128],
                                 ysb[k][:, ch * CH:(ch + 1) * CH], start=(k == 0),
                                 stop=(k == 7), wait=w if k == 0 else (),
                                 inc="sT" if k == 7 else None)
                        mmv = P.cnt["sT"]
                        sv = P.op("scalar", "activation", qq[:, ch * CH:(ch + 1) * CH],
                                  PS[b][:, :], AF.Copy,
                                  wait=[("sT", mmv)] + (qq_guard if ch == 0 else []))
                        P.bank_done(b, "sS", sv)
                    for ch in range(NCH):
                        P.op("scalar", "activation", sqb[ch][:, :],
                             qq[:, ch * CH:(ch + 1) * CH], AF.Square,
                             wait=[("sT", hon_guard[0])] if ch == 0 else ())
                    sqv = P.cnt["sS"]
                    for ch in range(NCH):
                        b2_ = 4 + (ch % 2)
                        w = P.bank_wait(b2_)
                        P.op("tensor", "matmul", PS[b2_][:, :], HON[:, :],
                             sqb[ch][:, :], start=True, stop=True,
                             wait=w + [("sS", sqv)], inc="sT")
                        hon_guard[0] = P.cnt["sT"]
                        mmv = P.cnt["sT"]
                        P.op("scalar", "activation", rq[:, :], PS[b2_][:, :], AF.Ln,
                             bias=cc(1e-30),
                             wait=[("sT", mmv), ("sV", P.cnt["sV"])])
                        sv = P.op("scalar", "activation", rq[:, :], rq[:, :], AF.Exp,
                                  scale=cc(-0.5))
                        P.bank_done(b2_, "sS", sv)
                        P.op("vector", "tensor_scalar", rq[:, :], rq[:, :], 1e12, sc_col,
                             ALU.min, op1=ALU.mult, wait=[("sS", sv)])
                        P.op("vector", "tensor_tensor", qq[:, ch * CH:(ch + 1) * CH],
                             qq[:, ch * CH:(ch + 1) * CH], rq[:, :], ALU.mult)
                        qrv = P.op("vector", "tensor_tensor",
                                   out_tiles[m][:, ch * CH:(ch + 1) * CH],
                                   qq[:, ch * CH:(ch + 1) * CH],
                                   rtab_s[:, ch * CH:(ch + 1) * CH], ALU.mult)
                        b3 = 6 + (ch % 2)
                        w = P.bank_wait(b3)
                        P.op("tensor", "matmul", PS[b3][:, :], ROT[:, :],
                             out_tiles[m][:, ch * CH:(ch + 1) * CH], start=True,
                             stop=True, wait=w + [("sV", qrv)], inc="sT")
                        mmv = P.cnt["sT"]
                        gv1 = P.op("gpsimd", "tensor_tensor", qq[:, ch * CH:(ch + 1) * CH],
                                   qq[:, ch * CH:(ch + 1) * CH],
                                   rtab_c[:, ch * CH:(ch + 1) * CH], ALU.mult,
                                   wait=[("sV", qrv)])
                        vv1 = P.op("vector", "tensor_tensor",
                                   out_tiles[m][:, ch * CH:(ch + 1) * CH],
                                   PS[b3][:, :], qq[:, ch * CH:(ch + 1) * CH], ALU.add,
                                   wait=[("sT", mmv), ("sG", gv1)])
                        P.bank_done(b3, "sV", vv1)

            hon_guard = [0]
            qk_pipeline(WQ, wvq, qt_, cq, sq_t, qsc[:, li:li + 1], hon_guard)
            wvv = P.dma("gpsimd", WV[:, :], fap_d(dram["lf_wqkv"], base + 2048,
                        [[3072, 128], [128 * 3072, 8], [1, 512]]),
                        wait=[("sT", P.cnt["sT"])], inc="qW")
            qk_pipeline(WK, wvk, kt_, ck, sk, ksc[:, li:li + 1], hon_guard)
            wvo = P.dma("gpsimd", WO[:, :], fap_d(dram["lf_wo"],
                        li * 1024 * 1024 + half * 512 * 1024,
                        [[1024, 128], [128 * 1024, 4], [1, 1024]]),
                        wait=[("sT", P.cnt["sT"])], inc="qW")

            for tt in range(16):
                b = tt % 4
                w = P.bank_wait(b)
                if tt == 0:
                    w = w + [("qW", wvv)]
                for k in range(8):
                    P.op("tensor", "matmul", PS[b][:, :],
                         ysb[k][:, tt * 128:(tt + 1) * 128],
                         WV[:, k * 512:(k + 1) * 512], start=(k == 0), stop=(k == 7),
                         wait=w if k == 0 else (), inc="sT" if k == 7 else None)
                mmv = P.cnt["sT"]
                sv = P.op("scalar", "activation", vt[tt][:, :], PS[b][:, :], AF.Copy,
                          wait=[("sT", mmv)])
                P.bank_done(b, "sS", sv)

            QKS = 8.0
            NROT = 3
            qk_done = [("sG", P.cnt["sG"]), ("sV", P.cnt["sV"]), ("sT", P.cnt["sT"])]
            lg_free = {r: list(qk_done) for r in range(NROT)}
            pt_free = {r: list(qk_done) for r in range(NROT)}
            rw_free = {r: list(qk_done) for r in range(NROT)}
            QKB = [0, 1, 5]
            insts = [(hh, qti) for hh in range(8) for qti in range(16)]
            mmq = {}

            def emit_qk(i):
                hh, qti = insts[i]
                ft = hh // 2
                pb = (hh % 2) * 64
                ctx0 = max(0, qti * 128 - 256)
                cw_ = qti * 128 + 128 - ctx0
                bq = QKB[i % 3]
                w = P.bank_wait(bq)
                P.op("tensor", "matmul", PS[bq][:, 0:cw_],
                     qt_[ft][pb:pb + 64, qti * 128:(qti + 1) * 128],
                     kt_[ft][pb:pb + 64, ctx0:ctx0 + cw_],
                     start=True, stop=True, wait=w, inc="sT")
                mmq[i] = P.cnt["sT"]

            emit_qk(0)
            emit_qk(1)
            for i, (hh, qti) in enumerate(insts):
                ft = hh // 2
                pb = (hh % 2) * 64
                ctx0 = max(0, qti * 128 - 256)
                cw_ = qti * 128 + 128 - ctx0
                nct = cw_ // 128
                r = i % NROT
                bq = QKB[i % 3]
                lgr, ptr, rwr = lgs[r], PTs[r], rws[r]
                sv = P.op("vector", "scalar_tensor_tensor", lgr[:, 0:cw_],
                          PS[bq][:, 0:cw_], QKS, M384[:, 384 - cw_:384],
                          ALU.mult, ALU.add, wait=[("sT", mmq[i])] + lg_free[r])
                lg_free[r] = []
                P.bank_done(bq, "sV", sv)
                P.op("vector", "tensor_reduce", rwr[:, 0:1], lgr[:, 0:cw_],
                     AX.X, ALU.max, wait=rw_free[r])
                rw_free[r] = []
                nm = P.op("vector", "tensor_scalar", rwr[:, 1:2], rwr[:, 0:1],
                          -1.0, None, ALU.mult)
                se = P.op("scalar", "activation", lgr[:, 0:cw_], lgr[:, 0:cw_],
                          AF.Exp, bias=rwr[:, 1:2], accum_out=rwr[:, 0:1],
                          wait=[("sV", nm)])
                P.op("vector", "reciprocal", rwr[:, 1:2], rwr[:, 0:1], wait=[("sS", se)])
                nv = P.op("vector", "tensor_scalar", lgr[:, 0:cw_], lgr[:, 0:cw_],
                          rwr[:, 1:2], None, ALU.mult)
                rw_free[r] = [("sV", nv)]
                if i + 2 < len(insts):
                    emit_qk(i + 2)
                b3 = 2 + (i % 2)
                w = P.bank_wait(b3)
                for t2 in range(nct):
                    P.op("tensor", "matmul", PS[b3][:, t2 * 128:(t2 + 1) * 128],
                         lgr[:, t2 * 128:(t2 + 1) * 128], ctx.IDNF[:, :],
                         is_transpose=True, start=(t2 == 0), stop=(t2 == nct - 1),
                         wait=(w + [("sV", nv)] + pt_free[r]) if t2 == 0 else (),
                         inc="sT" if t2 == nct - 1 else None)
                pt_free[r] = []
                trv = P.cnt["sT"]
                ptv = P.op("vector", "tensor_copy", ptr[:, 0:cw_], PS[b3][:, 0:cw_],
                           wait=[("sT", trv)])
                P.bank_done(b3, "sV", ptv)
                lg_free[r] = [("sT", P.cnt["sT"])]
                w4 = P.bank_wait(4)
                for t2 in range(nct):
                    P.op("tensor", "matmul", PS[4][pb:pb + 64, 0:128],
                         vt[ctx0 // 128 + t2][:, hh * 64:(hh + 1) * 64],
                         ptr[:, t2 * 128:(t2 + 1) * 128],
                         start=(t2 == 0), stop=(t2 == nct - 1),
                         tile_position=(0, pb),
                         wait=(w4 + [("sV", ptv)]) if t2 == 0 else (),
                         inc="sT" if t2 == nct - 1 else None)
                mmv = P.cnt["sT"]
                pt_free[r] = [("sT", mmv)]
                sv = P.op("scalar", "activation",
                          osb[ft][pb:pb + 64, qti * 128:(qti + 1) * 128],
                          PS[4][pb:pb + 64, 0:128], AF.Copy, wait=[("sT", mmv)])
                P.bank_done(4, "sS", sv)

            ov = P.cnt["sS"]
            for ch in range(NCH):
                cs = tok(ch * CH)
                for m in range(8):
                    w = P.bank_wait(m)
                    if ch == 0 and m == 0:
                        w = w + [("qW", wvo), ("sS", ov)]
                    for k in range(4):
                        P.op("tensor", "matmul", PS[m][:, :],
                             WO[:, k * 1024 + m * 128: k * 1024 + (m + 1) * 128],
                             osb[k][:, ch * CH:(ch + 1) * CH], start=(k == 0),
                             stop=(k == 3), wait=w if k == 0 else (),
                             inc="sT" if k == 3 else None)
                    mmv = P.cnt["sT"]
                    sv = P.op("vector", "tensor_tensor", X[m][:, cs], PS[m][:, :],
                              X[m][:, cs], ALU.add, wait=[("sT", mmv)])
                    P.bank_done(m, "sV", sv)
            AR.pop()
        AR.pop()
        attn_done = P.cnt["sV"]

        if SKIP_FFN:
            break
        for e in ("vector", "gpsimd", "scalar"):
            P.wait(e, ("sV", attn_done))
        gb = [(lng[:, li * 8 + i:li * 8 + i + 1], lnb[:, li * 8 + i:li * 8 + i + 1])
              for i in range(8)]
        for th in range(2):
            t0 = th * 1024
            for e in ("vector", "gpsimd", "scalar"):
                P.wait(e, ("sV", P.cnt["sV"]))
            AR.push()
            xn2 = [AR.alloc(f"lfxn{i}", [128, 1024], BF16) for i in range(8)]
            lnw2 = emit_ln_xn(ctx, xn2, gamma_beta=gb, tok0=t0, ntok=1024,
                              shifts=[0, 0, 0, 0, -1, -1, -1, -1])
            if DBG.get("lf_dump_xn2"):
                for i in range(8):
                    P.op("vector", "tensor_copy", X[i][:, tok(t0, 1024)], xn2[i][:, :],
                         wait=lnw2 if i == 0 else ())
                return
            hbuf = [AR.alloc(f"lfh{k}", [128, 1024], BF16) for k in range(NKS)]
            W1A = AR.alloc("lfW1A", [128, 4096], BF16)
            W1G = AR.alloc("lfW1G", [128, 4096], BF16)
            W2F = AR.alloc("lfW2F", [128, NKS * 512], BF16)
            gel = [AR.alloc("lfgel0", [128, 512], F32), AR.alloc("lfgel1", [128, 512], F32)]
            gel_free = {0: [], 1: []}
            gi = 0

            w1base = li * 1024 * 5632

            def load_w1(jb, guard):
                jw = JBW[jb]
                a1 = fap_d(dram["lf_w1"], w1base + jb * 512,
                           [[5632, 128], [128 * 5632, 8], [1, jw]])
                P.dma("gpsimd", W1A[:, 0:8 * jw], a1, wait=guard, inc="qW")
                a2 = fap_d(dram["lf_w1"], w1base + 2816 + jb * 512,
                           [[5632, 128], [128 * 5632, 8], [1, jw]])
                return P.dma("gpsimd", W1G[:, 0:8 * jw], a2, inc="qW")

            def load_w2(mh, guard):
                return P.dma("gpsimd", W2F[:, :],
                             fap_d(dram["lf_w2"], li * 2816 * 1024 + mh * 512,
                                   [[1024, 128], [128 * 1024, NKS], [1, 512]]),
                             wait=guard, inc="qW")

            wv1 = load_w1(0, lnw2)
            ks0 = 0
            for jb in range(NJB):
                jw = JBW[jb]
                nmt = jw // 128
                for sub in range(2):
                    ss = slice(sub * 512, (sub + 1) * 512)
                    for mt in range(nmt):
                        ba = mt % 4
                        bgt = 4 + mt % 4
                        w = P.bank_wait(ba)
                        if sub == 0 and mt == 0:
                            w = w + [("qW", wv1)]
                        for k in range(8):
                            P.op("tensor", "matmul", PS[ba][:, :],
                                 W1A[:, k * jw + mt * 128: k * jw + (mt + 1) * 128],
                                 xn2[k][:, ss], start=(k == 0), stop=(k == 7),
                                 wait=w if k == 0 else ())
                        w = P.bank_wait(bgt)
                        for k in range(8):
                            P.op("tensor", "matmul", PS[bgt][:, :],
                                 W1G[:, k * jw + mt * 128: k * jw + (mt + 1) * 128],
                                 xn2[k][:, ss], start=(k == 0), stop=(k == 7),
                                 wait=w if k == 0 else (), inc="sT" if k == 7 else None)
                        mmv = P.cnt["sT"]
                        kst = ks0 + mt
                        gr = gi % 2
                        gi += 1
                        sv = P.op("scalar", "activation", gel[gr][:, :], PS[bgt][:, :],
                                  AF.Gelu,
                                  bias=b1c[:, li * 44 + 22 + kst:li * 44 + 23 + kst],
                                  wait=[("sT", mmv)] + gel_free[gr])
                        P.bank_done(bgt, "sS", sv)
                        hv = P.op("vector", "scalar_tensor_tensor", hbuf[kst][:, ss],
                                  PS[ba][:, :], b1c[:, li * 44 + kst:li * 44 + 1 + kst],
                                  gel[gr][:, :], ALU.add, ALU.mult, wait=[("sS", sv)])
                        gel_free[gr] = [("sV", hv)]
                        P.bank_done(ba, "sV", hv)
                if jb + 1 < NJB:
                    wv1 = load_w1(jb + 1, [("sT", P.cnt["sT"])])
                ks0 += nmt
            h_all = P.cnt["sV"]

            if DBG.get("lf_dump_h"):
                for i in range(8):
                    P.op("vector", "tensor_copy", X[i][:, tok(t0, 1024)],
                         hbuf[i + DBG.get("lf_dump_h_off", 0)][:, :],
                         wait=[("sV", h_all)] if i == 0 else ())
                return
            wv2 = load_w2(0, [("sV", h_all)] if th == 0 and li == 0 else
                          [("sT", P.cnt["sT"])])
            for mh in range(2):
                for k in range(NKS):
                    for sub in range(2):
                        for m4 in range(4):
                            bb = sub * 4 + m4
                            w = []
                            if k == 0:
                                w = P.bank_wait(bb)
                                if sub == 0 and m4 == 0:
                                    w = w + [("sV", h_all), ("qW", wv2)]
                            P.op("tensor", "matmul", PS[bb][:, :],
                                 W2F[:, k * 512 + m4 * 128:k * 512 + (m4 + 1) * 128],
                                 hbuf[k][:, sub * 512:(sub + 1) * 512],
                                 start=(k == 0), stop=(k == NKS - 1),
                                 wait=w, inc="sT" if (sub == 1 and m4 == 3) else None)
                mm_all = P.cnt["sT"]
                if mh == 0:
                    wv2 = load_w2(1, [("sT", mm_all)])
                for sub in range(2):
                    for m4 in range(4):
                        bb = sub * 4 + m4
                        m = mh * 4 + m4
                        cs = tok(t0 + sub * 512)
                        sv = P.op("vector", "scalar_tensor_tensor", X[m][:, cs],
                                  PS[bb][:, :], b2c[:, li * 8 + m:li * 8 + m + 1],
                                  X[m][:, cs], ALU.add, ALU.add,
                                  wait=[("sT", mm_all)] if (sub == 0 and m4 == 0) else ())
                        P.bank_done(bb, "sV", sv)
            AR.pop()
        prev_done = [("sV", P.cnt["sV"])]

    tmpx = AR.alloc("lftmpx", [128, L], F32)
    for m in range(8):
        wv = P.dma("sync", tmpx[:, :], fap_d(xs, m * 128 * L, [[L, 128], [1, L]]),
                   wait=[("sV", P.cnt["sV"])], inc="qD")
        P.op("vector", "scalar_tensor_tensor", X[m][:, tok(0, L)], X[m][:, tok(0, L)],
             scale, tmpx[:, :], ALU.mult, ALU.add, wait=[("qD", wv)])
    AR.pop()


@stage("s4")
def s4(ctx):
    emit_l1ffn(ctx)


@stage("s6")
def s6(ctx):
    emit_l1ffn(ctx)



def emit_l1attn(ctx):
    P, X, PS, AR, dram, cc = ctx.P, ctx.X, ctx.PS, ctx.AR, ctx.dram, ctx.const_col
    AR.push()
    xn = [AR.alloc(f"laxn{i}", [128, L], BF16) for i in range(8)]
    lnw = emit_ln_xn(ctx, xn)

    bo = AR.alloc("labo", [128, 8], F32)
    dtri = AR.alloc("ladtri", [128, 128], F32)
    P.dma("sync", bo[:, :], dram["att_bo"].ap(), wait=lnw, inc="qD")
    P.dma("sync", dtri[:, :], dram["dtri"].ap(), inc="qD")
    wcst = P.cnt["qD"]

    GW = 2824
    for half in range(2):
        AR.push()
        q = [AR.alloc(f"laq{m}", [128, L], BF16) for m in range(4)]
        kk = [AR.alloc(f"lak{m}", [128, L], BF16) for m in range(4)]
        vt = [AR.alloc(f"lav{t}", [128, 512], BF16) for t in range(16)]
        WA = AR.alloc("laWA", [128, 4096], BF16)
        WB = AR.alloc("laWB", [128, 4096], BF16)
        osb = [AR.alloc(f"lao{m}", [128, L], BF16) for m in range(4)]
        G = AR.alloc("laG", [128, GW], F32)
        lg = AR.alloc("lalg", [128, L], F32)
        PT = AR.alloc("laPT", [128, L], BF16)
        rowm = AR.alloc("larow", [128, 2], F32)

        def load_whalf(name, buf, guard):
            t = dram[name]
            ap = fap_d(t, half * 512, [[1024, 128], [128 * 1024, 8], [1, 512]])
            return P.dma("gpsimd", buf[:, :], ap, wait=guard, inc="qW")

        def load_wo_half(buf, guard):
            t = dram["att_wo"]
            ap = fap_d(t, half * 512 * 1024, [[1024, 128], [128 * 1024, 4], [1, 1024]])
            return P.dma("gpsimd", buf[:, :], ap, wait=guard, inc="qW")

        hguard = lnw if half == 0 else [("sT", P.cnt["sT"])]
        wva = load_whalf("att_wq", WA, hguard)
        wvb = load_whalf("att_wk", WB, [])

        def proj(buf, wv, out_tiles, sem_after):
            for ch in range(NCH):
                for m in range(4):
                    b = (ch * 4 + m) % 4
                    w = P.bank_wait(b)
                    if ch == 0 and m == 0:
                        w = w + [("qW", wv)]
                    for k in range(8):
                        P.op("tensor", "matmul", PS[b][:, :],
                             buf[:, k * 512 + m * 128: k * 512 + (m + 1) * 128],
                             xn[k][:, ch * CH:(ch + 1) * CH], start=(k == 0), stop=(k == 7),
                             wait=w if k == 0 else (), inc="sT" if k == 7 else None)
                    mmv = P.cnt["sT"]
                    sv = P.op("scalar", "activation", out_tiles[m][:, ch * CH:(ch + 1) * CH],
                              PS[b][:, :], AF.Copy, wait=[("sT", mmv)])
                    P.bank_done(b, "sS", sv)
            return P.cnt["sT"]

        qmm = proj(WA, wva, q, None)
        wva2 = load_whalf("att_wv", WA, [("sT", qmm)])
        kmm = proj(WB, wvb, kk, None)
        wvb2 = load_wo_half(WB, [("sT", kmm)])

        for tt in range(16):
            b = tt % 4
            w = P.bank_wait(b)
            if tt == 0:
                w = w + [("qW", wva2)]
            for k in range(8):
                P.op("tensor", "matmul", PS[b][:, :],
                     xn[k][:, tt * 128:(tt + 1) * 128],
                     WA[:, k * 512:(k + 1) * 512], start=(k == 0), stop=(k == 7),
                     wait=w if k == 0 else (), inc="sT" if k == 7 else None)
            mmv = P.cnt["sT"]
            sv = P.op("scalar", "activation", vt[tt][:, :], PS[b][:, :], AF.Copy,
                      wait=[("sT", mmv)])
            P.bank_done(b, "sS", sv)

        g_guard = [("qD", wcst)] if half == 0 else [("sV", P.cnt["sV"])]
        for hh in range(8):
            h = half * 8 + hh
            gv = P.dma("sync", G[:, :], fap_d(dram["att_T"], h * 2952, [[1, 128], [1, GW]]),
                       wait=g_guard, inc="qD")
            ft = hh // 2
            pb = (hh % 2) * 64
            for qt in range(16):
                kw = (qt + 1) * 128
                nkc = (kw + 511) // 512
                for kc in range(nkc):
                    ks = min(512, kw - kc * 512)
                    b = kc % 2
                    w = P.bank_wait(b)
                    P.op("tensor", "matmul", PS[b][:, 0:ks],
                         q[ft][pb:pb + 64, qt * 128:(qt + 1) * 128],
                         kk[ft][pb:pb + 64, kc * 512:kc * 512 + ks],
                         start=True, stop=True, wait=w + [("qD", gv)] if kc == 0 else w,
                         inc="sT")
                    mmv = P.cnt["sT"]
                    base = qt * 128 - kc * 512 + 896
                    sv = P.op("vector", "tensor_tensor", lg[:, kc * 512:kc * 512 + ks],
                              PS[b][:, 0:ks], fap(G, base, [[-1, ks]]), ALU.add,
                              wait=[("sT", mmv)])
                    P.bank_done(b, "sV", sv)
                P.op("vector", "tensor_tensor", lg[:, kw - 128:kw], lg[:, kw - 128:kw],
                     dtri[:, :], ALU.min,
                     wait=[("qD", wcst)] if (hh == 0 and qt == 0 and half == 0) else ())
                P.op("vector", "tensor_reduce", rowm[:, 0:1], lg[:, 0:kw], AX.X, ALU.max)
                nm = P.op("vector", "tensor_scalar", rowm[:, 1:2], rowm[:, 0:1],
                          -1.0 / 64.0, None, ALU.mult)
                se = P.op("scalar", "activation", lg[:, 0:kw], lg[:, 0:kw], AF.Exp,
                          scale=cc(1.0 / 64.0), bias=rowm[:, 1:2], accum_out=rowm[:, 0:1],
                          wait=[("sV", nm)])
                P.op("vector", "reciprocal", rowm[:, 1:2], rowm[:, 0:1], wait=[("sS", se)])
                nv = P.op("vector", "tensor_scalar", lg[:, 0:kw], lg[:, 0:kw],
                          rowm[:, 1:2], None, ALU.mult)
                for g4 in range((qt + 4) // 4):
                    t2s = list(range(g4 * 4, min(qt + 1, g4 * 4 + 4)))
                    b = 2 + (g4 % 2)
                    w = P.bank_wait(b)
                    for j, t2 in enumerate(t2s):
                        P.op("tensor", "matmul", PS[b][:, j * 128:(j + 1) * 128],
                             lg[:, t2 * 128:(t2 + 1) * 128], ctx.IDNF[:, :],
                             is_transpose=True, start=(j == 0), stop=(j == len(t2s) - 1),
                             wait=(w + [("sV", nv)]) if j == 0 else (),
                             inc="sT" if j == len(t2s) - 1 else None)
                    trv = P.cnt["sT"]
                    sv = P.op("vector", "tensor_copy",
                              PT[:, g4 * 512:g4 * 512 + len(t2s) * 128],
                              PS[b][:, 0:len(t2s) * 128], wait=[("sT", trv)])
                    P.bank_done(b, "sV", sv)
                ptv = P.cnt["sV"]
                w4 = P.bank_wait(4)
                for t2 in range(qt + 1):
                    P.op("tensor", "matmul", PS[4][pb:pb + 64, 0:128],
                         vt[t2][:, hh * 64:(hh + 1) * 64], PT[:, t2 * 128:(t2 + 1) * 128],
                         start=(t2 == 0), stop=(t2 == qt),
                         tile_position=(0, pb),
                         wait=(w4 + [("sV", ptv)]) if t2 == 0 else (),
                         inc="sT" if t2 == qt else None)
                mmv = P.cnt["sT"]
                sv = P.op("scalar", "activation", osb[ft][pb:pb + 64, qt * 128:(qt + 1) * 128],
                          PS[4][pb:pb + 64, 0:128], AF.Copy, wait=[("sT", mmv)])
                P.bank_done(4, "sS", sv)
            g_guard = [("sV", P.cnt["sV"])]

        ov = P.cnt["sS"]
        for ch in range(NCH):
            cs = tok(ch * CH)
            for m in range(8):
                w = P.bank_wait(m)
                if ch == 0 and m == 0:
                    w = w + [("qW", wvb2), ("sS", ov)]
                for k in range(4):
                    P.op("tensor", "matmul", PS[m][:, :],
                         WB[:, k * 1024 + m * 128: k * 1024 + (m + 1) * 128],
                         osb[k][:, ch * CH:(ch + 1) * CH], start=(k == 0), stop=(k == 3),
                         wait=w if k == 0 else (), inc="sT" if k == 3 else None)
                mmv = P.cnt["sT"]
                if half == 0:
                    sv = P.op("vector", "scalar_tensor_tensor", X[m][:, cs], PS[m][:, :],
                              bo[:, m:m + 1], X[m][:, cs], ALU.add, ALU.add,
                              wait=[("sT", mmv)])
                else:
                    sv = P.op("vector", "tensor_tensor", X[m][:, cs], PS[m][:, :],
                              X[m][:, cs], ALU.add, wait=[("sT", mmv)])
                P.bank_done(m, "sV", sv)
        AR.pop()
    AR.pop()


@stage("s5")
def s5(ctx):
    emit_l1attn(ctx)




def build_nc(weights_meta, stages):
    nc = bass.Bass("TRN2", target_bir_lowering=False,
                   detect_race_conditions=not DBG.get("no_race_detect", False))
    P = Prog(nc)

    dram = {}
    for name, (shape, dt) in weights_meta.items():
        dram[name] = nc.dram_tensor(name, list(shape), dt, kind="ExternalInput")
    xout = nc.dram_tensor("xout", [D, L], F32, kind="ExternalOutput")

    X = [nc.alloc_sbuf_tensor(f"x{i}", [128, XW], F32) for i in range(8)]
    CONST = nc.alloc_sbuf_tensor("consts", [128, 16], F32)
    IDN = nc.alloc_sbuf_tensor("sb_idn", [128, 128], BF16)
    IDNF = nc.alloc_sbuf_tensor("sb_idnf", [128, 128], F32)
    ONES = nc.alloc_sbuf_tensor("ones", [128, 128], BF16)

    ctx = Ctx()
    ctx.nc, ctx.P, ctx.dram, ctx.X = nc, P, dram, X
    ctx.IDN, ctx.IDNF, ctx.ONES = IDN, IDNF, ONES
    ctx.xout = xout
    ctx.scratch_cw = nc.dram_tensor("scratch_cw", [8, L], F32, kind="Internal")
    ctx.scratch_x = nc.dram_tensor("scratch_x", [D, L], F32, kind="Internal")
    ctx.scratch_bc = nc.dram_tensor("scratch_bc", [32, 256], F32, kind="Internal")
    ctx.AR = Arena(nc, 142 * 1024)
    ctx.PS = [nc.alloc_psum_tensor(f"ps{i}", [128, 512], F32) for i in range(8)]

    CONSTS = {}

    def const_col(val):
        v = float(val)
        if v not in CONSTS:
            idx = len(CONSTS)
            assert idx < 16
            CONSTS[v] = idx
            P.op("vector", "memset", CONST[:, idx:idx + 1], v)
        return CONST[:, CONSTS[v]:CONSTS[v] + 1]
    ctx.const_col = const_col
    const_col(EPS)
    const_col(-0.5)

    for i in range(8):
        P.op("vector", "memset", X[i][:, 0:GUARD], 0.0)
    P.op("gpsimd", "memset", ONES[:, :], 1.0)
    P.dma("sync", IDN[:, :], dram["idn"].ap(), inc="qS")
    P.dma("sync", IDNF[:, :], dram["idnf"].ap(), inc="qS")
    for i in range(8):
        P.dma("sync", X[i][:, GUARD:GUARD + L], dram["x"].ap()[i * 128:(i + 1) * 128, :],
              inc="qS")
    qv = P.cnt["qS"]
    vv = P.cnt["sV"]
    gv = P.cnt["sG"]
    for e in ENGINES:
        P.wait(e, ("qS", qv), ("sV", vv), ("sG", gv))

    for st in stages:
        STAGE_FNS[st](ctx)
        P.barrier()

    for i in range(8):
        P.dma("sync", xout.ap()[i * 128:(i + 1) * 128, :], X[i][:, GUARD:GUARD + L],
              inc="qOut")
    P.wait("sync", ("qOut", P.cnt["qOut"]))

    P.build()
    return nc



def prep_core_inputs(ins, core):
    o = {}
    o["x"] = np.ascontiguousarray(np.asarray(ins["L0_attn_src"][core]).T).astype(np.float32)
    o["idn"] = np.eye(128, dtype=bfnp)
    o["idnf"] = np.eye(128, dtype=np.float32)
    o["moe_gwT"] = np.ascontiguousarray(np.asarray(ins["L0_ffn_gate_weight"]).T)
    o["moe_bias16"] = np.ascontiguousarray(
        np.tile(np.asarray(ins["L0_ffn_e_score_correction_bias"]), 16)[None])
    o["moe_gpT"] = np.ascontiguousarray(np.swapaxes(np.asarray(ins["L0_ffn_gate_proj"]), 1, 2))
    o["moe_upT"] = np.ascontiguousarray(np.swapaxes(np.asarray(ins["L0_ffn_up_proj"]), 1, 2))
    o["moe_dnT"] = np.ascontiguousarray(np.swapaxes(np.asarray(ins["L0_ffn_down_proj"]), 1, 2))
    o["att_wq"] = np.ascontiguousarray(np.asarray(ins["L1_attn_Wq"]))
    o["att_wk"] = np.ascontiguousarray(np.asarray(ins["L1_attn_Wk"]))
    o["att_wv"] = np.ascontiguousarray(np.asarray(ins["L1_attn_Wv"]))
    o["att_wo"] = np.ascontiguousarray(np.asarray(ins["L1_attn_Wo"]))
    o["att_bo"] = np.ascontiguousarray(np.asarray(ins["L1_attn_bo"]).reshape(8, 128).T)
    rel = np.asarray(ins["L1_attn_rel_pos_bias"])
    s = np.arange(2952)
    o["att_T"] = np.ascontiguousarray(rel[:, np.clip(1023 - (s - 896), 0, 1023)])
    i_ = np.arange(128)[:, None]
    j_ = np.arange(128)[None, :]
    o["dtri"] = np.where(j_ <= i_, 3e38, -3e37).astype(np.float32)

    o["lf_wqkv"] = np.ascontiguousarray(np.asarray(ins["L1_ffn_Wqkv"]))
    o["lf_wo"] = np.ascontiguousarray(np.asarray(ins["L1_ffn_Wo"]))
    w1 = np.asarray(ins["L1_ffn_W1"])
    w1p = np.zeros((6, 1024, 5632), np.float32)
    w1p[:, :, 0:2730] = w1[:, :, 0:2730]
    w1p[:, :, 2816:2816 + 2730] = w1[:, :, 2730:5460]
    o["lf_w1"] = w1p
    w2 = np.asarray(ins["L1_ffn_W2"])
    w2p = np.zeros((6, 2816, 1024), np.float32)
    w2p[:, 0:2730] = w2
    o["lf_w2"] = w2p
    b1 = np.asarray(ins["L1_ffn_b1"])
    b1p = np.zeros((6, 5632), np.float32)
    b1p[:, 0:2730] = b1[:, 0:2730]
    b1p[:, 2816:2816 + 2730] = b1[:, 2730:5460]
    o["lf_b1"] = np.ascontiguousarray(b1p.reshape(6, 44, 128).transpose(2, 0, 1).reshape(128, 264))
    o["lf_b2"] = np.ascontiguousarray(np.asarray(ins["L1_ffn_b2"]).reshape(6, 8, 128).transpose(2, 0, 1).reshape(128, 48))
    o["lf_lng"] = np.ascontiguousarray(np.asarray(ins["L1_ffn_ln_g"]).reshape(6, 8, 128).transpose(2, 0, 1).reshape(128, 48))
    o["lf_lnb"] = np.ascontiguousarray(np.asarray(ins["L1_ffn_ln_b"]).reshape(6, 8, 128).transpose(2, 0, 1).reshape(128, 48))
    o["lf_qs"] = np.ascontiguousarray(np.tile(np.asarray(ins["L1_ffn_q_scale"]), (1, 2)).T)
    o["lf_ks"] = np.ascontiguousarray(np.tile(np.asarray(ins["L1_ffn_k_scale"]), (1, 2)).T)
    t_ = np.arange(2048, dtype=np.float64)
    halfd = np.arange(0, 64, 2, dtype=np.float64)
    inv_freq = 1.0 / 10000.0 ** (halfd / 64.0)
    freqs = np.concatenate([t_[:, None] * inv_freq[None, :]] * 2, axis=1)
    base_scale = np.concatenate([(halfd + 0.4 * 64) / (1.4 * 64)] * 2)
    power = (t_ - 1024) / 128.0
    scl = base_scale[None, :] ** power[:, None]
    cq_ = (np.cos(freqs) * scl).T
    sq_ = (np.sin(freqs) * scl).T
    ck_ = (np.cos(freqs) / scl).T
    sk_ = (np.sin(freqs) / scl).T
    o["rot_cq"] = np.ascontiguousarray(np.tile(cq_, (2, 1))).astype(bfnp)
    sqp_ = np.concatenate([sq_[32:64], sq_[0:32]], axis=0)
    o["rot_sqp"] = np.ascontiguousarray(np.tile(sqp_, (2, 1))).astype(bfnp)
    o["rot_ck"] = np.ascontiguousarray(np.tile(ck_, (2, 1))).astype(bfnp)
    skp_ = np.concatenate([sk_[32:64], sk_[0:32]], axis=0)
    o["rot_skp"] = np.ascontiguousarray(np.tile(skp_, (2, 1))).astype(bfnp)
    M = np.zeros((128, 128), np.float32)
    for b_ in (0, 64):
        for j in range(32):
            M[b_ + j, b_ + 32 + j] = -1.0
            M[b_ + 32 + j, b_ + j] = 1.0
    o["rotmat"] = np.ascontiguousarray(M.T).astype(bfnp)
    H2 = np.zeros((128, 128), np.float32)
    H2[0:64, 0:64] = 1.0
    H2[64:128, 64:128] = 1.0
    o["hones2"] = H2.astype(bfnp)
    p_ = np.arange(128)[:, None]
    jj = np.arange(384)[None, :]
    o["m384f"] = np.where((jj >= p_) & (jj <= p_ + 256), 0.0, -3e37).astype(np.float32)

    o["ma_lng"] = np.ascontiguousarray(np.asarray(ins["L0_attn_ln_g"]).reshape(8, 128).T)
    o["ma_lnb"] = np.ascontiguousarray(np.asarray(ins["L0_attn_ln_b"]).reshape(8, 128).T)
    o["ma_rms"] = np.ascontiguousarray(np.asarray(ins["L0_attn_rms_w"]).reshape(2, 4, 128).transpose(2, 0, 1).reshape(128, 8))
    cw = np.asarray(ins["L0_attn_conv_w"])[:, :, 0, :]
    o["ma_convw"] = np.ascontiguousarray(cw.reshape(2, 8, 128, 4).transpose(2, 0, 3, 1).reshape(128, 64))
    o["ma_convb"] = np.ascontiguousarray(np.asarray(ins["L0_attn_conv_b"]).reshape(2, 8, 128).transpose(2, 0, 1).reshape(128, 16))
    o["ma_dtb"] = np.ascontiguousarray(np.asarray(ins["L0_attn_dt_b"]).reshape(2, 8, 128).transpose(2, 0, 1).reshape(128, 16))
    o["ma_Alog"] = np.ascontiguousarray(np.asarray(ins["L0_attn_A_log"]).reshape(2, 8, 128, 16).transpose(2, 0, 1, 3).reshape(128, 256))
    o["ma_Dp"] = np.ascontiguousarray(np.asarray(ins["L0_attn_Dp"]).reshape(2, 8, 128).transpose(2, 0, 1).reshape(128, 16))
    o["ma_pob"] = np.ascontiguousarray(np.asarray(ins["L0_attn_po_b"]).reshape(8, 128).T)
    o["ma_inWT"] = np.ascontiguousarray(np.swapaxes(np.asarray(ins["L0_attn_in_W"]), 1, 2))
    o["ma_xpWT"] = np.ascontiguousarray(np.swapaxes(np.asarray(ins["L0_attn_xp_W"]), 1, 2))
    o["ma_dtWT"] = np.ascontiguousarray(np.swapaxes(np.asarray(ins["L0_attn_dt_W"]), 1, 2))
    o["ma_moWT"] = np.ascontiguousarray(np.swapaxes(np.asarray(ins["L0_attn_mo_W"]), 1, 2))
    o["ma_poWT"] = np.ascontiguousarray(np.asarray(ins["L0_attn_po_W"]).T)
    idx8 = np.tile(np.arange(8, dtype=np.float32), 16)
    idx4 = np.tile(np.arange(4, dtype=np.float32), 16)
    o["idx8"] = idx8[None]
    o["idx8b"] = (idx8 + 1e4)[None]
    o["idx4"] = idx4[None]
    o["idx4b"] = (idx4 + 1e4)[None]
    return o


DT_MAP = {np.dtype(np.float32): F32, np.dtype(bfnp): BF16}


def kernel(**inputs):
    in_maps = [prep_core_inputs(inputs, c) for c in range(8)]
    meta = {k: (v.shape, DT_MAP[v.dtype]) for k, v in in_maps[0].items()}
    nc = build_nc(meta, stages=STAGES_TO_RUN)
    res = run_bass_kernel_spmd(nc, in_maps, core_ids=list(range(8)))
    out = np.stack([np.ascontiguousarray(r["xout"].T) for r in res.results], axis=0)
    return out.astype(np.float32)


STAGES_TO_RUN = ["s1", "s2", "s3", "s4", "s5", "s6"]


# version 10
# speedup vs baseline: 1.4236x; 1.0218x over previous
import numpy as np
import ml_dtypes
import concourse.bass as bass
import concourse.mybir as mybir
from concourse.bass_utils import run_bass_kernel_spmd

DBG = {}
F32 = mybir.dt.float32
BF16 = mybir.dt.bfloat16
AF = mybir.ActivationFunctionType
ALU = mybir.AluOpType
AX = mybir.AxisListType
bfnp = ml_dtypes.bfloat16

L = 2048
D = 1024
CH = 512
NCH = L // CH
GUARD = 4
XW = GUARD + L
EPS = 1e-5

ENGINES = ("sync", "scalar", "tensor", "vector", "gpsimd")
SELF_SEM = {"vector": "sV", "scalar": "sS", "gpsimd": "sG"}


class Prog:
    def __init__(self, nc):
        self.nc = nc
        self.ops = []
        self.cnt = {}
        self.sems = {}
        self.bank = [[] for _ in range(8)]

    def sem(self, name):
        if name not in self.sems:
            self.sems[name] = self.nc.alloc_semaphore(name=name)
            self.cnt[name] = 0
        return self.sems[name]

    def raw(self, engine, fn):
        self.ops.append((engine, fn))

    def op(self, engine, method, *a, wait=(), inc=None, incn=1, **kw):
        waits = list(wait)
        sems = self.sems
        auto = SELF_SEM.get(engine) if method != "dma_start" else None
        if auto is not None:
            self.sem(auto)
            prev = self.cnt[auto]
            if prev > 0:
                waits.append((auto, prev))
            assert inc is None or inc == auto, \
                f"op on {engine} cannot carry extra inc {inc} (one sync update per inst)"
            inc = auto
        if inc is not None:
            self.sem(inc)
        for s, _ in waits:
            self.sem(s)
        waits = tuple(waits)

        def fn(eng, method=method, a=a, kw=kw, waits=waits, inc=inc, incn=incn):
            for s, v in waits:
                if v > 0:
                    eng.wait_ge(sems[s], v)
            inst = getattr(eng, method)(*a, **kw)
            if inc is not None:
                inst.then_inc(sems[inc], incn)
        self.raw(engine, fn)
        ret = None
        if inc is not None:
            self.cnt[inc] += incn
            ret = self.cnt[inc]
        return ret

    def wait(self, engine, *wait):
        waits = tuple(wait)
        sems = self.sems
        for s, _ in waits:
            self.sem(s)

        def fn(eng, waits=waits):
            for s, v in waits:
                if v > 0:
                    eng.wait_ge(sems[s], v)
        self.raw(engine, fn)

    def dma(self, queue, out, in_, wait=(), inc=None, **kw):
        return self.op(queue, "dma_start", out, in_, wait=wait, inc=inc, incn=16, **kw)

    def barrier(self):
        vals = {}
        for e in ENGINES:
            n = {"tensor": "sT", "sync": "bar_sync"}.get(e, SELF_SEM.get(e))
            self.sem(n)
            self.op(e, "nop", inc=n if e in ("tensor", "sync") else None)
            vals[e] = (n, self.cnt[n])
        for e in ENGINES:
            self.wait(e, *[vals[x] for x in ENGINES])
        self.bank = [[] for _ in range(8)]

    def bank_wait(self, i):
        w = self.bank[i]
        self.bank[i] = []
        return w

    def bank_done(self, i, sem, val):
        self.bank[i].append((sem, val))

    def build(self):
        with self.nc.Block() as block:
            for en in ENGINES:
                ops = [fn for (e, fn) in self.ops if e == en]

                def body(eng, ops=ops):
                    for fn in ops:
                        fn(eng)
                getattr(block, en)(body)


class Arena:
    def __init__(self, nc, nbytes):
        self.nc = nc
        self.base, _ = nc.bump_sbuf(nbytes)
        self.nbytes = nbytes
        self.off = 0
        self.marks = []
        self.n = 0

    def alloc(self, name, shape, dtype):
        esz = mybir.dt.size(dtype)
        need = int(np.prod(shape[1:])) * esz
        off = (self.off + 31) // 32 * 32
        assert off + need <= self.nbytes, f"arena overflow: {name}: {off}+{need} > {self.nbytes}"
        self.off = off + need
        self.n += 1
        return self.nc.alloc_sbuf_tensor_at(f"ar_{name}_{self.n}", list(shape), dtype,
                                            offset=self.base + off)

    def push(self):
        self.marks.append(self.off)

    def pop(self):
        self.off = self.marks.pop()


def fap(t, off, dims):
    a = t.ap()
    return bass.AP(a.tensor, a.offset + off, [list(a.ap[0])] + [list(d) for d in dims])


def fap_d(t, off, dims):
    return bass.AP(t, off, [list(d) for d in dims])


STAGE_FNS = {}


def stage(name):
    def deco(fn):
        STAGE_FNS[name] = fn
        return fn
    return deco


class Ctx:
    pass


def tok(c0, n=CH):
    return slice(GUARD + c0, GUARD + c0 + n)



def emit_ln_xn(ctx, xn, gamma_beta=None, tok0=0, ntok=L, shifts=None, src=None,
               xn_slice=None):
    P, X, PS, AR, ONES, cc = ctx.P, ctx.X, ctx.PS, ctx.AR, ctx.ONES, ctx.const_col
    if src is None:
        src = X
    if shifts is None:
        shifts = [0] * 8
    inplace = xn is None
    xoff = 0 if xn_slice is None else xn_slice

    def ssl(i, c0, n):
        return slice(GUARD + tok0 + c0 + shifts[i], GUARD + tok0 + c0 + shifts[i] + n)

    nch = (ntok + CH - 1) // CH
    AR.push()
    xb = [AR.alloc(f"lnxb{i}", [128, ntok], BF16) for i in range(8)]
    sq = [AR.alloc(f"lnsq{i}", [128, ntok], BF16) for i in range(8)]
    rstd = AR.alloc("lnrstd", [128, ntok], F32)
    arow = AR.alloc("lnarow", [128, ntok], F32)
    mu = AR.alloc("lnmu", [128, CH], F32)
    var = AR.alloc("lnvar", [128, CH], F32)

    for i in range(8):
        eng = "vector" if i % 2 == 0 else "gpsimd"
        P.op(eng, "tensor_copy", xb[i][:, :], src[i][:, ssl(i, 0, ntok)])
        P.op("scalar", "activation", sq[i][:, :], src[i][:, ssl(i, 0, ntok)], AF.Square)
    vcast = P.cnt["sV"]
    gcast = P.cnt["sG"]
    scast = P.cnt["sS"]

    prev_ps_read_v = 0
    prev_var_read_s = 0
    for c in range(nch):
        cw = min(CH, ntok - c * CH)
        cs = slice(c * CH, c * CH + cw)
        w0 = P.bank_wait(0) + ([("sV", vcast), ("sG", gcast)] if c == 0
                               else [("sV", prev_ps_read_v)])
        for i in range(8):
            P.op("tensor", "matmul", PS[0][:, 0:cw], ONES[:, :], xb[i][:, cs],
                 start=(i == 0), stop=(i == 7), wait=w0 if i == 0 else ())
        w1 = P.bank_wait(1) + ([("sS", scast)] if c == 0 else [])
        for i in range(8):
            P.op("tensor", "matmul", PS[1][:, 0:cw], ONES[:, :], sq[i][:, cs],
                 start=(i == 0), stop=(i == 7), wait=w1 if i == 0 else (),
                 inc="sT" if i == 7 else None)
        mmv = P.cnt["sT"]
        P.op("vector", "tensor_scalar", mu[:, 0:cw], PS[0][:, 0:cw], 1.0 / D, None, ALU.mult,
             wait=[("sT", mmv)])
        P.op("vector", "tensor_scalar", var[:, 0:cw], PS[1][:, 0:cw], 1.0 / D, None, ALU.mult,
             wait=[("sS", prev_var_read_s)] if prev_var_read_s else ())
        prev_ps_read_v = P.cnt["sV"]
        P.bank_done(0, "sV", prev_ps_read_v)
        P.bank_done(1, "sV", prev_ps_read_v)
        P.op("vector", "tensor_tensor", rstd[:, cs], mu[:, 0:cw], mu[:, 0:cw], ALU.mult)
        vmm = P.op("vector", "tensor_tensor", var[:, 0:cw], var[:, 0:cw], rstd[:, cs],
                   ALU.subtract)
        P.op("scalar", "activation", var[:, 0:cw], var[:, 0:cw], AF.Ln, bias=cc(EPS),
             wait=[("sV", vmm)])
        prev_var_read_s = P.cnt["sS"]
        srow = P.op("scalar", "activation", rstd[:, cs], var[:, 0:cw], AF.Exp, scale=cc(-0.5))
        varow = P.op("vector", "tensor_tensor", arow[:, cs], mu[:, 0:cw], rstd[:, cs],
                     ALU.mult, wait=[("sS", srow)])
        prev_ps_read_v = varow
        for i in range(8):
            eng = "vector" if i % 2 == 0 else "gpsimd"
            wt = [("sV", varow)] if eng == "gpsimd" else ()
            if inplace:
                P.op(eng, "tensor_tensor", src[i][:, ssl(i, c * CH, cw)],
                     src[i][:, ssl(i, c * CH, cw)], rstd[:, cs], ALU.mult, wait=wt)
                P.op(eng, "tensor_tensor", src[i][:, ssl(i, c * CH, cw)],
                     src[i][:, ssl(i, c * CH, cw)], arow[:, cs], ALU.subtract)
                continue
            xsl = slice(xoff + c * CH, xoff + c * CH + cw)
            P.op(eng, "tensor_tensor", xb[i][:, cs], src[i][:, ssl(i, c * CH, cw)],
                 rstd[:, cs], ALU.mult, wait=wt)
            if gamma_beta is None:
                P.op(eng, "tensor_tensor", xn[i][:, xsl], xb[i][:, cs], arow[:, cs],
                     ALU.subtract)
            else:
                P.op(eng, "tensor_tensor", xb[i][:, cs], xb[i][:, cs], arow[:, cs],
                     ALU.subtract)
                g_ap, b_ap = gamma_beta[i]
                P.op(eng, "tensor_scalar", xn[i][:, xsl], xb[i][:, cs], g_ap, b_ap,
                     ALU.mult, op1=ALU.add)
    AR.pop()
    return [("sV", P.cnt["sV"]), ("sG", P.cnt["sG"])]



def emit_top2_sel(P, x, sel, width, gsz, ng, m, eq, cand, IDX, IDXB):
    for p in range(2):
        P.op("vector", "tensor_reduce", m[:, 0:ng], fap(x, 0, [[gsz, ng], [1, gsz]]),
             AX.X, ALU.max)
        P.op("vector", "tensor_tensor", eq[:, 0:width], x[:, 0:width],
             fap(m, 0, [[1, ng], [0, gsz]]), ALU.is_ge)
        P.op("vector", "scalar_tensor_tensor", cand[:, 0:width], eq[:, 0:width], -1e4,
             IDXB[:, 0:width], ALU.mult, ALU.add)
        P.op("vector", "tensor_reduce", m[:, 0:ng], fap(cand, 0, [[gsz, ng], [1, gsz]]),
             AX.X, ALU.min)
        tgt = sel if p == 0 else eq
        P.op("vector", "tensor_tensor", tgt[:, 0:width], IDX[:, 0:width],
             fap(m, 0, [[1, ng], [0, gsz]]), ALU.is_equal)
        if p == 0:
            P.op("vector", "scalar_tensor_tensor", x[:, 0:width], sel[:, 0:width], -1e30,
                 x[:, 0:width], ALU.mult, ALU.add)
        else:
            P.op("vector", "tensor_tensor", sel[:, 0:width], sel[:, 0:width],
                 eq[:, 0:width], ALU.add)


def emit_moe(ctx):
    P, X, PS, AR, dram, cc = ctx.P, ctx.X, ctx.PS, ctx.AR, ctx.dram, ctx.const_col
    E = 8

    AR.push()
    xn = [AR.alloc(f"moxn{i}", [128, L], BF16) for i in range(8)]
    lnw = emit_ln_xn(ctx, xn)

    IDX8 = AR.alloc("moidx8", [128, 128], F32)
    IDX8B = AR.alloc("moidx8b", [128, 128], F32)
    IDX4 = AR.alloc("moidx4", [128, 64], F32)
    IDX4B = AR.alloc("moidx4b", [128, 64], F32)
    gw = AR.alloc("mogw", [128, 64], BF16)
    biasb = AR.alloc("mobias", [128, 128], F32)
    ltm = AR.alloc("moltm", [128, 128], F32)
    scc = AR.alloc("moscc", [128, 128], F32)
    gsb = AR.alloc("mogsb", [128, 64], F32)
    gsel = AR.alloc("mogsel", [128, 64], F32)
    smask = AR.alloc("mosmask", [128, 128], F32)
    sel = AR.alloc("mosel", [128, 128], F32)
    eqb = AR.alloc("moeq", [128, 128], F32)
    cnd = AR.alloc("mocnd", [128, 128], F32)
    mrow = AR.alloc("momrow", [128, 32], F32)
    den = AR.alloc("moden", [128, 16], F32)
    cwt = AR.alloc("mocwt", [128, 128], F32)
    cwR = AR.alloc("mocwR", [128, L], F32)
    gpb = [AR.alloc(f"mogp{b}", [128, 4096], BF16) for b in range(2)]
    upb = [AR.alloc(f"moup{b}", [128, 4096], BF16) for b in range(2)]
    dnb = [AR.alloc(f"modn{b}", [128, 4096], BF16) for b in range(2)]
    hall = [AR.alloc(f"moh{e}", [128, 4 * CH], BF16) for e in range(E)]
    cwb = [AR.alloc(f"mocwb{b}", [128, CH], F32) for b in range(2)]
    sil = AR.alloc("mosil", [128, CH], F32)

    P.dma("gpsimd", gw[:, :], fap_d(dram["moe_gwT"], 0, [[8, 128], [8 * 128, 8], [1, 8]]),
          wait=lnw, inc="qW")
    wgw = P.cnt["qW"]
    P.dma("sync", biasb[:, :], fap_d(dram["moe_bias16"], 0, [[0, 128], [1, 128]]),
          wait=lnw, inc="qD")
    wbias = P.cnt["qD"]

    for tt in range(16):
        b = tt % 2
        w = P.bank_wait(b)
        if tt == 0:
            w = w + [("qW", wgw)]
        for k in range(8):
            P.op("tensor", "matmul", PS[b][0:128, 0:8], xn[k][:, tt * 128:(tt + 1) * 128],
                 gw[:, k * 8:(k + 1) * 8], start=(k == 0), stop=(k == 7),
                 wait=w if k == 0 else (), inc="sT" if k == 7 else None)
        mmv = P.cnt["sT"]
        sv = P.op("scalar", "activation", ltm[:, tt * 8:(tt + 1) * 8], PS[b][0:128, 0:8],
                  AF.Sigmoid, wait=[("sT", mmv)])
        P.bank_done(b, "sS", sv)
    sigv = P.cnt["sS"]

    for nm, t, wdt in [("idx8", IDX8, 128), ("idx8b", IDX8B, 128), ("idx4", IDX4, 64),
                       ("idx4b", IDX4B, 64)]:
        P.dma("sync", t[:, :], fap_d(dram[nm], 0, [[0, 128], [1, wdt]]), wait=lnw, inc="qD")
    widx = P.cnt["qD"]
    P.op("vector", "tensor_tensor", scc[:, :], ltm[:, :], biasb[:, :], ALU.add,
         wait=[("sS", sigv), ("qD", widx)])
    P.op("vector", "tensor_tensor", gsb[:, :], fap(scc, 0, [[2, 64]]),
         fap(scc, 1, [[2, 64]]), ALU.add)
    emit_top2_sel(P, gsb, gsel, 64, 4, 16, mrow, eqb, cnd, IDX4, IDX4B)
    P.op("vector", "tensor_copy", smask[:, :], fap(gsel, 0, [[4, 16], [1, 4], [0, 2]]))
    P.op("vector", "tensor_tensor", scc[:, :], scc[:, :], smask[:, :], ALU.mult)
    emit_top2_sel(P, scc, sel, 128, 8, 16, mrow, eqb, cnd, IDX8, IDX8B)
    P.op("vector", "tensor_tensor", cwt[:, :], ltm[:, :], sel[:, :], ALU.mult)
    P.op("vector", "tensor_reduce", den[:, 0:16], fap(cwt, 0, [[8, 16], [1, 8]]),
         AX.X, ALU.add)
    P.op("vector", "tensor_single_scalar", den[:, 0:16], den[:, 0:16], 1e-20, ALU.add)
    P.op("vector", "reciprocal", den[:, 0:16], den[:, 0:16])
    gatev = P.op("vector", "tensor_tensor", cwt[:, :], cwt[:, :],
                 fap(den, 0, [[1, 16], [0, 8]]), ALU.mult)

    for blk in range(4):
        w = P.bank_wait(2)
        for j in range(4):
            tt = blk * 4 + j
            P.op("tensor", "transpose", PS[2][0:8, j * 128:(j + 1) * 128],
                 cwt[:, tt * 8:(tt + 1) * 8], ctx.IDNF[:, :],
                 wait=(w + [("sV", gatev)]) if j == 0 else (),
                 inc="sT" if j == 3 else None)
        trv = P.cnt["sT"]
        sv = P.op("vector", "tensor_copy", cwR[0:8, blk * 512:(blk + 1) * 512],
                  PS[2][0:8, 0:512], wait=[("sT", trv)])
        P.bank_done(2, "sV", sv)
    cwrv = P.cnt["sV"]
    cwd = ctx.scratch_cw
    P.dma("sync", cwd.ap(), cwR[0:8, :], wait=[("sV", cwrv)], inc="qD")
    wcw = P.cnt["qD"]

    def load_gu(e, b, guard):
        t = dram["moe_gpT"]
        ap = fap_d(t, e * 1024 * 512, [[512, 128], [128 * 512, 8], [1, 512]])
        P.dma("gpsimd", gpb[b][:, :], ap, wait=guard, inc="qW")
        t = dram["moe_upT"]
        ap = fap_d(t, e * 1024 * 512, [[512, 128], [128 * 512, 8], [1, 512]])
        return P.dma("gpsimd", upb[b][:, :], ap, inc="qW")

    def load_dn(e, b, guard):
        t = dram["moe_dnT"]
        ap = fap_d(t, e * 512 * 1024, [[1024, 128], [128 * 1024, 4], [1, 1024]])
        return P.dma("gpsimd", dnb[b][:, :], ap, wait=guard, inc="qW")

    gufree = {0: [], 1: []}
    dnfree = {0: [], 1: []}
    cwfree = {0: [], 1: []}
    wv_gu = {0: load_gu(0, 0, [])}
    wv_dn = {}

    for c in range(NCH):
        cs = tok(c * CH)
        hv = {}
        for e in range(E):
            pb = e % 2
            nxt = e + 1 if e + 1 < E else (0 if c + 1 < NCH else None)
            if nxt is not None:
                nb = (e + 1) % 2
                wv_gu[nxt] = load_gu(nxt, nb, gufree[nb])
                gufree[nb] = []
            cwv = P.dma("sync", cwb[pb][:, :],
                        fap_d(cwd, e * L + c * CH, [[0, 128], [1, CH]]),
                        wait=[("qD", wcw)] + cwfree[pb], inc="qD")
            cwfree[pb] = []
            for m in range(4):
                bg = 4 + (m % 2)
                bu = 6 + (m % 2)
                w = P.bank_wait(bg)
                if m == 0:
                    w = w + [("qW", wv_gu[e])]
                for k in range(8):
                    P.op("tensor", "matmul", PS[bg][:, :],
                         gpb[pb][:, k * 512 + m * 128: k * 512 + (m + 1) * 128],
                         xn[k][:, c * CH:(c + 1) * CH], start=(k == 0), stop=(k == 7),
                         wait=w if k == 0 else ())
                w = P.bank_wait(bu)
                for k in range(8):
                    P.op("tensor", "matmul", PS[bu][:, :],
                         upb[pb][:, k * 512 + m * 128: k * 512 + (m + 1) * 128],
                         xn[k][:, c * CH:(c + 1) * CH], start=(k == 0), stop=(k == 7),
                         wait=w if k == 0 else (), inc="sT" if k == 7 else None)
                mmv = P.cnt["sT"]
                if m == 3:
                    gufree[pb] = [("sT", mmv)]
                sv_ = P.op("scalar", "activation", sil[:, :], PS[bg][:, :], AF.Silu,
                           wait=[("sT", mmv)])
                P.bank_done(bg, "sS", sv_)
                hv1 = P.op("vector", "tensor_tensor", hall[e][:, m * CH:(m + 1) * CH],
                           sil[:, :], PS[bu][:, :], ALU.mult,
                           wait=[("sS", sv_), ("qD", cwv)])
                P.bank_done(bu, "sV", hv1)
                hv[e] = P.op("vector", "tensor_tensor", hall[e][:, m * CH:(m + 1) * CH],
                             hall[e][:, m * CH:(m + 1) * CH], cwb[pb][:, :], ALU.mult)
            cwfree[pb] = [("sV", hv[e])]

        wv_dn[0] = load_dn(0, 0, dnfree[0])
        dnfree[0] = []
        for e in range(E):
            db = e % 2
            if e + 1 < E:
                wv_dn[e + 1] = load_dn(e + 1, 1 - db, dnfree[1 - db])
                dnfree[1 - db] = []
            for m in range(8):
                for k in range(4):
                    w = []
                    if e == 0 and k == 0:
                        w = P.bank_wait(m) + [("sV", hv[E - 1])]
                    if k == 0:
                        w = list(w) + [("qW", wv_dn[e])]
                    P.op("tensor", "matmul", PS[m][:, :],
                         dnb[db][:, k * 1024 + m * 128: k * 1024 + (m + 1) * 128],
                         hall[e][:, k * CH:(k + 1) * CH],
                         start=(e == 0 and k == 0), stop=(e == E - 1 and k == 3),
                         wait=w, inc="sT" if (m == 7 and k == 3) else None)
            dnfree[db] = [("sT", P.cnt["sT"])]
        dn_all = P.cnt["sT"]
        for m in range(8):
            sv = P.op("vector", "scalar_tensor_tensor", X[m][:, cs], PS[m][:, :], 0.5,
                      X[m][:, cs], ALU.mult, ALU.add, wait=[("sT", dn_all)] if m == 0 else ())
            P.bank_done(m, "sV", sv)
    AR.pop()


@stage("s1")
def s1(ctx):
    emit_moe(ctx)


@stage("s3")
def s3(ctx):
    emit_moe(ctx)








def emit_ln_rows(ctx, rtot, arow, gelem=None, double=True):
    P, X, PS, AR, ONES, cc = ctx.P, ctx.X, ctx.PS, ctx.AR, ctx.ONES, ctx.const_col
    AR.push()
    xb = [AR.alloc(f"lrxb{i}", [128, L], BF16) for i in range(8)]
    sq = [AR.alloc(f"lrsq{i}", [128, L], BF16) for i in range(8)]
    mu = AR.alloc("lrmu", [128, CH], F32)
    var = AR.alloc("lrvar", [128, CH], F32)
    t1 = AR.alloc("lrt1", [128, CH], F32)

    for i in range(8):
        eng = "vector" if i % 2 == 0 else "gpsimd"
        P.op(eng, "tensor_copy", xb[i][:, :], X[i][:, tok(0, L)])
        P.op("scalar", "activation", sq[i][:, :], X[i][:, tok(0, L)], AF.Square)
    vcast, gcast, scast = P.cnt["sV"], P.cnt["sG"], P.cnt["sS"]

    prev_v = 0
    prev_s = 0
    for c in range(NCH):
        cs = slice(c * CH, (c + 1) * CH)
        w0 = P.bank_wait(0) + ([("sV", vcast), ("sG", gcast)] if c == 0 else [("sV", prev_v)])
        for i in range(8):
            P.op("tensor", "matmul", PS[0][:, :], ONES[:, :], xb[i][:, cs],
                 start=(i == 0), stop=(i == 7), wait=w0 if i == 0 else ())
        w1 = P.bank_wait(1) + ([("sS", scast)] if c == 0 else [])
        for i in range(8):
            P.op("tensor", "matmul", PS[1][:, :], ONES[:, :], sq[i][:, cs],
                 start=(i == 0), stop=(i == 7), wait=w1 if i == 0 else (),
                 inc="sT" if i == 7 else None)
        mmv = P.cnt["sT"]
        P.op("vector", "tensor_scalar", mu[:, :], PS[0][:, :], 1.0 / D, None, ALU.mult,
             wait=[("sT", mmv), ("sS", prev_s)] if prev_s else [("sT", mmv)])
        P.op("vector", "tensor_scalar", var[:, :], PS[1][:, :], 1.0 / D, None, ALU.mult)
        prev_v = P.cnt["sV"]
        P.bank_done(0, "sV", prev_v)
        P.bank_done(1, "sV", prev_v)
        P.op("vector", "tensor_tensor", t1[:, :], mu[:, :], mu[:, :], ALU.mult)
        vmm = P.op("vector", "tensor_tensor", var[:, :], var[:, :], t1[:, :], ALU.subtract)
        P.op("scalar", "activation", t1[:, :], var[:, :], AF.Ln, bias=cc(EPS),
             wait=[("sV", vmm)])
        s1 = P.op("scalar", "activation", rtot[:, cs], t1[:, :], AF.Exp, scale=cc(-0.5))
        prev_s = s1
        if double:
            P.op("vector", "tensor_tensor", t1[:, :], rtot[:, cs], rtot[:, cs], ALU.mult,
                 wait=[("sS", s1)])
            v2 = P.op("vector", "tensor_tensor", t1[:, :], t1[:, :], var[:, :], ALU.mult)
            P.op("scalar", "activation", t1[:, :], t1[:, :], AF.Ln, bias=cc(EPS),
                 wait=[("sV", v2)])
            s2_ = P.op("scalar", "activation", t1[:, :], t1[:, :], AF.Exp, scale=cc(-0.5))
            prev_s = s2_
            P.op("vector", "tensor_tensor", rtot[:, cs], rtot[:, cs], t1[:, :],
                 ALU.mult, wait=[("sS", s2_)])
            P.op("vector", "tensor_tensor", arow[:, cs], mu[:, :], rtot[:, cs], ALU.mult)
        else:
            P.op("vector", "tensor_tensor", arow[:, cs], mu[:, :], rtot[:, cs], ALU.mult,
                 wait=[("sS", s1)])
        prev_v = P.cnt["sV"]
    AR.pop()
    return [("sV", P.cnt["sV"]), ("sS", P.cnt["sS"])]


def emit_l0attn(ctx):
    P, X, PS, AR, dram, cc = ctx.P, ctx.X, ctx.PS, ctx.AR, ctx.dram, ctx.const_col
    NL, NS, DI, DC, DTR = 2, 16, 1024, 512, 32
    MC = 256
    NMC = L // MC
    scr = ctx.scratch_bc

    AR.push()
    rtot = AR.alloc("martot", [128, L], F32)
    arow = AR.alloc("maarow", [128, L], F32)
    lnw = emit_ln_rows(ctx, rtot, arow, double=True)

    lng = AR.alloc("malng", [128, 8], F32)
    lnb = AR.alloc("malnb", [128, 8], F32)
    rms = AR.alloc("marms", [128, 8], F32)
    cwc = AR.alloc("macwc", [128, 64], F32)
    cbc = AR.alloc("macbc", [128, 16], F32)
    dtb = AR.alloc("madtb", [128, 16], F32)
    Alog = AR.alloc("maAlog", [128, 256], F32)
    Dpc = AR.alloc("maDpc", [128, 16], F32)
    pob = AR.alloc("mapob", [128, 8], F32)
    H = [AR.alloc(f"maH{g}", [128, L], F32) for g in range(4)]
    hstate = AR.alloc("mahst", [128, 128], F32)
    for nm, t in [("ma_lng", lng), ("ma_lnb", lnb), ("ma_rms", rms), ("ma_convw", cwc),
                  ("ma_convb", cbc), ("ma_dtb", dtb), ("ma_Alog", Alog), ("ma_Dp", Dpc),
                  ("ma_pob", pob)]:
        P.dma("sync", t[:, :], dram[nm].ap(), wait=lnw if nm == "ma_lng" else (), inc="qD")
    wcst = P.cnt["qD"]
    av = P.op("scalar", "activation", Alog[:, :], Alog[:, :], AF.Exp, wait=[("qD", wcst)])
    av = P.op("vector", "tensor_scalar", Alog[:, :], Alog[:, :], -1.0, None, ALU.mult,
              wait=[("sS", av)])

    for g in range(4):
        wt = [("sV", av)] if g == 0 else ()
        P.op("vector", "tensor_tensor", H[g][:, :], X[g][:, tok(0, L)], rtot[:, :],
             ALU.mult, wait=wt)
        P.op("vector", "tensor_tensor", H[g][:, :], H[g][:, :], arow[:, :], ALU.subtract)
        P.op("vector", "tensor_scalar", H[g][:, :], H[g][:, :], lng[:, g:g + 1],
             lnb[:, g:g + 1], ALU.mult, op1=ALU.add)
    hiv = [("sV", P.cnt["sV"])]

    WIN = AR.alloc("maWIN", [128, 4 * 2048], BF16)
    WXP = AR.alloc("maWXP", [128, 8 * 64], BF16)
    WDT = AR.alloc("maWDT", [128, 1024], BF16)
    WMO = AR.alloc("maWMO", [128, 8 * 512], BF16)

    for li in range(DBG.get('ma_layers', NL)):
        guard = [("sT", P.cnt["sT"])] if li > 0 else list(hiv)
        P.dma("gpsimd", WIN[:, :], fap_d(dram["ma_inWT"], li * 512 * 2048,
              [[2048, 128], [128 * 2048, 4], [1, 2048]]), wait=guard, inc="qW")
        P.dma("gpsimd", WXP[:, :], fap_d(dram["ma_xpWT"], li * 1024 * 64,
              [[64, 128], [128 * 64, 8], [1, 64]]), inc="qW")
        P.dma("gpsimd", WDT[0:32, :], fap_d(dram["ma_dtWT"], li * 32 * 1024,
              [[1024, 32], [1, 1024]]), inc="qW")
        wlv = P.dma("gpsimd", WMO[:, :], fap_d(dram["ma_moWT"], li * 1024 * 512,
                    [[512, 128], [128 * 512, 8], [1, 512]]), inc="qW")
        zs = P.op("vector", "memset", hstate[:, :], 0.0,
                  wait=[("sG", P.cnt["sG"])])

        scr_free = []
        bb_free = {0: [], 1: []}
        da_free = {r: [] for r in range(3)}
        sc_free = {r: [] for r in range(3)}
        db_free = {r: [] for r in range(3)}
        tv_free = {r: [] for r in range(3)}
        for mc in range(DBG.get('ma_chunks', NMC)):
            ts0 = mc * MC
            for e in ("vector", "gpsimd", "scalar"):
                P.wait(e, ("sT", P.cnt["sT"]), ("sV", P.cnt["sV"]), ("sG", P.cnt["sG"]))
            AR.push()
            hn = [AR.alloc(f"mahn{g}", [128, MC], BF16) for g in range(4)]
            rr = AR.alloc("marr", [128, MC], F32)
            XC = [AR.alloc(f"maXC{g}", [128, 4 + MC], F32) for g in range(8)]
            XTL = AR.alloc("maXTL", [128, 4 * 8], F32)
            XS = [AR.alloc(f"maXS{g}", [128, MC], F32) for g in range(2)]
            XSF = [AR.alloc(f"maXSF{g}", [128, MC], F32) for g in range(2)]
            XSB = [AR.alloc(f"maXSB{g}", [128, MC], BF16) for g in range(8)]
            SZ = [AR.alloc(f"maSZ{g}", [128, MC], BF16) for g in range(8)]
            DT_ = [AR.alloc(f"maDT{g}", [128, MC], F32) for g in range(8)]
            UU = [AR.alloc(f"maUU{g}", [128, MC], BF16) for g in range(8)]
            YG = [AR.alloc(f"maYG{g}", [128, MC], F32) for g in range(8)]
            YB = [AR.alloc(f"maYB{g}", [128, MC], BF16) for g in range(8)]
            dtt = AR.alloc("madtt", [32, MC], BF16)
            NR = 3
            DA = [AR.alloc(f"maDA{r}", [128, MC], F32) for r in range(NR)]
            DB = [AR.alloc(f"maDB{r}", [128, MC], F32) for r in range(NR)]
            SC = [AR.alloc(f"maSC{r}", [128, MC], F32) for r in range(NR)]
            TV = [AR.alloc(f"maTV{r}", [128, MC], F32) for r in range(NR)]
            Bb = [AR.alloc(f"maBb{r}", [128, MC], F32) for r in range(2)]
            Cb = [AR.alloc(f"maCb{r}", [128, MC], F32) for r in range(2)]
            tmp = AR.alloc("matmp", [128, MC], F32)

            for g in range(4):
                P.op("scalar", "activation", hn[g][:, :], H[g][:, ts0:ts0 + MC], AF.Square)
            sqv = P.cnt["sS"]
            w = P.bank_wait(0) + [("sS", sqv)]
            for g in range(4):
                P.op("tensor", "matmul", PS[0][:, 0:MC], ctx.ONES[:, :], hn[g][:, :],
                     start=(g == 0), stop=(g == 3), wait=w if g == 0 else (),
                     inc="sT" if g == 3 else None)
            mmv = P.cnt["sT"]
            P.op("scalar", "activation", rr[:, :], PS[0][:, 0:MC], AF.Ln, bias=cc(EPS),
                 scale=cc(1.0 / DC), wait=[("sT", mmv)])
            sv = P.op("scalar", "activation", rr[:, :], rr[:, :], AF.Exp, scale=cc(-0.5))
            P.bank_done(0, "sS", sv)
            for g in range(4):
                P.op("vector", "tensor_tensor", tmp[:, :], H[g][:, ts0:ts0 + MC],
                     rr[:, :], ALU.mult, wait=[("sS", sv)] if g == 0 else ())
                P.op("vector", "tensor_scalar", hn[g][:, :], tmp[:, :],
                     rms[:, li * 4 + g:li * 4 + g + 1], None, ALU.mult)
            hnv = [("sV", P.cnt["sV"])]

            for m in range(16):
                b = m % 4
                w = P.bank_wait(b)
                if m == 0:
                    w = w + [("qW", wlv)] + hnv
                for k in range(4):
                    P.op("tensor", "matmul", PS[b][:, 0:MC],
                         WIN[:, k * 2048 + m * 128:k * 2048 + (m + 1) * 128],
                         hn[k][:, :], start=(k == 0), stop=(k == 3),
                         wait=w if k == 0 else (), inc="sT" if k == 3 else None)
                mmv = P.cnt["sT"]
                if m < 8:
                    sv = P.op("scalar", "activation", XC[m][:, 4:4 + MC], PS[b][:, 0:MC],
                              AF.Copy, wait=[("sT", mmv)])
                else:
                    sv = P.op("scalar", "activation", SZ[m - 8][:, :], PS[b][:, 0:MC],
                              AF.Silu, wait=[("sT", mmv)])
                P.bank_done(b, "sS", sv)
            xzv = P.cnt["sS"]

            if mc == 0:
                tlv = P.op("vector", "memset", XTL[:, :], 0.0)
            for g in range(8):
                eng = "vector" if g % 2 == 0 else "gpsimd"
                P.op(eng, "tensor_copy", XC[g][:, 0:4], XTL[:, g * 4:(g + 1) * 4],
                     wait=[("sS", xzv)] if g < 2 else ())
            tcv = [("sV", P.cnt["sV"]), ("sG", P.cnt["sG"])]
            for g in range(8):
                eng = "vector" if g % 2 == 0 else "gpsimd"
                P.op(eng, "tensor_copy", XTL[:, g * 4:(g + 1) * 4], XC[g][:, MC:MC + 4])
            tsv = [("sV", P.cnt["sV"]), ("sG", P.cnt["sG"])]

            xs_free = {0: [], 1: []}
            xsf_free = {0: [], 1: []}
            for g in range(8):
                xr = g % 2
                wt = (tcv + tsv if g == 0 else []) + xs_free[xr]
                P.op("vector", "tensor_scalar", XS[xr][:, :], XC[g][:, 1:1 + MC],
                     cwc[:, li * 32 + 0 * 8 + g:li * 32 + 0 * 8 + g + 1], None, ALU.mult,
                     wait=wt)
                for kk2 in range(1, 4):
                    P.op("vector", "scalar_tensor_tensor", XS[xr][:, :],
                         XC[g][:, 1 + kk2:1 + kk2 + MC],
                         cwc[:, li * 32 + kk2 * 8 + g:li * 32 + kk2 * 8 + g + 1],
                         XS[xr][:, :], ALU.mult, ALU.add)
                cvg = P.cnt["sV"]
                sx = P.op("scalar", "activation", XSF[xr][:, :], XS[xr][:, :], AF.Silu,
                          bias=cbc[:, li * 8 + g:li * 8 + g + 1],
                          wait=[("sV", cvg)] + xsf_free[xr])
                xs_free[xr] = [("sS", sx)]
                sx2 = P.op("scalar", "activation", XSB[g][:, :], XSF[xr][:, :], AF.Copy)
                ygi = P.op("vector", "tensor_scalar", YG[g][:, :], XSF[xr][:, :],
                           Dpc[:, li * 8 + g:li * 8 + g + 1], None, ALU.mult,
                           wait=[("sS", sx2)])
                xsf_free[xr] = [("sV", ygi)]
            xsv = P.cnt["sS"]
            ygv = P.cnt["sV"]

            w = P.bank_wait(4) + [("sS", xsv)]
            for k in range(8):
                P.op("tensor", "matmul", PS[4][0:64, 0:MC], WXP[:, k * 64:(k + 1) * 64],
                     XSB[k][:, :], start=(k == 0), stop=(k == 7),
                     wait=w if k == 0 else (), inc="sT" if k == 7 else None)
            mmv = P.cnt["sT"]
            dte = P.op("vector", "tensor_copy", dtt[0:32, :], PS[4][0:32, 0:MC],
                       wait=[("sT", mmv)])
            bce = P.op("vector", "tensor_copy", tmp[32:64, :], PS[4][32:64, 0:MC])
            P.bank_done(4, "sV", bce)
            bcd = P.dma("sync", fap_d(scr, 0, [[MC, 32], [1, MC]]),
                        tmp[32:64, :], wait=[("sV", bce)] + scr_free, inc="qD")
            scr_free = []

            for mg in range(8):
                b = mg % 2
                w = P.bank_wait(b)
                P.op("tensor", "matmul", PS[b][:, 0:MC],
                     WDT[0:32, mg * 128:(mg + 1) * 128], dtt[0:32, :],
                     start=True, stop=True, wait=w + ([("sV", dte)] if mg == 0 else []),
                     inc="sT")
                mmv = P.cnt["sT"]
                sv = P.op("scalar", "activation", DT_[mg][:, :], PS[b][:, 0:MC], AF.Abs,
                          bias=dtb[:, li * 8 + mg:li * 8 + mg + 1], wait=[("sT", mmv)])
                sv = P.op("scalar", "activation", tmp[:, :], DT_[mg][:, :], AF.Exp,
                          scale=cc(-1.0),
                          wait=[("sV", P.cnt["sV"])] if mg > 0 else [("qD", bcd)])
                sv = P.op("scalar", "activation", tmp[:, :], tmp[:, :], AF.Ln, bias=cc(1.0))
                rv = P.op("vector", "tensor_scalar", DT_[mg][:, :], PS[b][:, 0:MC],
                          dtb[:, li * 8 + mg:li * 8 + mg + 1], 0.0, ALU.add, op1=ALU.max,
                          wait=[("sS", sv)])
                rv = P.op("vector", "tensor_tensor", DT_[mg][:, :], DT_[mg][:, :],
                          tmp[:, :], ALU.add)
                P.bank_done(b, "sV", rv)
                P.op("gpsimd", "tensor_tensor", UU[mg][:, :], DT_[mg][:, :], XSB[mg][:, :],
                     ALU.mult, wait=[("sV", rv), ("sS", xsv)] if mg == 0 else [("sV", rv)])
            duv = [("sV", P.cnt["sV"]), ("sG", P.cnt["sG"])]


            cvs = {}

            def load_bc(n):
                r2 = n % 2
                P.dma("sync", Bb[r2][:, :], fap_d(scr, n * MC, [[0, 128], [1, MC]]),
                      wait=([("qD", bcd)] if n == 0 else []) + bb_free[r2], inc="qD")
                cvs[n] = P.dma("sync", Cb[r2][:, :],
                               fap_d(scr, (16 + n) * MC, [[0, 128], [1, MC]]), inc="qD")

            load_bc(0)
            for n in range(NS):
                r = n % 2
                cv2 = cvs[n]
                if n + 1 < NS:
                    load_bc(n + 1)
                if n == NS - 1:
                    scr_free = [("qD", cv2)]
                for g in range(8):
                    rg = (n * 8 + g) % 3
                    col = g * 16 + n
                    da = P.op("scalar", "activation", DA[rg][:, :], DT_[g][:, :], AF.Exp,
                              scale=Alog[:, li * 128 + col:li * 128 + col + 1],
                              wait=da_free[rg] + (duv if n == 0 and g == 0 else []))
                    da_free[rg] = []
                    db = P.op("gpsimd", "tensor_tensor", DB[rg][:, :], UU[g][:, :],
                              Bb[r][:, :], ALU.mult,
                              wait=([("qD", cv2)] if g == 0 else []) + db_free[rg])
                    db_free[rg] = []
                    scv = P.op("vector", "tensor_tensor_scan", SC[rg][:, :], DA[rg][:, :],
                               DB[rg][:, :], hstate[:, col:col + 1], ALU.mult, ALU.add,
                               wait=[("sS", da), ("sG", db)] + sc_free[rg] +
                                    ([("sV", ygv)] if n == 0 and g == 0 else []))
                    da_free[rg] = [("sV", scv)]
                    hsv = P.op("gpsimd", "tensor_copy", hstate[:, col:col + 1],
                               SC[rg][:, MC - 1:MC], wait=[("sV", scv)])
                    P.op("vector", "tensor_tensor", TV[rg][:, :], SC[rg][:, :],
                         Cb[r][:, :], ALU.mult)
                    yv = P.op("vector", "tensor_tensor", YG[g][:, :], YG[g][:, :],
                              TV[rg][:, :], ALU.add)
                    db_free[rg] = []
                    tv_free[rg] = [("sV", yv)]
                    sc_free[rg] = [("sV", yv), ("sG", hsv)]
                    if g == 7:
                        bb_free[r] = [("sV", yv), ("sG", hsv)]
            scan_done = P.cnt["sV"]

            for g in range(8):
                eng = "vector" if g % 2 == 0 else "gpsimd"
                wt = [("sV", scan_done)] if g < 2 else ()
                P.op(eng, "tensor_tensor", YB[g][:, :], YG[g][:, :], SZ[g][:, :],
                     ALU.mult, wait=wt)
            ybv = [("sV", P.cnt["sV"]), ("sG", P.cnt["sG"])]

            for m in range(4):
                b = 4 + m % 2
                w = P.bank_wait(b)
                if m == 0:
                    w = w + ybv
                for k in range(8):
                    P.op("tensor", "matmul", PS[b][:, 0:MC],
                         WMO[:, k * 512 + m * 128:k * 512 + (m + 1) * 128],
                         YB[k][:, :], start=(k == 0), stop=(k == 7),
                         wait=w if k == 0 else (), inc="sT" if k == 7 else None)
                mmv = P.cnt["sT"]
                hv2 = P.op("vector", "tensor_tensor", H[m][:, ts0:ts0 + MC],
                           PS[b][:, 0:MC], H[m][:, ts0:ts0 + MC], ALU.add,
                           wait=[("sT", mmv)])
                P.bank_done(b, "sV", hv2)
            xc_tail_guard = []
            AR.pop()

    if DBG.get("ma_skip_final"):
        AR.pop()
        return
    WPO = AR.alloc("maWPO", [128, 8 * 1024], BF16)
    CAT = [AR.alloc(f"maCAT{g}", [128, L], BF16) for g in range(8)]
    stmp = AR.alloc("mastmp", [128, CH], F32)
    wpo = P.dma("gpsimd", WPO[:, :], fap_d(dram["ma_poWT"], 0,
                [[1024, 128], [128 * 1024, 8], [1, 1024]]),
                wait=[("sT", P.cnt["sT"]), ("sV", P.cnt["sV"])], inc="qW")
    hdone = P.cnt["sV"]
    for g in range(8):
        wt = [("sV", hdone)] if g == 0 else ()
        P.op("vector", "tensor_tensor", CAT[g][:, :], X[g][:, tok(0, L)], rtot[:, :],
             ALU.mult, wait=wt)
        P.op("vector", "tensor_tensor", CAT[g][:, :], CAT[g][:, :], arow[:, :], ALU.subtract)
        P.op("vector", "tensor_scalar", CAT[g][:, :], CAT[g][:, :], lng[:, g:g + 1],
             lnb[:, g:g + 1], ALU.mult, op1=ALU.add)
        if g >= 4:
            P.op("vector", "tensor_tensor", CAT[g][:, :], CAT[g][:, :], H[g - 4][:, :],
                 ALU.add)
    catv = [("sV", P.cnt["sV"])]

    for ch in range(NCH):
        cs = tok(ch * CH)
        for m in range(8):
            w = P.bank_wait(m)
            if ch == 0 and m == 0:
                w = w + [("qW", wpo)] + catv
            for k in range(8):
                P.op("tensor", "matmul", PS[m][:, :],
                     WPO[:, k * 1024 + m * 128:k * 1024 + (m + 1) * 128],
                     CAT[k][:, ch * CH:(ch + 1) * CH], start=(k == 0), stop=(k == 7),
                     wait=w if k == 0 else (), inc="sT" if k == 7 else None)
            mmv = P.cnt["sT"]
            c0 = ch * CH
            P.op("vector", "tensor_tensor", stmp[:, :], X[m][:, cs], rtot[:, c0:c0 + CH],
                 ALU.mult, wait=[("sT", mmv)])
            P.op("vector", "tensor_tensor", stmp[:, :], stmp[:, :], arow[:, c0:c0 + CH],
                 ALU.subtract)
            P.op("vector", "tensor_scalar", stmp[:, :], stmp[:, :], lng[:, m:m + 1],
                 lnb[:, m:m + 1], ALU.mult, op1=ALU.add)
            P.op("vector", "tensor_tensor", stmp[:, :], stmp[:, :], X[m][:, cs], ALU.add)
            sv = P.op("vector", "scalar_tensor_tensor", X[m][:, cs], PS[m][:, :],
                      pob[:, m:m + 1], stmp[:, :], ALU.add, ALU.add)
            P.bank_done(m, "sV", sv)
    AR.pop()


@stage("s2")
def s2(ctx):
    emit_l0attn(ctx)



def emit_l1ffn(ctx, scale=0.5):
    P, X, PS, AR, dram, cc = ctx.P, ctx.X, ctx.PS, ctx.AR, ctx.dram, ctx.const_col
    DEPTH = DBG.get("lf_depth", 6)
    SKIP_FFN = DBG.get("lf_skip_ffn", False)
    SKIP_ATTN = DBG.get("lf_skip_attn", False)
    HID = 2816
    NJB = 6
    JBW = [512, 512, 512, 512, 512, 256]
    NKS = HID // 128

    xs = ctx.scratch_x
    for i in range(8):
        P.dma("sync", fap_d(xs, i * 128 * L, [[L, 128], [1, L]]), X[i][:, tok(0, L)],
              inc="qD")
    xsv = P.cnt["qD"]
    for e in ("vector", "gpsimd", "scalar"):
        P.wait(e, ("qD", xsv))
    lnw = emit_ln_xn(ctx, None)

    AR.push()
    lng = AR.alloc("lflng", [128, 48], F32)
    lnb = AR.alloc("lflnb", [128, 48], F32)
    b1c = AR.alloc("lfb1", [128, 264], F32)
    b2c = AR.alloc("lfb2", [128, 48], F32)
    qsc = AR.alloc("lfqs", [128, 6], F32)
    ksc = AR.alloc("lfks", [128, 6], F32)
    for nm, t in [("lf_lng", lng), ("lf_lnb", lnb), ("lf_b1", b1c),
                  ("lf_b2", b2c), ("lf_qs", qsc), ("lf_ks", ksc)]:
        P.dma("sync", t[:, :], dram[nm].ap(), wait=lnw if nm == "lf_lng" else (), inc="qD")
    cstv = P.cnt["qD"]

    prev_done = [("qD", cstv)]
    for li in range(DEPTH):
        AR.push()
        ysb = [AR.alloc(f"lfys{i}", [128, L], BF16) for i in range(8)]
        cq = AR.alloc("lfcq", [128, L], BF16)
        sq_t = AR.alloc("lfsqt", [128, L], BF16)
        ck = AR.alloc("lfck", [128, L], BF16)
        sk = AR.alloc("lfsk", [128, L], BF16)
        ROT = AR.alloc("lfrot", [128, 128], BF16)
        HON = AR.alloc("lfhon", [128, 128], BF16)
        M384 = AR.alloc("lfm384", [128, 384], F32)
        cguard = list(prev_done) if li > 0 else [("sT", P.cnt["sT"])]
        for nm, t in [("rot_cq", cq), ("rot_sqp", sq_t), ("rot_ck", ck), ("rot_skp", sk),
                      ("rotmat", ROT), ("hones2", HON), ("m384f", M384)]:
            P.dma("sync", t[:, :], dram[nm].ap(),
                  wait=cguard if nm == "rot_cq" else (), inc="qD")
        cstv2 = P.cnt["qD"]
        for i in range(8):
            eng = "vector" if i % 2 == 0 else "gpsimd"
            wt = []
            if i < 2:
                wt = list(prev_done) + [("qD", cstv2)]
                if li == 0:
                    wt += lnw
            sh = GUARD - (1 if i >= 4 else 0)
            P.op(eng, "tensor_copy", ysb[i][:, :], X[i][:, sh:sh + L], wait=wt)
        ysv = [("sV", P.cnt["sV"]), ("sG", P.cnt["sG"])]

        for half in range(2 if not SKIP_ATTN else 0):
            AR.push()
            W0 = AR.alloc("lfW0", [128, 4096], BF16)
            W1_ = AR.alloc("lfW1x", [128, 4096], BF16)
            qt_ = [AR.alloc(f"lfq{m}", [128, L], BF16) for m in range(4)]
            kt_ = [AR.alloc(f"lfk{m}", [128, L], BF16) for m in range(4)]
            vt = [AR.alloc(f"lfv{t}", [128, 512], BF16) for t in range(16)]
            osb = [AR.alloc(f"lfo{m}", [128, L], BF16) for m in range(4)]
            mark = AR.off
            qq = AR.alloc("lfqq", [128, L], BF16)
            rq = AR.alloc("lfrq", [128, 512], F32)
            qq_end = AR.off
            AR.off = mark
            lgs = [AR.alloc(f"lflg{r}", [128, 384], F32) for r in range(3)]
            PTs = [AR.alloc(f"lfPT{r}", [128, 384], BF16) for r in range(3)]
            rws = [AR.alloc(f"lfrw{r}", [128, 2], F32) for r in range(3)]
            lg = lgs[0]
            AR.off = max(AR.off, qq_end)
            sqb = vt

            base = li * 1024 * 3072 + half * 512
            guard = [("sT", P.cnt["sT"])]
            wvq = P.dma("gpsimd", W0[:, :], fap_d(dram["lf_wqkv"], base,
                        [[3072, 128], [128 * 3072, 8], [1, 512]]), wait=guard, inc="qW")
            wvk = P.dma("gpsimd", W1_[:, :], fap_d(dram["lf_wqkv"], base + 1024,
                        [[3072, 128], [128 * 3072, 8], [1, 512]]), inc="qW")
            WQ, WK, WV, WO = W0, W1_, W0, W1_

            def qk_pipeline(Wbuf, wv, out_tiles, rtab_c, rtab_s, sc_col, hon_guard):
                for m in range(4):
                    qq_guard = [("sG", P.cnt["sG"])]
                    for ch in range(NCH):
                        b = (m * NCH + ch) % 4
                        w = P.bank_wait(b)
                        if m == 0 and ch == 0:
                            w = w + [("qW", wv)] + ysv
                        for k in range(8):
                            P.op("tensor", "matmul", PS[b][:, :],
                                 Wbuf[:, k * 512 + m * 128: k * 512 + (m + 1) * 128],
                                 ysb[k][:, ch * CH:(ch + 1) * CH], start=(k == 0),
                                 stop=(k == 7), wait=w if k == 0 else (),
                                 inc="sT" if k == 7 else None)
                        mmv = P.cnt["sT"]
                        sv = P.op("scalar", "activation", qq[:, ch * CH:(ch + 1) * CH],
                                  PS[b][:, :], AF.Copy,
                                  wait=[("sT", mmv)] + (qq_guard if ch == 0 else []))
                        P.bank_done(b, "sS", sv)
                    for ch in range(NCH):
                        P.op("scalar", "activation", sqb[ch][:, :],
                             qq[:, ch * CH:(ch + 1) * CH], AF.Square,
                             wait=[("sT", hon_guard[0])] if ch == 0 else ())
                    sqv = P.cnt["sS"]
                    for ch in range(NCH):
                        b2_ = 4 + (ch % 2)
                        w = P.bank_wait(b2_)
                        P.op("tensor", "matmul", PS[b2_][:, :], HON[:, :],
                             sqb[ch][:, :], start=True, stop=True,
                             wait=w + [("sS", sqv)], inc="sT")
                        hon_guard[0] = P.cnt["sT"]
                        mmv = P.cnt["sT"]
                        P.op("scalar", "activation", rq[:, :], PS[b2_][:, :], AF.Ln,
                             bias=cc(1e-30),
                             wait=[("sT", mmv), ("sV", P.cnt["sV"])])
                        sv = P.op("scalar", "activation", rq[:, :], rq[:, :], AF.Exp,
                                  scale=cc(-0.5))
                        P.bank_done(b2_, "sS", sv)
                        P.op("vector", "tensor_scalar", rq[:, :], rq[:, :], 1e12, sc_col,
                             ALU.min, op1=ALU.mult, wait=[("sS", sv)])
                        P.op("vector", "tensor_tensor", qq[:, ch * CH:(ch + 1) * CH],
                             qq[:, ch * CH:(ch + 1) * CH], rq[:, :], ALU.mult)
                        qrv = P.op("vector", "tensor_tensor",
                                   out_tiles[m][:, ch * CH:(ch + 1) * CH],
                                   qq[:, ch * CH:(ch + 1) * CH],
                                   rtab_s[:, ch * CH:(ch + 1) * CH], ALU.mult)
                        b3 = 6 + (ch % 2)
                        w = P.bank_wait(b3)
                        P.op("tensor", "matmul", PS[b3][:, :], ROT[:, :],
                             out_tiles[m][:, ch * CH:(ch + 1) * CH], start=True,
                             stop=True, wait=w + [("sV", qrv)], inc="sT")
                        mmv = P.cnt["sT"]
                        gv1 = P.op("gpsimd", "tensor_tensor", qq[:, ch * CH:(ch + 1) * CH],
                                   qq[:, ch * CH:(ch + 1) * CH],
                                   rtab_c[:, ch * CH:(ch + 1) * CH], ALU.mult,
                                   wait=[("sV", qrv)])
                        vv1 = P.op("vector", "tensor_tensor",
                                   out_tiles[m][:, ch * CH:(ch + 1) * CH],
                                   PS[b3][:, :], qq[:, ch * CH:(ch + 1) * CH], ALU.add,
                                   wait=[("sT", mmv), ("sG", gv1)])
                        P.bank_done(b3, "sV", vv1)

            hon_guard = [0]
            qk_pipeline(WQ, wvq, qt_, cq, sq_t, qsc[:, li:li + 1], hon_guard)
            wvv = P.dma("gpsimd", WV[:, :], fap_d(dram["lf_wqkv"], base + 2048,
                        [[3072, 128], [128 * 3072, 8], [1, 512]]),
                        wait=[("sT", P.cnt["sT"])], inc="qW")
            qk_pipeline(WK, wvk, kt_, ck, sk, ksc[:, li:li + 1], hon_guard)
            wvo = P.dma("gpsimd", WO[:, :], fap_d(dram["lf_wo"],
                        li * 1024 * 1024 + half * 512 * 1024,
                        [[1024, 128], [128 * 1024, 4], [1, 1024]]),
                        wait=[("sT", P.cnt["sT"])], inc="qW")

            for tt in range(16):
                b = tt % 4
                w = P.bank_wait(b)
                if tt == 0:
                    w = w + [("qW", wvv)]
                for k in range(8):
                    P.op("tensor", "matmul", PS[b][:, :],
                         ysb[k][:, tt * 128:(tt + 1) * 128],
                         WV[:, k * 512:(k + 1) * 512], start=(k == 0), stop=(k == 7),
                         wait=w if k == 0 else (), inc="sT" if k == 7 else None)
                mmv = P.cnt["sT"]
                sv = P.op("scalar", "activation", vt[tt][:, :], PS[b][:, :], AF.Copy,
                          wait=[("sT", mmv)])
                P.bank_done(b, "sS", sv)

            QKS = 8.0
            NROT = 3
            qk_done = [("sG", P.cnt["sG"]), ("sV", P.cnt["sV"]), ("sT", P.cnt["sT"])]
            lg_free = {r: list(qk_done) for r in range(NROT)}
            pt_free = {r: list(qk_done) for r in range(NROT)}
            rw_free = {r: list(qk_done) for r in range(NROT)}
            QKB = [0, 1, 5]
            insts = [(hh, qti) for hh in range(8) for qti in range(16)]
            mmq = {}

            def emit_qk(i):
                hh, qti = insts[i]
                ft = hh // 2
                pb = (hh % 2) * 64
                ctx0 = max(0, qti * 128 - 256)
                cw_ = qti * 128 + 128 - ctx0
                bq = QKB[i % 3]
                w = P.bank_wait(bq)
                P.op("tensor", "matmul", PS[bq][:, 0:cw_],
                     qt_[ft][pb:pb + 64, qti * 128:(qti + 1) * 128],
                     kt_[ft][pb:pb + 64, ctx0:ctx0 + cw_],
                     start=True, stop=True, wait=w, inc="sT")
                mmq[i] = P.cnt["sT"]

            emit_qk(0)
            emit_qk(1)
            for i, (hh, qti) in enumerate(insts):
                ft = hh // 2
                pb = (hh % 2) * 64
                ctx0 = max(0, qti * 128 - 256)
                cw_ = qti * 128 + 128 - ctx0
                nct = cw_ // 128
                r = i % NROT
                bq = QKB[i % 3]
                lgr, ptr, rwr = lgs[r], PTs[r], rws[r]
                sv = P.op("vector", "scalar_tensor_tensor", lgr[:, 0:cw_],
                          PS[bq][:, 0:cw_], QKS, M384[:, 384 - cw_:384],
                          ALU.mult, ALU.add, wait=[("sT", mmq[i])] + lg_free[r])
                lg_free[r] = []
                P.bank_done(bq, "sV", sv)
                P.op("vector", "tensor_reduce", rwr[:, 0:1], lgr[:, 0:cw_],
                     AX.X, ALU.max, wait=rw_free[r])
                rw_free[r] = []
                nm = P.op("vector", "tensor_scalar", rwr[:, 1:2], rwr[:, 0:1],
                          -1.0, None, ALU.mult)
                se = P.op("scalar", "activation", lgr[:, 0:cw_], lgr[:, 0:cw_],
                          AF.Exp, bias=rwr[:, 1:2], accum_out=rwr[:, 0:1],
                          wait=[("sV", nm)])
                P.op("vector", "reciprocal", rwr[:, 1:2], rwr[:, 0:1], wait=[("sS", se)])
                nv = P.op("vector", "tensor_scalar", lgr[:, 0:cw_], lgr[:, 0:cw_],
                          rwr[:, 1:2], None, ALU.mult)
                rw_free[r] = [("sV", nv)]
                if i + 2 < len(insts):
                    emit_qk(i + 2)
                b3 = 2 + (i % 2)
                w = P.bank_wait(b3)
                for t2 in range(nct):
                    P.op("tensor", "matmul", PS[b3][:, t2 * 128:(t2 + 1) * 128],
                         lgr[:, t2 * 128:(t2 + 1) * 128], ctx.IDNF[:, :],
                         is_transpose=True, start=(t2 == 0), stop=(t2 == nct - 1),
                         wait=(w + [("sV", nv)] + pt_free[r]) if t2 == 0 else (),
                         inc="sT" if t2 == nct - 1 else None)
                pt_free[r] = []
                trv = P.cnt["sT"]
                ptv = P.op("vector", "tensor_copy", ptr[:, 0:cw_], PS[b3][:, 0:cw_],
                           wait=[("sT", trv)])
                P.bank_done(b3, "sV", ptv)
                lg_free[r] = [("sT", P.cnt["sT"])]
                w4 = P.bank_wait(4)
                for t2 in range(nct):
                    P.op("tensor", "matmul", PS[4][pb:pb + 64, 0:128],
                         vt[ctx0 // 128 + t2][:, hh * 64:(hh + 1) * 64],
                         ptr[:, t2 * 128:(t2 + 1) * 128],
                         start=(t2 == 0), stop=(t2 == nct - 1),
                         tile_position=(0, pb),
                         wait=(w4 + [("sV", ptv)]) if t2 == 0 else (),
                         inc="sT" if t2 == nct - 1 else None)
                mmv = P.cnt["sT"]
                pt_free[r] = [("sT", mmv)]
                sv = P.op("scalar", "activation",
                          osb[ft][pb:pb + 64, qti * 128:(qti + 1) * 128],
                          PS[4][pb:pb + 64, 0:128], AF.Copy, wait=[("sT", mmv)])
                P.bank_done(4, "sS", sv)

            ov = P.cnt["sS"]
            for ch in range(NCH):
                cs = tok(ch * CH)
                for m in range(8):
                    w = P.bank_wait(m)
                    if ch == 0 and m == 0:
                        w = w + [("qW", wvo), ("sS", ov)]
                    for k in range(4):
                        P.op("tensor", "matmul", PS[m][:, :],
                             WO[:, k * 1024 + m * 128: k * 1024 + (m + 1) * 128],
                             osb[k][:, ch * CH:(ch + 1) * CH], start=(k == 0),
                             stop=(k == 3), wait=w if k == 0 else (),
                             inc="sT" if k == 3 else None)
                    mmv = P.cnt["sT"]
                    sv = P.op("vector", "tensor_tensor", X[m][:, cs], PS[m][:, :],
                              X[m][:, cs], ALU.add, wait=[("sT", mmv)])
                    P.bank_done(m, "sV", sv)
            AR.pop()
        AR.pop()
        attn_done = P.cnt["sV"]

        if SKIP_FFN:
            break
        for e in ("vector", "gpsimd", "scalar"):
            P.wait(e, ("sV", attn_done))
        gb = [(lng[:, li * 8 + i:li * 8 + i + 1], lnb[:, li * 8 + i:li * 8 + i + 1])
              for i in range(8)]
        for th in range(2):
            t0 = th * 1024
            for e in ("vector", "gpsimd", "scalar"):
                P.wait(e, ("sV", P.cnt["sV"]))
            AR.push()
            xn2 = [AR.alloc(f"lfxn{i}", [128, 1024], BF16) for i in range(8)]
            lnw2 = emit_ln_xn(ctx, xn2, gamma_beta=gb, tok0=t0, ntok=1024,
                              shifts=[0, 0, 0, 0, -1, -1, -1, -1])
            if DBG.get("lf_dump_xn2"):
                for i in range(8):
                    P.op("vector", "tensor_copy", X[i][:, tok(t0, 1024)], xn2[i][:, :],
                         wait=lnw2 if i == 0 else ())
                return
            hbuf = [AR.alloc(f"lfh{k}", [128, 1024], BF16) for k in range(NKS)]
            W1A = AR.alloc("lfW1A", [128, 4096], BF16)
            W1G = AR.alloc("lfW1G", [128, 4096], BF16)
            W2F = AR.alloc("lfW2F", [128, NKS * 512], BF16)
            gel = [AR.alloc("lfgel0", [128, 512], F32), AR.alloc("lfgel1", [128, 512], F32)]
            gel_free = {0: [], 1: []}
            gi = 0

            w1base = li * 1024 * 5632

            def load_w1(jb, guard):
                jw = JBW[jb]
                a1 = fap_d(dram["lf_w1"], w1base + jb * 512,
                           [[5632, 128], [128 * 5632, 8], [1, jw]])
                P.dma("gpsimd", W1A[:, 0:8 * jw], a1, wait=guard, inc="qW")
                a2 = fap_d(dram["lf_w1"], w1base + 2816 + jb * 512,
                           [[5632, 128], [128 * 5632, 8], [1, jw]])
                return P.dma("gpsimd", W1G[:, 0:8 * jw], a2, inc="qW")

            def load_w2(mh, guard):
                return P.dma("gpsimd", W2F[:, :],
                             fap_d(dram["lf_w2"], li * 2816 * 1024 + mh * 512,
                                   [[1024, 128], [128 * 1024, NKS], [1, 512]]),
                             wait=guard, inc="qW")

            wv1 = load_w1(0, lnw2)
            ks0 = 0
            for jb in range(NJB):
                jw = JBW[jb]
                nmt = jw // 128
                for sub in range(2):
                    ss = slice(sub * 512, (sub + 1) * 512)
                    for mt in range(nmt):
                        ba = mt % 4
                        bgt = 4 + mt % 4
                        w = P.bank_wait(ba)
                        if sub == 0 and mt == 0:
                            w = w + [("qW", wv1)]
                        for k in range(8):
                            P.op("tensor", "matmul", PS[ba][:, :],
                                 W1A[:, k * jw + mt * 128: k * jw + (mt + 1) * 128],
                                 xn2[k][:, ss], start=(k == 0), stop=(k == 7),
                                 wait=w if k == 0 else ())
                        w = P.bank_wait(bgt)
                        for k in range(8):
                            P.op("tensor", "matmul", PS[bgt][:, :],
                                 W1G[:, k * jw + mt * 128: k * jw + (mt + 1) * 128],
                                 xn2[k][:, ss], start=(k == 0), stop=(k == 7),
                                 wait=w if k == 0 else (), inc="sT" if k == 7 else None)
                        mmv = P.cnt["sT"]
                        kst = ks0 + mt
                        gr = gi % 2
                        gi += 1
                        sv = P.op("scalar", "activation", gel[gr][:, :], PS[bgt][:, :],
                                  AF.Gelu,
                                  bias=b1c[:, li * 44 + 22 + kst:li * 44 + 23 + kst],
                                  wait=[("sT", mmv)] + gel_free[gr])
                        P.bank_done(bgt, "sS", sv)
                        hv = P.op("vector", "scalar_tensor_tensor", hbuf[kst][:, ss],
                                  PS[ba][:, :], b1c[:, li * 44 + kst:li * 44 + 1 + kst],
                                  gel[gr][:, :], ALU.add, ALU.mult, wait=[("sS", sv)])
                        gel_free[gr] = [("sV", hv)]
                        P.bank_done(ba, "sV", hv)
                if jb + 1 < NJB:
                    wv1 = load_w1(jb + 1, [("sT", P.cnt["sT"])])
                ks0 += nmt
            h_all = P.cnt["sV"]

            if DBG.get("lf_dump_h"):
                for i in range(8):
                    P.op("vector", "tensor_copy", X[i][:, tok(t0, 1024)],
                         hbuf[i + DBG.get("lf_dump_h_off", 0)][:, :],
                         wait=[("sV", h_all)] if i == 0 else ())
                return
            wv2 = load_w2(0, [("sV", h_all)] if th == 0 and li == 0 else
                          [("sT", P.cnt["sT"])])
            for mh in range(2):
                for k in range(NKS):
                    for sub in range(2):
                        for m4 in range(4):
                            bb = sub * 4 + m4
                            w = []
                            if k == 0:
                                w = P.bank_wait(bb)
                                if sub == 0 and m4 == 0:
                                    w = w + [("sV", h_all), ("qW", wv2)]
                            P.op("tensor", "matmul", PS[bb][:, :],
                                 W2F[:, k * 512 + m4 * 128:k * 512 + (m4 + 1) * 128],
                                 hbuf[k][:, sub * 512:(sub + 1) * 512],
                                 start=(k == 0), stop=(k == NKS - 1),
                                 wait=w, inc="sT" if (sub == 1 and m4 == 3) else None)
                mm_all = P.cnt["sT"]
                if mh == 0:
                    wv2 = load_w2(1, [("sT", mm_all)])
                for sub in range(2):
                    for m4 in range(4):
                        bb = sub * 4 + m4
                        m = mh * 4 + m4
                        cs = tok(t0 + sub * 512)
                        sv = P.op("vector", "scalar_tensor_tensor", X[m][:, cs],
                                  PS[bb][:, :], b2c[:, li * 8 + m:li * 8 + m + 1],
                                  X[m][:, cs], ALU.add, ALU.add,
                                  wait=[("sT", mm_all)] if (sub == 0 and m4 == 0) else ())
                        P.bank_done(bb, "sV", sv)
            AR.pop()
        prev_done = [("sV", P.cnt["sV"])]

    tmpx = AR.alloc("lftmpx", [128, L], F32)
    for m in range(8):
        wv = P.dma("sync", tmpx[:, :], fap_d(xs, m * 128 * L, [[L, 128], [1, L]]),
                   wait=[("sV", P.cnt["sV"])], inc="qD")
        P.op("vector", "scalar_tensor_tensor", X[m][:, tok(0, L)], X[m][:, tok(0, L)],
             scale, tmpx[:, :], ALU.mult, ALU.add, wait=[("qD", wv)])
    AR.pop()


@stage("s4")
def s4(ctx):
    emit_l1ffn(ctx)


@stage("s6")
def s6(ctx):
    emit_l1ffn(ctx)



def emit_l1attn(ctx):
    P, X, PS, AR, dram, cc = ctx.P, ctx.X, ctx.PS, ctx.AR, ctx.dram, ctx.const_col
    AR.push()
    xn = [AR.alloc(f"laxn{i}", [128, L], BF16) for i in range(8)]
    lnw = emit_ln_xn(ctx, xn)

    bo = AR.alloc("labo", [128, 8], F32)
    dtri = AR.alloc("ladtri", [128, 128], F32)
    P.dma("sync", bo[:, :], dram["att_bo"].ap(), wait=lnw, inc="qD")
    P.dma("sync", dtri[:, :], dram["dtri"].ap(), inc="qD")
    wcst = P.cnt["qD"]

    GW = 2824
    for half in range(2):
        AR.push()
        q = [AR.alloc(f"laq{m}", [128, L], BF16) for m in range(4)]
        kk = [AR.alloc(f"lak{m}", [128, L], BF16) for m in range(4)]
        vt = [AR.alloc(f"lav{t}", [128, 512], BF16) for t in range(16)]
        WA = AR.alloc("laWA", [128, 4096], BF16)
        WB = AR.alloc("laWB", [128, 4096], BF16)
        osb = [AR.alloc(f"lao{m}", [128, L], BF16) for m in range(4)]
        G = AR.alloc("laG", [128, GW], F32)
        lg = AR.alloc("lalg", [128, L], F32)
        PT = AR.alloc("laPT", [128, L], BF16)
        rowm = AR.alloc("larow", [128, 2], F32)

        def load_whalf(name, buf, guard):
            t = dram[name]
            ap = fap_d(t, half * 512, [[1024, 128], [128 * 1024, 8], [1, 512]])
            return P.dma("gpsimd", buf[:, :], ap, wait=guard, inc="qW")

        def load_wo_half(buf, guard):
            t = dram["att_wo"]
            ap = fap_d(t, half * 512 * 1024, [[1024, 128], [128 * 1024, 4], [1, 1024]])
            return P.dma("gpsimd", buf[:, :], ap, wait=guard, inc="qW")

        hguard = lnw if half == 0 else [("sT", P.cnt["sT"])]
        wva = load_whalf("att_wq", WA, hguard)
        wvb = load_whalf("att_wk", WB, [])

        def proj(buf, wv, out_tiles, sem_after):
            for ch in range(NCH):
                for m in range(4):
                    b = (ch * 4 + m) % 4
                    w = P.bank_wait(b)
                    if ch == 0 and m == 0:
                        w = w + [("qW", wv)]
                    for k in range(8):
                        P.op("tensor", "matmul", PS[b][:, :],
                             buf[:, k * 512 + m * 128: k * 512 + (m + 1) * 128],
                             xn[k][:, ch * CH:(ch + 1) * CH], start=(k == 0), stop=(k == 7),
                             wait=w if k == 0 else (), inc="sT" if k == 7 else None)
                    mmv = P.cnt["sT"]
                    sv = P.op("scalar", "activation", out_tiles[m][:, ch * CH:(ch + 1) * CH],
                              PS[b][:, :], AF.Copy, wait=[("sT", mmv)])
                    P.bank_done(b, "sS", sv)
            return P.cnt["sT"]

        qmm = proj(WA, wva, q, None)
        wva2 = load_whalf("att_wv", WA, [("sT", qmm)])
        kmm = proj(WB, wvb, kk, None)
        wvb2 = load_wo_half(WB, [("sT", kmm)])

        for tt in range(16):
            b = tt % 4
            w = P.bank_wait(b)
            if tt == 0:
                w = w + [("qW", wva2)]
            for k in range(8):
                P.op("tensor", "matmul", PS[b][:, :],
                     xn[k][:, tt * 128:(tt + 1) * 128],
                     WA[:, k * 512:(k + 1) * 512], start=(k == 0), stop=(k == 7),
                     wait=w if k == 0 else (), inc="sT" if k == 7 else None)
            mmv = P.cnt["sT"]
            sv = P.op("scalar", "activation", vt[tt][:, :], PS[b][:, :], AF.Copy,
                      wait=[("sT", mmv)])
            P.bank_done(b, "sS", sv)

        g_guard = [("qD", wcst)] if half == 0 else [("sV", P.cnt["sV"])]
        for hh in range(8):
            h = half * 8 + hh
            gv = P.dma("sync", G[:, :], fap_d(dram["att_T"], h * 2952, [[1, 128], [1, GW]]),
                       wait=g_guard, inc="qD")
            ft = hh // 2
            pb = (hh % 2) * 64
            for qt in range(16):
                kw = (qt + 1) * 128
                nkc = (kw + 511) // 512
                for kc in range(nkc):
                    ks = min(512, kw - kc * 512)
                    b = kc % 2
                    w = P.bank_wait(b)
                    P.op("tensor", "matmul", PS[b][:, 0:ks],
                         q[ft][pb:pb + 64, qt * 128:(qt + 1) * 128],
                         kk[ft][pb:pb + 64, kc * 512:kc * 512 + ks],
                         start=True, stop=True, wait=w + [("qD", gv)] if kc == 0 else w,
                         inc="sT")
                    mmv = P.cnt["sT"]
                    base = qt * 128 - kc * 512 + 896
                    sv = P.op("vector", "tensor_tensor", lg[:, kc * 512:kc * 512 + ks],
                              PS[b][:, 0:ks], fap(G, base, [[-1, ks]]), ALU.add,
                              wait=[("sT", mmv)])
                    P.bank_done(b, "sV", sv)
                P.op("vector", "tensor_tensor", lg[:, kw - 128:kw], lg[:, kw - 128:kw],
                     dtri[:, :], ALU.min,
                     wait=[("qD", wcst)] if (hh == 0 and qt == 0 and half == 0) else ())
                P.op("vector", "tensor_reduce", rowm[:, 0:1], lg[:, 0:kw], AX.X, ALU.max)
                nm = P.op("vector", "tensor_scalar", rowm[:, 1:2], rowm[:, 0:1],
                          -1.0 / 64.0, None, ALU.mult)
                se = P.op("scalar", "activation", lg[:, 0:kw], lg[:, 0:kw], AF.Exp,
                          scale=cc(1.0 / 64.0), bias=rowm[:, 1:2], accum_out=rowm[:, 0:1],
                          wait=[("sV", nm)])
                P.op("vector", "reciprocal", rowm[:, 1:2], rowm[:, 0:1], wait=[("sS", se)])
                nv = P.op("vector", "tensor_scalar", lg[:, 0:kw], lg[:, 0:kw],
                          rowm[:, 1:2], None, ALU.mult)
                for g4 in range((qt + 4) // 4):
                    t2s = list(range(g4 * 4, min(qt + 1, g4 * 4 + 4)))
                    b = 2 + (g4 % 2)
                    w = P.bank_wait(b)
                    for j, t2 in enumerate(t2s):
                        P.op("tensor", "matmul", PS[b][:, j * 128:(j + 1) * 128],
                             lg[:, t2 * 128:(t2 + 1) * 128], ctx.IDNF[:, :],
                             is_transpose=True, start=(j == 0), stop=(j == len(t2s) - 1),
                             wait=(w + [("sV", nv)]) if j == 0 else (),
                             inc="sT" if j == len(t2s) - 1 else None)
                    trv = P.cnt["sT"]
                    sv = P.op("vector", "tensor_copy",
                              PT[:, g4 * 512:g4 * 512 + len(t2s) * 128],
                              PS[b][:, 0:len(t2s) * 128], wait=[("sT", trv)])
                    P.bank_done(b, "sV", sv)
                ptv = P.cnt["sV"]
                w4 = P.bank_wait(4)
                for t2 in range(qt + 1):
                    P.op("tensor", "matmul", PS[4][pb:pb + 64, 0:128],
                         vt[t2][:, hh * 64:(hh + 1) * 64], PT[:, t2 * 128:(t2 + 1) * 128],
                         start=(t2 == 0), stop=(t2 == qt),
                         tile_position=(0, pb),
                         wait=(w4 + [("sV", ptv)]) if t2 == 0 else (),
                         inc="sT" if t2 == qt else None)
                mmv = P.cnt["sT"]
                sv = P.op("scalar", "activation", osb[ft][pb:pb + 64, qt * 128:(qt + 1) * 128],
                          PS[4][pb:pb + 64, 0:128], AF.Copy, wait=[("sT", mmv)])
                P.bank_done(4, "sS", sv)
            g_guard = [("sV", P.cnt["sV"])]

        ov = P.cnt["sS"]
        for ch in range(NCH):
            cs = tok(ch * CH)
            for m in range(8):
                w = P.bank_wait(m)
                if ch == 0 and m == 0:
                    w = w + [("qW", wvb2), ("sS", ov)]
                for k in range(4):
                    P.op("tensor", "matmul", PS[m][:, :],
                         WB[:, k * 1024 + m * 128: k * 1024 + (m + 1) * 128],
                         osb[k][:, ch * CH:(ch + 1) * CH], start=(k == 0), stop=(k == 3),
                         wait=w if k == 0 else (), inc="sT" if k == 3 else None)
                mmv = P.cnt["sT"]
                if half == 0:
                    sv = P.op("vector", "scalar_tensor_tensor", X[m][:, cs], PS[m][:, :],
                              bo[:, m:m + 1], X[m][:, cs], ALU.add, ALU.add,
                              wait=[("sT", mmv)])
                else:
                    sv = P.op("vector", "tensor_tensor", X[m][:, cs], PS[m][:, :],
                              X[m][:, cs], ALU.add, wait=[("sT", mmv)])
                P.bank_done(m, "sV", sv)
        AR.pop()
    AR.pop()


@stage("s5")
def s5(ctx):
    emit_l1attn(ctx)




def build_nc(weights_meta, stages):
    nc = bass.Bass("TRN2", target_bir_lowering=False,
                   detect_race_conditions=not DBG.get("no_race_detect", False))
    P = Prog(nc)

    dram = {}
    for name, (shape, dt) in weights_meta.items():
        dram[name] = nc.dram_tensor(name, list(shape), dt, kind="ExternalInput")
    xout = nc.dram_tensor("xout", [D, L], F32, kind="ExternalOutput")

    X = [nc.alloc_sbuf_tensor(f"x{i}", [128, XW], F32) for i in range(8)]
    CONST = nc.alloc_sbuf_tensor("consts", [128, 16], F32)
    IDN = nc.alloc_sbuf_tensor("sb_idn", [128, 128], BF16)
    IDNF = nc.alloc_sbuf_tensor("sb_idnf", [128, 128], F32)
    ONES = nc.alloc_sbuf_tensor("ones", [128, 128], BF16)

    ctx = Ctx()
    ctx.nc, ctx.P, ctx.dram, ctx.X = nc, P, dram, X
    ctx.IDN, ctx.IDNF, ctx.ONES = IDN, IDNF, ONES
    ctx.xout = xout
    ctx.scratch_cw = nc.dram_tensor("scratch_cw", [8, L], F32, kind="Internal")
    ctx.scratch_x = nc.dram_tensor("scratch_x", [D, L], F32, kind="Internal")
    ctx.scratch_bc = nc.dram_tensor("scratch_bc", [32, 256], F32, kind="Internal")
    ctx.AR = Arena(nc, 142 * 1024)
    ctx.PS = [nc.alloc_psum_tensor(f"ps{i}", [128, 512], F32) for i in range(8)]

    CONSTS = {}

    def const_col(val):
        v = float(val)
        if v not in CONSTS:
            idx = len(CONSTS)
            assert idx < 16
            CONSTS[v] = idx
            P.op("vector", "memset", CONST[:, idx:idx + 1], v)
        return CONST[:, CONSTS[v]:CONSTS[v] + 1]
    ctx.const_col = const_col
    const_col(EPS)
    const_col(-0.5)

    for i in range(8):
        P.op("vector", "memset", X[i][:, 0:GUARD], 0.0)
    P.op("gpsimd", "memset", ONES[:, :], 1.0)
    P.dma("sync", IDN[:, :], dram["idn"].ap(), inc="qS")
    P.dma("sync", IDNF[:, :], dram["idnf"].ap(), inc="qS")
    for i in range(8):
        P.dma("sync", X[i][:, GUARD:GUARD + L], dram["x"].ap()[i * 128:(i + 1) * 128, :],
              inc="qS")
    qv = P.cnt["qS"]
    vv = P.cnt["sV"]
    gv = P.cnt["sG"]
    for e in ENGINES:
        P.wait(e, ("qS", qv), ("sV", vv), ("sG", gv))

    for st in stages:
        STAGE_FNS[st](ctx)
        P.barrier()

    for i in range(8):
        P.dma("sync", xout.ap()[i * 128:(i + 1) * 128, :], X[i][:, GUARD:GUARD + L],
              inc="qOut")
    P.wait("sync", ("qOut", P.cnt["qOut"]))

    P.build()
    return nc



def prep_core_inputs(ins, core):
    o = {}
    o["x"] = np.ascontiguousarray(np.asarray(ins["L0_attn_src"][core]).T).astype(np.float32)
    o["idn"] = np.eye(128, dtype=bfnp)
    o["idnf"] = np.eye(128, dtype=np.float32)
    o["moe_gwT"] = np.ascontiguousarray(np.asarray(ins["L0_ffn_gate_weight"]).T)
    o["moe_bias16"] = np.ascontiguousarray(
        np.tile(np.asarray(ins["L0_ffn_e_score_correction_bias"]), 16)[None])
    o["moe_gpT"] = np.ascontiguousarray(np.swapaxes(np.asarray(ins["L0_ffn_gate_proj"]), 1, 2))
    o["moe_upT"] = np.ascontiguousarray(np.swapaxes(np.asarray(ins["L0_ffn_up_proj"]), 1, 2))
    o["moe_dnT"] = np.ascontiguousarray(np.swapaxes(np.asarray(ins["L0_ffn_down_proj"]), 1, 2))
    o["att_wq"] = np.ascontiguousarray(np.asarray(ins["L1_attn_Wq"]))
    o["att_wk"] = np.ascontiguousarray(np.asarray(ins["L1_attn_Wk"]))
    o["att_wv"] = np.ascontiguousarray(np.asarray(ins["L1_attn_Wv"]))
    o["att_wo"] = np.ascontiguousarray(np.asarray(ins["L1_attn_Wo"]))
    o["att_bo"] = np.ascontiguousarray(np.asarray(ins["L1_attn_bo"]).reshape(8, 128).T)
    rel = np.asarray(ins["L1_attn_rel_pos_bias"])
    s = np.arange(2952)
    o["att_T"] = np.ascontiguousarray(rel[:, np.clip(1023 - (s - 896), 0, 1023)])
    i_ = np.arange(128)[:, None]
    j_ = np.arange(128)[None, :]
    o["dtri"] = np.where(j_ <= i_, 3e38, -3e37).astype(np.float32)

    o["lf_wqkv"] = np.ascontiguousarray(np.asarray(ins["L1_ffn_Wqkv"]))
    o["lf_wo"] = np.ascontiguousarray(np.asarray(ins["L1_ffn_Wo"]))
    w1 = np.asarray(ins["L1_ffn_W1"])
    w1p = np.zeros((6, 1024, 5632), np.float32)
    w1p[:, :, 0:2730] = w1[:, :, 0:2730]
    w1p[:, :, 2816:2816 + 2730] = w1[:, :, 2730:5460]
    o["lf_w1"] = w1p
    w2 = np.asarray(ins["L1_ffn_W2"])
    w2p = np.zeros((6, 2816, 1024), np.float32)
    w2p[:, 0:2730] = w2
    o["lf_w2"] = w2p
    b1 = np.asarray(ins["L1_ffn_b1"])
    b1p = np.zeros((6, 5632), np.float32)
    b1p[:, 0:2730] = b1[:, 0:2730]
    b1p[:, 2816:2816 + 2730] = b1[:, 2730:5460]
    o["lf_b1"] = np.ascontiguousarray(b1p.reshape(6, 44, 128).transpose(2, 0, 1).reshape(128, 264))
    o["lf_b2"] = np.ascontiguousarray(np.asarray(ins["L1_ffn_b2"]).reshape(6, 8, 128).transpose(2, 0, 1).reshape(128, 48))
    o["lf_lng"] = np.ascontiguousarray(np.asarray(ins["L1_ffn_ln_g"]).reshape(6, 8, 128).transpose(2, 0, 1).reshape(128, 48))
    o["lf_lnb"] = np.ascontiguousarray(np.asarray(ins["L1_ffn_ln_b"]).reshape(6, 8, 128).transpose(2, 0, 1).reshape(128, 48))
    o["lf_qs"] = np.ascontiguousarray(np.tile(np.asarray(ins["L1_ffn_q_scale"]), (1, 2)).T)
    o["lf_ks"] = np.ascontiguousarray(np.tile(np.asarray(ins["L1_ffn_k_scale"]), (1, 2)).T)
    t_ = np.arange(2048, dtype=np.float64)
    halfd = np.arange(0, 64, 2, dtype=np.float64)
    inv_freq = 1.0 / 10000.0 ** (halfd / 64.0)
    freqs = np.concatenate([t_[:, None] * inv_freq[None, :]] * 2, axis=1)
    base_scale = np.concatenate([(halfd + 0.4 * 64) / (1.4 * 64)] * 2)
    power = (t_ - 1024) / 128.0
    scl = base_scale[None, :] ** power[:, None]
    cq_ = (np.cos(freqs) * scl).T
    sq_ = (np.sin(freqs) * scl).T
    ck_ = (np.cos(freqs) / scl).T
    sk_ = (np.sin(freqs) / scl).T
    o["rot_cq"] = np.ascontiguousarray(np.tile(cq_, (2, 1))).astype(bfnp)
    sqp_ = np.concatenate([sq_[32:64], sq_[0:32]], axis=0)
    o["rot_sqp"] = np.ascontiguousarray(np.tile(sqp_, (2, 1))).astype(bfnp)
    o["rot_ck"] = np.ascontiguousarray(np.tile(ck_, (2, 1))).astype(bfnp)
    skp_ = np.concatenate([sk_[32:64], sk_[0:32]], axis=0)
    o["rot_skp"] = np.ascontiguousarray(np.tile(skp_, (2, 1))).astype(bfnp)
    M = np.zeros((128, 128), np.float32)
    for b_ in (0, 64):
        for j in range(32):
            M[b_ + j, b_ + 32 + j] = -1.0
            M[b_ + 32 + j, b_ + j] = 1.0
    o["rotmat"] = np.ascontiguousarray(M.T).astype(bfnp)
    H2 = np.zeros((128, 128), np.float32)
    H2[0:64, 0:64] = 1.0
    H2[64:128, 64:128] = 1.0
    o["hones2"] = H2.astype(bfnp)
    p_ = np.arange(128)[:, None]
    jj = np.arange(384)[None, :]
    o["m384f"] = np.where((jj >= p_) & (jj <= p_ + 256), 0.0, -3e37).astype(np.float32)

    o["ma_lng"] = np.ascontiguousarray(np.asarray(ins["L0_attn_ln_g"]).reshape(8, 128).T)
    o["ma_lnb"] = np.ascontiguousarray(np.asarray(ins["L0_attn_ln_b"]).reshape(8, 128).T)
    o["ma_rms"] = np.ascontiguousarray(np.asarray(ins["L0_attn_rms_w"]).reshape(2, 4, 128).transpose(2, 0, 1).reshape(128, 8))
    cw = np.asarray(ins["L0_attn_conv_w"])[:, :, 0, :]
    o["ma_convw"] = np.ascontiguousarray(cw.reshape(2, 8, 128, 4).transpose(2, 0, 3, 1).reshape(128, 64))
    o["ma_convb"] = np.ascontiguousarray(np.asarray(ins["L0_attn_conv_b"]).reshape(2, 8, 128).transpose(2, 0, 1).reshape(128, 16))
    o["ma_dtb"] = np.ascontiguousarray(np.asarray(ins["L0_attn_dt_b"]).reshape(2, 8, 128).transpose(2, 0, 1).reshape(128, 16))
    o["ma_Alog"] = np.ascontiguousarray(np.asarray(ins["L0_attn_A_log"]).reshape(2, 8, 128, 16).transpose(2, 0, 1, 3).reshape(128, 256))
    o["ma_Dp"] = np.ascontiguousarray(np.asarray(ins["L0_attn_Dp"]).reshape(2, 8, 128).transpose(2, 0, 1).reshape(128, 16))
    o["ma_pob"] = np.ascontiguousarray(np.asarray(ins["L0_attn_po_b"]).reshape(8, 128).T)
    o["ma_inWT"] = np.ascontiguousarray(np.swapaxes(np.asarray(ins["L0_attn_in_W"]), 1, 2))
    o["ma_xpWT"] = np.ascontiguousarray(np.swapaxes(np.asarray(ins["L0_attn_xp_W"]), 1, 2))
    o["ma_dtWT"] = np.ascontiguousarray(np.swapaxes(np.asarray(ins["L0_attn_dt_W"]), 1, 2))
    o["ma_moWT"] = np.ascontiguousarray(np.swapaxes(np.asarray(ins["L0_attn_mo_W"]), 1, 2))
    o["ma_poWT"] = np.ascontiguousarray(np.asarray(ins["L0_attn_po_W"]).T)
    idx8 = np.tile(np.arange(8, dtype=np.float32), 16)
    idx4 = np.tile(np.arange(4, dtype=np.float32), 16)
    o["idx8"] = idx8[None]
    o["idx8b"] = (idx8 + 1e4)[None]
    o["idx4"] = idx4[None]
    o["idx4b"] = (idx4 + 1e4)[None]
    return o


DT_MAP = {np.dtype(np.float32): F32, np.dtype(bfnp): BF16}


def kernel(**inputs):
    in_maps = [prep_core_inputs(inputs, c) for c in range(8)]
    meta = {k: (v.shape, DT_MAP[v.dtype]) for k, v in in_maps[0].items()}
    nc = build_nc(meta, stages=STAGES_TO_RUN)
    res = run_bass_kernel_spmd(nc, in_maps, core_ids=list(range(8)))
    out = np.stack([np.ascontiguousarray(r["xout"].T) for r in res.results], axis=0)
    return out.astype(np.float32)


STAGES_TO_RUN = ["s1", "s2", "s3", "s4", "s5", "s6"]
